# version 01
# baseline (speedup 1.0000x reference)
import math
import numpy as np
import ml_dtypes
import concourse.bass as bass
import concourse.mybir as mybir
from concourse.bass_utils import run_bass_kernel_spmd

F32 = mybir.dt.float32
BF16 = mybir.dt.bfloat16
I32 = mybir.dt.int32
AF = mybir.ActivationFunctionType
ALU = mybir.AluOpType
AX = mybir.AxisListType

B, T, D = 8, 4096, 1024
H, DK, DV = 4, 256, 256
NT = T // 128
ND = D // 128
CHUNK = 32
NCH = T // CHUNK
DFF = 3072
NF = DFF // 128

BF = ml_dtypes.bfloat16


class Sem:

    def __init__(self, nc, name):
        self.h = nc.alloc_semaphore(name)
        self.n = 0

    def inc(self, instr, k=1):
        instr.then_inc(self.h, k)
        self.n += k
        return instr

    def wait(self, eng, target=None):
        eng.wait_ge(self.h, self.n if target is None else target)


class P:

    def __init__(self, nc, block):
        self.nc = nc
        self.block = block
        self.bar = nc.alloc_semaphore("bar")
        self.cnt = 0
        self._pool = {}

    def sem(self, name):
        if name not in self._pool:
            self._pool[name] = Sem(self.nc, name)
        return self._pool[name]

    def aux_sem(self):
        raise RuntimeError("use p.sem(name) — hardware semaphores are scarce")

    def step(self, engine, fn):
        need = self.cnt

        def body(eng):
            if need:
                eng.wait_ge(self.bar, need)
            last = fn(eng)
            assert last is not None, "step fn must return its last instruction"
            last.then_inc(self.bar, 1)
        getattr(self.block, engine)(body)
        self.cnt += 1

    def dma_step(self, engine, fn):
        need = self.cnt
        cnt_holder = [None]

        def body(eng):
            if need:
                eng.wait_ge(self.bar, need)
            insts = fn(eng)
            for ins in insts:
                ins.then_inc(self.bar, 16)
            cnt_holder[0] = len(insts)
        getattr(self.block, engine)(body)
        assert cnt_holder[0] and cnt_holder[0] > 0
        self.cnt += 16 * cnt_holder[0]

    def mm_pipe(self, jobs, emit_mms, emit_evac, evac_engine="vector", nbanks=2):
        s_mm = self.sem("mmp_mm")
        s_ev = self.sem("mmp_ev")
        mm0, ev0 = s_mm.n, s_ev.n
        start = self.cnt
        njobs = len(jobs)

        def body_t(t):
            if start:
                t.wait_ge(self.bar, start)
            for i, job in enumerate(jobs):
                if i >= nbanks:
                    t.wait_ge(s_ev.h, ev0 + i - nbanks + 1)
                emit_mms(t, job, i % nbanks).then_inc(s_mm.h, 1)

        def body_v(v):
            for i, job in enumerate(jobs):
                v.wait_ge(s_mm.h, mm0 + i + 1)
                emit_evac(v, job, i % nbanks).then_inc(s_ev.h, 1)
            v.wait_ge(s_ev.h, ev0 + njobs)
            v.nop().then_inc(self.bar, 1)
        self.block.tensor(body_t)
        getattr(self.block, evac_engine)(body_v)
        s_mm.n += njobs
        s_ev.n += njobs
        self.cnt += 1


def conv_taps_fm(p, src, dst, wcol_ap_fn, ntaps, width=T):
    n = len(src)

    def tap_al(v):
        last = None
        for ot in range(n):
            last = v.tensor_scalar(dst[ot][:, :], src[ot][:, :],
                                   wcol_ap_fn(ot, ntaps - 1), None, ALU.mult)
        return last
    p.step("vector", tap_al)
    for d in range(1, ntaps):
        def tap_d(v, d=d):
            last = None
            for ot in range(n):
                last = v.scalar_tensor_tensor(
                    dst[ot][:, d:width], src[ot][:, 0:width - d],
                    wcol_ap_fn(ot, ntaps - 1 - d), dst[ot][:, d:width],
                    ALU.mult, ALU.add)
            return last
        p.step("vector", tap_d)


def build_qkvb():
    nc = bass.Bass("TRN2", target_bir_lowering=False)
    x = nc.dram_tensor("x", [T, D], F32, kind="ExternalInput").ap()
    wT = {nm: nc.dram_tensor(f"w{nm}T", [D, D], F32, kind="ExternalInput").ap()
          for nm in "qkv"}
    bwT = nc.dram_tensor("bwT", [D, H], F32, kind="ExternalInput").ap()
    cw = {nm: nc.dram_tensor(f"{nm}cw", [128, ND * 4], F32, kind="ExternalInput").ap()
          for nm in "qkv"}
    qn_tm = nc.dram_tensor("qn_tm", [T, D], BF16, kind="ExternalOutput").ap()
    kn_tm = nc.dram_tensor("kn_tm", [T, D], BF16, kind="ExternalOutput").ap()
    v_tm = nc.dram_tensor("v_tm", [T, D], BF16, kind="ExternalOutput").ap()
    v_fm = nc.dram_tensor("v_fm", [D, T], BF16, kind="ExternalOutput").ap()
    beta_tm = nc.dram_tensor("beta_tm", [T, H], F32, kind="ExternalOutput").ap()
    x_bf = nc.dram_tensor("x_bf", [T, D], BF16, kind="ExternalOutput").ap()
    q_raw = nc.dram_tensor("q_raw", [D, T], BF16)
    k_raw = nc.dram_tensor("k_raw", [D, T], BF16)

    with nc.Block() as block:
        p = P(nc, block)
        xT = [nc.alloc_sbuf_tensor(f"xT{i}", [128, T], BF16) for i in range(ND)]
        w_sb = [nc.alloc_sbuf_tensor(f"wsb{i}", [128, D], BF16) for i in range(ND)]
        bw_sb = [nc.alloc_sbuf_tensor(f"bw{i}", [128, H], BF16) for i in range(ND)]
        cw_sb = {nm: nc.alloc_sbuf_tensor(f"cw{nm}", [128, ND * 4], F32) for nm in "qkv"}
        xf32 = nc.alloc_sbuf_tensor("xf32", [128, D], F32)
        xbf = nc.alloc_sbuf_tensor("xbf", [128, D], BF16)
        pf32 = nc.alloc_sbuf_tensor("pf32", [128, T], F32)
        pcv = nc.alloc_sbuf_tensor("pcv", [128, T], F32)
        pbf = nc.alloc_sbuf_tensor("pbf", [128, T], BF16)
        ps = [nc.alloc_psum_tensor(f"ps{i}", [128, 512], F32) for i in range(2)]
        psb = nc.alloc_psum_tensor("psb", [128, H], F32)
        btile = nc.alloc_sbuf_tensor("btile", [128, H], F32)
        nb = 8
        tmi = [nc.alloc_sbuf_tensor(f"tmi{i}", [128, D], BF16) for i in range(nb)]
        tmo = [nc.alloc_sbuf_tensor(f"tmo{i}", [128, D], BF16) for i in range(nb)]
        sqs = nc.alloc_sbuf_tensor("sqs", [128, 256], F32)
        ss = nc.alloc_sbuf_tensor("ss", [128, nb * H], F32)
        rss = nc.alloc_sbuf_tensor("rss", [128, nb * H], F32)
        eps = nc.alloc_sbuf_tensor("eps", [128, 1], F32)

        def ld_cw(g):
            out = []
            for nm in "qkv":
                out.append(g.dma_start(out=cw_sb[nm][:, :], in_=cw[nm][:, :]))
            for i in range(ND):
                out.append(g.dma_start(out=bw_sb[i][:, :], in_=bwT[128 * i:128 * (i + 1), :]))
            return out
        p.dma_step("gpsimd", ld_cw)
        p.step("vector", lambda v: v.memset(eps[:, :], 1e-6))

        for i in range(NT):
            p.dma_step("sync", lambda e, i=i: [e.dma_start(out=xf32[:, :], in_=x[128 * i:128 * (i + 1), :])])
            p.step("vector", lambda v: v.tensor_copy(xbf[:, :], xf32[:, :]))
            p.dma_step("scalar", lambda e, i=i: [e.dma_start(out=x_bf[128 * i:128 * (i + 1), :], in_=xbf[:, :])])

        p.dma_step("sync", lambda e: [
            e.dma_start(out=xT[i][:, :], in_=x_bf[:, 128 * i:128 * (i + 1)], transpose=True)
            for i in range(ND)])

        btile2 = nc.alloc_sbuf_tensor("btile2", [128, H], F32)
        s_mx = p.sem("s_mx").h
        s_sg = p.sem("s_sg").h
        s_sg2 = p.sem("s_sg2").h
        s_st = p.sem("s_st").h
        start0 = p.cnt

        def beta_t(t):
            t.wait_ge(p.bar, start0)
            for tt in range(NT):
                if tt >= 1:
                    t.wait_ge(s_sg, tt)
                for k in range(ND):
                    last = t.matmul(psb[:, :], xT[k][:, 128 * tt:128 * (tt + 1)],
                                    bw_sb[k][:, :], start=(k == 0), stop=(k == ND - 1))
                last.then_inc(s_mx, 1)

        def beta_sc(se):
            for tt in range(NT):
                se.wait_ge(s_mx, tt + 1)
                if tt >= 1:
                    se.wait_ge(s_sg2, tt)
                se.activation(out=btile[:, :], in_=psb[:, :], func=AF.Sigmoid).then_inc(s_sg, 1)

        def beta_v(v):
            for tt in range(NT):
                v.wait_ge(s_sg, tt + 1)
                if tt >= 1:
                    v.wait_ge(s_st, 16 * tt)
                v.tensor_scalar(btile2[:, :], btile[:, :], 1e-6, None, ALU.max).then_inc(s_sg2, 1)

        def beta_g(g):
            for tt in range(NT):
                g.wait_ge(s_sg2, tt + 1)
                g.dma_start(out=beta_tm[128 * tt:128 * (tt + 1), :], in_=btile2[:, :]).then_inc(s_st, 16)
            g.wait_ge(s_st, 16 * NT)
            g.nop().then_inc(p.bar, 1)
        block.tensor(beta_t)
        block.scalar(beta_sc)
        block.vector(beta_v)
        block.gpsimd(beta_g)
        p.cnt += 1

        for nm, stash in (("q", q_raw), ("k", k_raw), ("v", v_fm)):
            p.dma_step("gpsimd", lambda g, nm=nm: [
                g.dma_start(out=w_sb[i][:, :], in_=wT[nm][128 * i:128 * (i + 1), :])
                for i in range(ND)])
            for ot in range(ND):
                def mms(t, tb, bank, ot=ot):
                    for k in range(ND):
                        last = t.matmul(ps[bank][:, :], w_sb[k][:, 128 * ot:128 * (ot + 1)],
                                        xT[k][:, 512 * tb:512 * (tb + 1)],
                                        start=(k == 0), stop=(k == ND - 1))
                    return last

                def ev(v, tb, bank, ot=ot):
                    return v.tensor_copy(pf32[:, 512 * tb:512 * (tb + 1)], ps[bank][:, :])
                p.mm_pipe(list(range(8)), mms, ev)
                conv_taps_fm(p, [pf32], [pcv],
                             lambda o, j, nm=nm, ot=ot: cw_sb[nm][:, 4 * ot + j:4 * ot + j + 1], 4)
                p.step("scalar", lambda se: se.activation(out=pbf[:, :], in_=pcv[:, :], func=AF.Silu))
                p.dma_step("scalar", lambda e, ot=ot, stash=stash: [
                    e.dma_start(out=stash[128 * ot:128 * (ot + 1), :], in_=pbf[:, :])])

        for src, dst, norm in ((q_raw, qn_tm, True), (k_raw, kn_tm, True), (v_fm, v_tm, False)):
            for b0 in range(0, NT, nb):
                p.dma_step("sync", lambda e, src=src, b0=b0: [
                    e.dma_start(out=tmi[i][:, :], in_=src[:, 128 * (b0 + i):128 * (b0 + i + 1)],
                                transpose=True)
                    for i in range(nb)])
                if norm:
                    def sq_acc(se):
                        last = None
                        for i in range(nb):
                            for h in range(H):
                                last = se.activation(out=sqs[:, :], in_=tmi[i][:, 256 * h:256 * (h + 1)],
                                                     func=AF.Square,
                                                     accum_out=ss[:, i * H + h:i * H + h + 1])
                        return last
                    p.step("scalar", sq_acc)
                    p.step("scalar", lambda se: se.activation(out=rss[:, :], in_=ss[:, :],
                                                              func=AF.Sqrt, bias=eps[:, :]))
                    p.step("vector", lambda v: v.reciprocal(ss[:, :], rss[:, :]))
                    def scale(v):
                        last = None
                        for i in range(nb):
                            for h in range(H):
                                last = v.tensor_scalar(tmo[i][:, 256 * h:256 * (h + 1)],
                                                       tmi[i][:, 256 * h:256 * (h + 1)],
                                                       ss[:, i * H + h:i * H + h + 1], None, ALU.mult)
                        return last
                    p.step("vector", scale)
                    out_t = tmo
                else:
                    out_t = tmi
                p.dma_step("scalar", lambda e, dst=dst, b0=b0, out_t=out_t: [
                    e.dma_start(out=dst[128 * (b0 + i):128 * (b0 + i + 1), :], in_=out_t[i][:, :])
                    for i in range(nb)])

        def fin(g):
            return g.nop()
        p.step("gpsimd", fin)
    return nc


C2 = 64
NC2 = T // C2


def build_delta(debug=False):
    nc = bass.Bass("TRN2", target_bir_lowering=False)
    qn_tm = nc.dram_tensor("qn_tm", [T, D], BF16, kind="ExternalInput").ap()
    kn_tm = nc.dram_tensor("kn_tm", [T, D], BF16, kind="ExternalInput").ap()
    v_tm = nc.dram_tensor("v_tm", [T, D], BF16, kind="ExternalInput").ap()
    beta_tm = nc.dram_tensor("beta_tm", [T, H], F32, kind="ExternalInput").ap()
    eye64 = nc.dram_tensor("eye64", [C2, C2], BF16, kind="ExternalInput").ap()
    masks = nc.dram_tensor("masks", [C2, 3 * C2], F32, kind="ExternalInput").ap()
    delta_tm = nc.dram_tensor("delta_tm", [T, D], BF16, kind="ExternalOutput").ap()
    kb_st = nc.dram_tensor("kb_st", [T, DK], BF16)
    vb_st = nc.dram_tensor("vb_st", [T, DK], BF16)
    if debug:
        dbg = {nm: nc.dram_tensor(f"dbg_{nm}", shp, BF16, kind="ExternalOutput").ap()
               for nm, shp in [("TT", [C2, NC2, C2]), ("u", [C2, NC2, DK]),
                               ("attnT", [C2, NC2, C2]), ("N", [C2, NC2, C2]),
                               ("wT0", [128, T]), ("wT1", [128, T]),
                               ("kcm", [C2, NC2, DK]), ("kbcm", [C2, NC2, DK])]}

    with nc.Block() as block:
        p = P(nc, block)
        qT = [nc.alloc_sbuf_tensor(f"qT{i}", [128, T], BF16) for i in range(2)]
        kT = [nc.alloc_sbuf_tensor(f"kT{i}", [128, T], BF16) for i in range(2)]
        kbT = [nc.alloc_sbuf_tensor(f"kbT{i}", [128, T], BF16) for i in range(2)]
        wT = kbT
        k_cm = nc.alloc_sbuf_tensor("k_cm", [C2, NC2, DK], BF16)
        kb_cm = nc.alloc_sbuf_tensor("kb_cm", [C2, NC2, DK], BF16)
        vb_cm = nc.alloc_sbuf_tensor("vb_cm", [C2, NC2, DK], BF16)
        u_cm = vb_cm
        ktmp = nc.alloc_sbuf_tensor("ktmp", [128, 8, DK], BF16)
        beta_all = nc.alloc_sbuf_tensor("beta_all", [128, NT, H], F32)
        kbtmp = nc.alloc_sbuf_tensor("kbtmp", [128, 8, DK], BF16)
        N_all = nc.alloc_sbuf_tensor("N_all", [C2, NC2, C2], BF16)
        M_all = nc.alloc_sbuf_tensor("M_all", [C2, NC2, C2], BF16)
        NM = nc.alloc_sbuf_tensor("NM", [C2, NC2, 2 * C2], BF16)
        P_c = nc.alloc_sbuf_tensor("P_c", [C2, NC2, C2], BF16)
        attnT = nc.alloc_sbuf_tensor("attnT", [C2, NC2, C2], BF16)
        eye_sb = nc.alloc_sbuf_tensor("eye_sb", [C2, C2], BF16)
        msk_sb = nc.alloc_sbuf_tensor("msk_sb", [C2, 3 * C2], F32)
        S_sb = nc.alloc_sbuf_tensor("S_sb", [128, 2, DK], BF16)
        u_adj = nc.alloc_sbuf_tensor("u_adj", [C2, DK], BF16)
        d_ring = [nc.alloc_sbuf_tensor(f"d_ring{i}", [C2, DK], BF16) for i in range(4)]
        pp = [nc.alloc_psum_tensor(f"pp{i}", [C2, 256], F32) for i in range(2)]
        pw = [nc.alloc_psum_tensor(f"pw{i}", [128, 2 * C2], F32) for i in range(2)]
        ps_q = nc.alloc_psum_tensor("ps_q", [C2, 256], F32)
        ps_w = nc.alloc_psum_tensor("ps_w", [C2, 256], F32)
        ps_S = nc.alloc_psum_tensor("ps_S", [128, 512], F32)

        p.dma_step("gpsimd", lambda g: [
            g.dma_start(out=eye_sb[:, :], in_=eye64[:, :]),
            g.dma_start(out=msk_sb[:, :], in_=masks[:, :]),
            g.dma_start(out=beta_all[:, :, :],
                        in_=beta_tm.rearrange("(n p) c -> p n c", p=128))])

        def ch(n, w=C2):
            return slice(C2 * n, C2 * n + w)

        m_su = msk_sb[:, 0:C2]
        m_sl = msk_sb[:, C2:2 * C2]
        m_iu = msk_sb[:, 2 * C2:3 * C2]

        for h in range(H):
            seg = slice(256 * h, 256 * (h + 1))
            for src, dst in ((kn_tm, kb_st), (v_tm, vb_st)):
                for r in range(4):
                    rows = slice(1024 * r, 1024 * (r + 1))
                    p.dma_step("sync", lambda e, src=src, rows=rows: [
                        e.dma_start(out=ktmp[:, :, :],
                                    in_=src[rows, seg].rearrange("(n p) c -> p n c", p=128))])

                    def bmul(v, r=r):
                        last = None
                        for n in range(8):
                            last = v.tensor_scalar(kbtmp[:, n, :], ktmp[:, n, :],
                                                   beta_all[:, 8 * r + n, h:h + 1].opt(),
                                                   None, ALU.mult)
                        return last
                    p.step("vector", bmul)
                    p.dma_step("scalar", lambda e, dst=dst, rows=rows: [
                        e.dma_start(out=dst[rows, :].rearrange("(n p) c -> p n c", p=128),
                                    in_=kbtmp[:, :, :])])
            p.dma_step("sync", lambda e: [
                e.dma_start(out=qT[i][:, :],
                            in_=qn_tm[:, 256 * h + 128 * i:256 * h + 128 * (i + 1)],
                            transpose=True) for i in range(2)] + [
                e.dma_start(out=kT[i][:, :],
                            in_=kn_tm[:, 256 * h + 128 * i:256 * h + 128 * (i + 1)],
                            transpose=True) for i in range(2)] + [
                e.dma_start(out=kbT[i][:, :], in_=kb_st[:, 128 * i:128 * (i + 1)],
                            transpose=True) for i in range(2)] + [
                e.dma_start(out=k_cm[:, :, :],
                            in_=kn_tm[:, seg].rearrange("(n p) c -> p n c", p=C2)),
                e.dma_start(out=kb_cm[:, :, :],
                            in_=kb_st.rearrange("(n p) c -> p n c", p=C2)),
                e.dma_start(out=vb_cm[:, :, :],
                            in_=vb_st.rearrange("(n p) c -> p n c", p=C2))])

            CH = list(range(NC2))

            def s13_mm(t, n, b):
                t.matmul(pp[b][:, 0:C2], kT[0][:, ch(n)], qT[0][:, ch(n)], start=True, stop=False)
                return t.matmul(pp[b][:, 0:C2], kT[1][:, ch(n)], qT[1][:, ch(n)],
                                start=False, stop=True)
            p.mm_pipe(CH, s13_mm,
                      lambda v, n, b: v.tensor_tensor(attnT[:, n, :], pp[b][:, 0:C2], m_iu, ALU.mult))

            def s1_mm(t, n, b):
                t.matmul(pp[b][:, 0:C2], kT[0][:, ch(n)], kbT[0][:, ch(n)], start=True, stop=False)
                t.matmul(pp[b][:, 0:C2], kT[1][:, ch(n)], kbT[1][:, ch(n)], start=False, stop=True)
                t.matmul(pp[b][:, C2:2 * C2], kbT[0][:, ch(n)], kT[0][:, ch(n)], start=True, stop=False)
                return t.matmul(pp[b][:, C2:2 * C2], kbT[1][:, ch(n)], kT[1][:, ch(n)],
                                start=False, stop=True)

            def s1_ev(v, n, b):
                v.tensor_tensor(N_all[:, n, :], pp[b][:, 0:C2], m_su, ALU.mult)
                return v.tensor_tensor(M_all[:, n, :], pp[b][:, C2:2 * C2], m_sl, ALU.mult)
            p.mm_pipe(CH, s1_mm, s1_ev)

            def s6(v):
                return v.tensor_tensor(P_c[:, :, :],
                                       eye_sb[:, :].unsqueeze(1).broadcast_to([C2, NC2, C2]),
                                       N_all[:, :, :], ALU.subtract)
            p.step("vector", s6)

            def sq_stage(nsrc, msrc):
                def mm(t, n, b):
                    t.matmul(pp[b][:, 0:C2], msrc(n), nsrc(n), start=True, stop=True)
                    return t.matmul(pp[b][:, C2:2 * C2], nsrc(n), msrc(n), start=True, stop=True)

                def ev(v, n, b):
                    return v.tensor_copy(NM[:, n, :], pp[b][:, 0:2 * C2])
                p.mm_pipe(CH, mm, ev)

            def prod_stage(msrc):
                def mm(t, n, b):
                    t.matmul(pp[b][:, 0:C2], eye_sb[:, :], P_c[:, n, :], start=True, stop=False)
                    return t.matmul(pp[b][:, 0:C2], msrc(n), P_c[:, n, :], start=False, stop=True)

                def ev(v, n, b):
                    return v.tensor_copy(P_c[:, n, :], pp[b][:, 0:C2])
                p.mm_pipe(CH, mm, ev)

            sq_stage(lambda n: N_all[:, n, :], lambda n: M_all[:, n, :])
            prod_stage(lambda n: NM[:, n, C2:2 * C2])
            sq_stage(lambda n: NM[:, n, 0:C2], lambda n: NM[:, n, C2:2 * C2])
            prod_stage(lambda n: NM[:, n, C2:2 * C2])
            sq_stage(lambda n: NM[:, n, 0:C2], lambda n: NM[:, n, C2:2 * C2])
            prod_stage(lambda n: NM[:, n, C2:2 * C2])
            sq_stage(lambda n: NM[:, n, 0:C2], lambda n: NM[:, n, C2:2 * C2])
            prod_stage(lambda n: NM[:, n, C2:2 * C2])

            def s5_mm(t, n, b):
                return t.matmul(pp[b][:, 0:C2], NM[:, n, 0:C2], NM[:, n, C2:2 * C2],
                                start=True, stop=True)
            p.mm_pipe(CH, s5_mm, lambda v, n, b: v.tensor_copy(M_all[:, n, :], pp[b][:, 0:C2]))
            prod_stage(lambda n: M_all[:, n, :])

            def s11_mm(t, n, b):
                return t.matmul(pp[b][:, 0:256], P_c[:, n, :], vb_cm[:, n, :],
                                start=True, stop=True)
            p.mm_pipe(CH, s11_mm, lambda v, n, b: v.tensor_copy(u_cm[:, n, :], pp[b][:, 0:256]))

            def s12_mm(t, n, b):
                t.matmul(pw[b][:, 0:C2], kb_cm[:, n, 0:128], P_c[:, n, :], start=True, stop=True)
                return t.matmul(pw[b][:, C2:2 * C2], kb_cm[:, n, 128:256], P_c[:, n, :],
                                start=True, stop=True)

            def s12_ev(v, n, b):
                v.tensor_copy(wT[0][:, ch(n)], pw[b][:, 0:C2])
                return v.tensor_copy(wT[1][:, ch(n)], pw[b][:, C2:2 * C2])
            p.mm_pipe(CH, s12_mm, s12_ev)

            if debug and h == 0:
                p.dma_step("gpsimd", lambda g: [
                    g.dma_start(out=dbg["TT"][:, :, :], in_=P_c[:, :, :]),
                    g.dma_start(out=dbg["u"][:, :, :], in_=u_cm[:, :, :]),
                    g.dma_start(out=dbg["attnT"][:, :, :], in_=attnT[:, :, :]),
                    g.dma_start(out=dbg["N"][:, :, :], in_=N_all[:, :, :]),
                    g.dma_start(out=dbg["wT0"][:, :], in_=wT[0][:, :]),
                    g.dma_start(out=dbg["wT1"][:, :], in_=wT[1][:, :]),
                    g.dma_start(out=dbg["kcm"][:, :, :], in_=k_cm[:, :, :]),
                    g.dma_start(out=dbg["kbcm"][:, :, :], in_=kb_cm[:, :, :])])

            p.step("vector", lambda v: v.memset(S_sb[:, :, :], 0.0))
            S_pw, S_pq, S_pS = p.sem("s_pw"), p.sem("s_pq"), p.sem("s_pS")
            S_ua, S_do, S_Sc, S_dd = p.sem("s_ua"), p.sem("s_do"), p.sem("s_Sc"), p.sem("s_dd")
            pw0, pq0, pS0 = S_pw.n, S_pq.n, S_pS.n
            ua0, do0, Sc0, dd0 = S_ua.n, S_do.n, S_Sc.n, S_dd.n
            s_pw, s_pq, s_pS = S_pw.h, S_pq.h, S_pS.h
            s_ua, s_do, s_Sc, s_dd = S_ua.h, S_do.h, S_Sc.h, S_dd.h
            start2 = p.cnt

            def scan_pe(t):
                t.wait_ge(p.bar, start2)
                for n in range(NC2):
                    t.wait_ge(s_Sc, Sc0 + n)
                    t.wait_ge(s_ua, ua0 + n)
                    t.matmul(ps_w[:, :], wT[0][:, ch(n)], S_sb[:, 0, :],
                             start=True, stop=False, skip_group_check=True)
                    t.matmul(ps_w[:, :], wT[1][:, ch(n)], S_sb[:, 1, :],
                             start=False, stop=True, skip_group_check=True).then_inc(s_pw, 1)
                    t.wait_ge(s_do, do0 + n)
                    t.matmul(ps_q[:, :], qT[0][:, ch(n)], S_sb[:, 0, :],
                             start=True, stop=False, skip_group_check=True)
                    t.matmul(ps_q[:, :], qT[1][:, ch(n)], S_sb[:, 1, :],
                             start=False, stop=False, skip_group_check=True)
                    t.wait_ge(s_ua, ua0 + n + 1)
                    t.matmul(ps_q[:, :], attnT[:, n, :], u_adj[:, :],
                             start=False, stop=True, skip_group_check=True).then_inc(s_pq, 1)
                    t.matmul(ps_S[:, 0:256], k_cm[:, n, 0:128], u_adj[:, :],
                             start=(n == 0), stop=False, skip_group_check=True)
                    t.matmul(ps_S[:, 256:512], k_cm[:, n, 128:256], u_adj[:, :],
                             start=False, stop=(n == NC2 - 1),
                             skip_group_check=True).then_inc(s_pS, 1)
                t.nop().then_inc(p.bar, 1)

            def scan_dve(v):
                for n in range(NC2):
                    v.wait_ge(s_pw, pw0 + n + 1)
                    v.tensor_tensor(u_adj[:, :], u_cm[:, n, :], ps_w[:, :],
                                    ALU.subtract).then_inc(s_ua, 1)
                    v.wait_ge(s_pq, pq0 + n + 1)
                    if n >= 4:
                        v.wait_ge(s_dd, dd0 + 16 * (n - 3))
                    v.tensor_copy(d_ring[n % 4][:, :], ps_q[:, :]).then_inc(s_do, 1)
                v.nop().then_inc(p.bar, 1)

            def scan_act(se):
                for n in range(NC2):
                    se.wait_ge(s_pS, pS0 + n + 1)
                    se.activation(out=S_sb[:, :, :], in_=ps_S[:, :], func=AF.Copy).then_inc(s_Sc, 1)
                se.nop().then_inc(p.bar, 1)

            def scan_g(g):
                for n in range(NC2):
                    g.wait_ge(s_do, do0 + n + 1)
                    g.dma_start(out=delta_tm[C2 * n:C2 * (n + 1), seg],
                                in_=d_ring[n % 4][:, :]).then_inc(s_dd, 16)
                g.wait_ge(s_dd, dd0 + 16 * NC2)
                g.nop().then_inc(p.bar, 1)
            S_pw.n += NC2; S_pq.n += NC2; S_pS.n += NC2
            S_ua.n += NC2; S_do.n += NC2; S_Sc.n += NC2; S_dd.n += 16 * NC2
            block.tensor(scan_pe)
            block.vector(scan_dve)
            block.scalar(scan_act)
            block.gpsimd(scan_g)
            p.cnt += 4

        p.step("gpsimd", lambda g: g.nop())
    return nc


def host_l2_inputs(l1_out):
    mask_su = np.triu(np.ones((C2, C2), np.float32), 1)
    mask_sl = mask_su.T.copy()
    mask_iu = np.triu(np.ones((C2, C2), np.float32), 0)
    return {
        "qn_tm": l1_out["qn_tm"], "kn_tm": l1_out["kn_tm"], "v_tm": l1_out["v_tm"],
        "beta_tm": l1_out["beta_tm"],
        "eye64": np.eye(C2, dtype=BF),
        "masks": np.ascontiguousarray(np.concatenate([mask_su, mask_sl, mask_iu], 1)),
    }


def build_fir():
    nc = bass.Bass("TRN2", target_bir_lowering=False)
    v_fm = nc.dram_tensor("v_fm", [D, T], BF16, kind="ExternalInput").ap()
    fsw = nc.dram_tensor("fsw", [128, ND * 5], F32, kind="ExternalInput").ap()
    flw = nc.dram_tensor("flw", [128, ND * 64], F32, kind="ExternalInput").ap()
    ls_fm = nc.dram_tensor("ls_fm", [D, T], BF16, kind="ExternalOutput").ap()
    ll_fm = nc.dram_tensor("ll_fm", [D, T], BF16, kind="ExternalOutput").ap()

    with nc.Block() as block:
        p = P(nc, block)
        vt = [nc.alloc_sbuf_tensor(f"vt{i}", [128, T], BF16) for i in range(ND)]
        dst = [nc.alloc_sbuf_tensor(f"dst{i}", [128, T], F32) for i in range(ND)]
        obf = nc.alloc_sbuf_tensor("obf", [128, T], BF16)
        fsw_sb = nc.alloc_sbuf_tensor("fsw_sb", [128, ND * 5], F32)
        flw_sb = nc.alloc_sbuf_tensor("flw_sb", [128, ND * 64], F32)

        p.dma_step("gpsimd", lambda g: [
            g.dma_start(out=fsw_sb[:, :], in_=fsw[:, :]),
            g.dma_start(out=flw_sb[:, :], in_=flw[:, :])])
        p.dma_step("sync", lambda e: [
            e.dma_start(out=vt[i][:, :], in_=v_fm[128 * i:128 * (i + 1), :])
            for i in range(ND)])
        for wsb, ntaps, out in ((fsw_sb, 5, ls_fm), (flw_sb, 64, ll_fm)):
            conv_taps_fm(p, vt, dst,
                         lambda ot, j, wsb=wsb, ntaps=ntaps: wsb[:, ntaps * ot + j:ntaps * ot + j + 1],
                         ntaps)
            for i in range(ND):
                p.step("vector", lambda v, i=i: v.tensor_copy(obf[:, :], dst[i][:, :]))
                p.dma_step("scalar", lambda e, i=i, out=out: [
                    e.dma_start(out=out[128 * i:128 * (i + 1), :], in_=obf[:, :])])
        p.step("gpsimd", lambda g: g.nop())
    return nc


def host_l3_inputs(inputs, l1_out):
    def lay(w, k):
        return np.ascontiguousarray(
            w.reshape(1024, k).reshape(ND, 128, k).transpose(1, 0, 2).reshape(128, ND * k))
    return {"v_fm": l1_out["v_fm"],
            "fsw": lay(inputs["L0_attn_fir_short_w"], 5),
            "flw": lay(inputs["L0_attn_fir_long_w"], 64)}


def build_gate():
    nc = bass.Bass("TRN2", target_bir_lowering=False)
    x_bf = nc.dram_tensor("x_bf", [T, D], BF16, kind="ExternalInput").ap()
    v_tm = nc.dram_tensor("v_tm", [T, D], BF16, kind="ExternalInput").ap()
    delta_tm = nc.dram_tensor("delta_tm", [T, D], BF16, kind="ExternalInput").ap()
    ls_fm = nc.dram_tensor("ls_fm", [D, T], BF16, kind="ExternalInput").ap()
    ll_fm = nc.dram_tensor("ll_fm", [D, T], BF16, kind="ExternalInput").ap()
    mlp1T = nc.dram_tensor("mlp1T", [1040, D], F32, kind="ExternalInput").ap()
    mlp1b = nc.dram_tensor("mlp1b", [128, ND], F32, kind="ExternalInput").ap()
    mlp2T = nc.dram_tensor("mlp2T", [D, H], F32, kind="ExternalInput").ap()
    mlp2b = nc.dram_tensor("mlp2b", [1, H], F32, kind="ExternalInput").ap()
    ltemp = nc.dram_tensor("ltemp", [1, 1], F32, kind="ExternalInput").ap()
    crl = nc.dram_tensor("crl", [1, H], F32, kind="ExternalInput").ap()
    ident = nc.dram_tensor("ident", [128, 128], F32, kind="ExternalInput").ap()
    o_tm = nc.dram_tensor("o_tm", [T, D], BF16, kind="ExternalOutput").ap()

    GB = 512

    with nc.Block() as block:
        p = P(nc, block)
        xT = [nc.alloc_sbuf_tensor(f"xT{i}", [128, T], BF16) for i in range(ND)]
        w1 = [nc.alloc_sbuf_tensor(f"w1_{i}", [128, D], BF16) for i in range(ND)]
        w1s = nc.alloc_sbuf_tensor("w1s", [16, D], BF16)
        w2 = [nc.alloc_sbuf_tensor(f"w2_{i}", [128, H], BF16) for i in range(ND)]
        b1 = nc.alloc_sbuf_tensor("b1", [128, ND], F32)
        b2 = nc.alloc_sbuf_tensor("b2", [1, H], BF16)
        onesr = nc.alloc_sbuf_tensor("onesr", [1, 128], BF16)
        id_sb = nc.alloc_sbuf_tensor("id_sb", [128, 128], F32)
        ltp = nc.alloc_sbuf_tensor("ltp", [1, 8], F32)
        crl_sb = nc.alloc_sbuf_tensor("crl_sb", [1, H], F32)
        scrl_bc = nc.alloc_sbuf_tensor("scrl_bc", [128, H], F32)
        itmp_bc = nc.alloc_sbuf_tensor("itmp_bc", [128, 1], F32)
        one1 = nc.alloc_sbuf_tensor("one1", [1, 1], F32)
        onesf = nc.alloc_sbuf_tensor("onesf", [1, 128], F32)
        nb = 2
        tmb = {nm: [nc.alloc_sbuf_tensor(f"tm_{nm}{i}", [128, D], BF16) for i in range(nb)]
               for nm in ("ls", "ll", "dl", "vv")}
        acc3 = nc.alloc_sbuf_tensor("acc3", [128, 3], F32)
        sscr = nc.alloc_sbuf_tensor("sscr", [128, 256], F32)
        dstat = nc.alloc_sbuf_tensor("dstat", [128, 16], F32)
        statsT = [nc.alloc_sbuf_tensor(f"statsT{h}", [16, T], BF16) for h in range(H)]
        pstat = nc.alloc_psum_tensor("pstat", [16, 128], F32)
        hps = [nc.alloc_psum_tensor(f"hps{i}", [128, GB], F32) for i in range(4)]
        z = nc.alloc_sbuf_tensor("z", [128, GB], F32)
        sq = nc.alloc_sbuf_tensor("sq", [128, GB], F32)
        v2 = nc.alloc_sbuf_tensor("v2", [128, GB], F32)
        ug = nc.alloc_sbuf_tensor("ug", [128, GB], F32)
        h1b = [nc.alloc_sbuf_tensor(f"h1b{i}", [128, GB], BF16) for i in range(ND)]
        plog = nc.alloc_psum_tensor("plog", [128, H], F32)
        lgt = nc.alloc_sbuf_tensor("lgt", [128, H], F32)
        p_all = nc.alloc_sbuf_tensor("p_all", [128, NT * H, 1 * H], F32)
        psc = nc.alloc_sbuf_tensor("psc", [128, 8], F32)
        otile = nc.alloc_sbuf_tensor("otile", [128, D], F32)
        obf = nc.alloc_sbuf_tensor("obf", [128, D], BF16)

        p.dma_step("gpsimd", lambda g: [
            g.dma_start(out=w1[i][:, :], in_=mlp1T[128 * i:128 * (i + 1), :]) for i in range(ND)] + [
            g.dma_start(out=w1s[:, :], in_=mlp1T[1024:1040, :]),
            g.dma_start(out=b1[:, :], in_=mlp1b[:, :]),
            g.dma_start(out=b2[:, :], in_=mlp2b[:, :]),
            g.dma_start(out=ltp[0:1, 0:1], in_=ltemp[:, :]),
            g.dma_start(out=crl_sb[:, :], in_=crl[:, :]),
            g.dma_start(out=id_sb[:, :], in_=ident[:, :])])
        p.dma_step("gpsimd", lambda g: [
            g.dma_start(out=w2[i][:, :], in_=mlp2T[128 * i:128 * (i + 1), :]) for i in range(ND)])
        p.dma_step("sync", lambda e: [
            e.dma_start(out=xT[i][:, :], in_=x_bf[:, 128 * i:128 * (i + 1)], transpose=True)
            for i in range(ND)])
        p.step("vector", lambda v: v.memset(onesr[:, :], 1.0))
        p.step("vector", lambda v: v.memset(onesf[:, :], 1.0))
        p.step("vector", lambda v: v.memset(one1[:, :], 1.0))

        def scl1(se):
            se.activation(out=ltp[0:1, 1:2], in_=ltp[0:1, 0:1], func=AF.Abs)
            return se.activation(out=ltp[0:1, 2:3], in_=ltp[0:1, 1:2], func=AF.Exp, scale=-1.0)
        p.step("scalar", scl1)
        p.step("scalar", lambda se: se.activation(out=ltp[0:1, 3:4], in_=ltp[0:1, 2:3],
                                                  func=AF.Ln, bias=one1[0:1, 0:1]))
        p.step("vector", lambda v: v.tensor_scalar(ltp[0:1, 4:5], ltp[0:1, 0:1], 0.0, None, ALU.max))
        p.step("vector", lambda v: v.tensor_add(ltp[0:1, 5:6], ltp[0:1, 3:4], ltp[0:1, 4:5]))
        p.step("vector", lambda v: v.reciprocal(ltp[0:1, 6:7], ltp[0:1, 5:6]))
        p.step("scalar", lambda se: se.activation(out=ltp[0:1, 7:8], in_=ltp[0:1, 6:7], func=AF.Copy))
        p.step("scalar", lambda se: se.activation(out=crl_sb[0:1, :], in_=crl_sb[0:1, :], func=AF.Sigmoid))

        def bc_mm(t):
            t.matmul(itmp_bc[:, :] if False else pstat[0:128, 0:1], onesf[:, :], ltp[0:1, 6:7],
                     start=True, stop=True)
            return t.matmul(pstat[0:128, 1:5] if False else plog[:, 0:4], onesf[:, :],
                            crl_sb[0:1, :], start=True, stop=True)
        def bc_mm2(t):
            t.matmul(hps[0][:, 0:1], onesf[:, :], ltp[0:1, 6:7], start=True, stop=True)
            return t.matmul(plog[:, :], onesf[:, :], crl_sb[0:1, :], start=True, stop=True)
        p.step("tensor", bc_mm2)
        p.step("vector", lambda v: v.tensor_copy(itmp_bc[:, :], hps[0][:, 0:1]))
        p.step("vector", lambda v: v.tensor_copy(scrl_bc[:, :], plog[:, :]))

        eps5 = eps_ap(nc, p, 1e-5, "eps5")

        for b0 in range(0, NT, nb):
            def ld(e, b0=b0):
                out = []
                for i in range(nb):
                    tt = b0 + i
                    out += [e.dma_start(out=tmb["ls"][i][:, :], in_=ls_fm[:, 128 * tt:128 * (tt + 1)], transpose=True),
                            e.dma_start(out=tmb["ll"][i][:, :], in_=ll_fm[:, 128 * tt:128 * (tt + 1)], transpose=True),
                            e.dma_start(out=tmb["dl"][i][:, :], in_=delta_tm[128 * tt:128 * (tt + 1), :]),
                            e.dma_start(out=tmb["vv"][i][:, :], in_=v_tm[128 * tt:128 * (tt + 1), :])]
                return out
            p.dma_step("sync", ld)
            for i in range(nb):
                tt = b0 + i
                for h in range(H):
                    def stat_acc(se, i=i, h=h):
                        last = None
                        for ti, nm in enumerate(("ls", "ll", "dl", "vv")):
                            seg = tmb[nm][i][:, 256 * h:256 * (h + 1)]
                            se.activation(out=sscr[:, :], in_=seg, func=AF.Copy,
                                          accum_out=acc3[:, 0:1])
                            se.activation(out=sscr[:, :], in_=seg, func=AF.Square,
                                          accum_out=acc3[:, 1:2])
                            last = se.activation(out=sscr[:, :], in_=seg, func=AF.Abs,
                                                 accum_out=acc3[:, 2:3])
                        return last
                    def stat_one(p=p, i=i, h=h):
                        for ti, nm in enumerate(("ls", "ll", "dl", "vv")):
                            seg_ap = (nm, ti)

                            def acc_f(se, i=i, h=h, nm=nm):
                                seg = tmb[nm][i][:, 256 * h:256 * (h + 1)]
                                se.activation(out=sscr[:, :], in_=seg, func=AF.Copy,
                                              accum_out=acc3[:, 0:1])
                                se.activation(out=sscr[:, :], in_=seg, func=AF.Square,
                                              accum_out=acc3[:, 1:2])
                                return se.activation(out=sscr[:, :], in_=seg, func=AF.Abs,
                                                     accum_out=acc3[:, 2:3])
                            p.step("scalar", acc_f)

                            def der_f(v, ti=ti):
                                v.tensor_scalar(dstat[:, 4 * ti + 0:4 * ti + 1], acc3[:, 0:1],
                                                1.0 / 256, None, ALU.mult)
                                v.tensor_scalar(dstat[:, 4 * ti + 2:4 * ti + 3], acc3[:, 2:3],
                                                1.0 / 256, None, ALU.mult)
                                return v.tensor_scalar(dstat[:, 4 * ti + 1:4 * ti + 2], acc3[:, 1:2],
                                                       1.0 / 256, None, ALU.mult)
                            p.step("vector", der_f)

                            def der2_f(v, ti=ti):
                                v.scalar_tensor_tensor(dstat[:, 4 * ti + 1:4 * ti + 2],
                                                       dstat[:, 4 * ti + 0:4 * ti + 1],
                                                       dstat[:, 4 * ti + 0:4 * ti + 1],
                                                       dstat[:, 4 * ti + 1:4 * ti + 2],
                                                       ALU.mult, ALU.subtract)
                                return v.nop()
                    stat_one()

                    def fix_var(v, i=i, h=h):
                        last = None
                        for ti in range(4):
                            v.tensor_mul(acc3[:, 0:1], dstat[:, 4 * ti:4 * ti + 1],
                                         dstat[:, 4 * ti:4 * ti + 1])
                            last = v.tensor_sub(dstat[:, 4 * ti + 1:4 * ti + 2],
                                                dstat[:, 4 * ti + 1:4 * ti + 2], acc3[:, 0:1])
                        return last
                    def mk_m2(v, i=i, h=h):
                        last = None
                        for ti in range(4):
                            last = v.tensor_mul(psc[:, ti:ti + 1], dstat[:, 4 * ti:4 * ti + 1],
                                                dstat[:, 4 * ti:4 * ti + 1])
                        return last
                    p.step("vector", mk_m2)

                    def mk_var(v, i=i, h=h):
                        last = None
                        for ti in range(4):
                            last = v.tensor_sub(dstat[:, 4 * ti + 1:4 * ti + 2],
                                                dstat[:, 4 * ti + 1:4 * ti + 2], psc[:, ti:ti + 1])
                        return last
                    p.step("vector", mk_var)

                    def mk_l2(se, i=i, h=h):
                        last = None
                        for ti in range(4):
                            last = se.activation(out=dstat[:, 4 * ti + 3:4 * ti + 4],
                                                 in_=dstat[:, 4 * ti + 1:4 * ti + 2]
                                                 if False else sscr[:, 0:1], func=AF.Copy)
                        return last
                    def mk_l2b(v, i=i, h=h):
                        last = None
                        for ti in range(4):
                            last = v.tensor_add(acc3[:, 0:1] if False else psc[:, 4 + (ti % 4):5 + (ti % 4)],
                                                dstat[:, 4 * ti + 1:4 * ti + 2], psc[:, ti:ti + 1])
                        return last
                    def mk_l2c(v, i=i, h=h):
                        last = None
                        for ti in range(4):
                            last = v.scalar_tensor_tensor(psc[:, 4 + ti:5 + ti],
                                                          dstat[:, 4 * ti + 1:4 * ti + 2], 256.0,
                                                          psc[:, ti:ti + 1], ALU.bypass, ALU.add)
                        return last
                    p.step("vector", mk_l2c)

                    def mk_l2d(se, i=i, h=h):
                        last = None
                        for ti in range(4):
                            last = se.activation(out=dstat[:, 4 * ti + 3:4 * ti + 4],
                                                 in_=psc[:, 4 + ti:5 + ti], func=AF.Sqrt,
                                                 scale=256.0)
                        return last
                    p.step("scalar", mk_l2d)

                    def tr(t, i=i, h=h):
                        return t.transpose(pstat[:, :], dstat[:, :], id_sb[:, :])
                    p.step("tensor", tr)

                    def evst(v, tt=tt, h=h):
                        return v.tensor_copy(statsT[h][:, 128 * tt:128 * (tt + 1)], pstat[:, :])
                    p.step("vector", evst)

        for h in range(H):
            for tb in range(8):
                t0 = GB * tb

                for half in range(2):
                    def mm1(t, h=h, t0=t0, half=half):
                        for oi in range(4):
                            ot = 4 * half + oi
                            for k in range(ND):
                                t.matmul(hps[oi][:, :], w1[k][:, 128 * ot:128 * (ot + 1)],
                                         xT[k][:, t0:t0 + GB], start=(k == 0), stop=False)
                            last = t.matmul(hps[oi][:, :], w1s[:, 128 * ot:128 * (ot + 1)],
                                            statsT[h][:, t0:t0 + GB], start=False, stop=True)
                        return last
                    p.step("tensor", mm1)

                    for oi in range(4):
                        ot = 4 * half + oi
                        p.step("vector", lambda v, oi=oi, ot=ot: v.tensor_scalar(
                            z[:, :], hps[oi][:, :], b1[:, ot:ot + 1], None, ALU.add))
                        p.step("scalar", lambda se: se.activation(out=sq[:, :], in_=z[:, :], func=AF.Square))
                        p.step("vector", lambda v: v.tensor_scalar(sq[:, :], sq[:, :], 0.044715, 1.0,
                                                                   ALU.mult, ALU.add))
                        p.step("vector", lambda v: v.tensor_mul(v2[:, :], z[:, :], sq[:, :]))
                        p.step("scalar", lambda se: se.activation(out=ug[:, :], in_=v2[:, :],
                                                                  func=AF.Sigmoid, scale=1.5957691216057308))
                        p.step("vector", lambda v, ot=ot: v.tensor_mul(h1b[ot][:, :], z[:, :], ug[:, :]))

                for g in range(4):
                    def mm2(t, h=h, t0=t0, g=g):
                        for k in range(ND):
                            t.matmul(plog[:, :], h1b[k][:, 128 * g:128 * (g + 1)], w2[k][:, :],
                                     start=(k == 0), stop=False)
                        return t.matmul(plog[:, :], onesr[:, :], b2[:, :], start=False, stop=True)
                    p.step("tensor", mm2)
                    p.step("vector", lambda v: v.tensor_copy(lgt[:, :], plog[:, :]))
                    p.step("vector", lambda v: v.tensor_reduce(psc[:, 0:1], lgt[:, :], AX.X, ALU.max))
                    p.step("vector", lambda v: v.tensor_scalar(psc[:, 6:7], psc[:, 0:1],
                                                               itmp_bc[:, :], None, ALU.mult))
                    p.step("vector", lambda v: v.tensor_scalar(psc[:, 1:2], psc[:, 6:7],
                                                               -1.0, None, ALU.mult))
                    p.step("scalar", lambda se: se.activation(out=lgt[:, :], in_=lgt[:, :], func=AF.Exp,
                                                              scale=itmp_bc[:, :], bias=psc[:, 1:2],
                                                              accum_out=psc[:, 2:3]))
                    p.step("vector", lambda v: v.reciprocal(psc[:, 3:4], psc[:, 2:3]))
                    p.step("vector", lambda v: v.tensor_scalar(lgt[:, :], lgt[:, :], psc[:, 3:4],
                                                               None, ALU.mult))
                    p.step("vector", lambda v: v.tensor_scalar(lgt[:, :], lgt[:, :], 0.02, None, ALU.max))
                    p.step("vector", lambda v: v.tensor_reduce(psc[:, 4:5], lgt[:, :], AX.X, ALU.add))
                    p.step("vector", lambda v: v.reciprocal(psc[:, 5:6], psc[:, 4:5]))
                    p.step("vector", lambda v, h=h, t0=t0, g=g: v.tensor_scalar(
                        p_all[:, h * NT + (t0 // 128 + g), :], lgt[:, :], psc[:, 5:6], None, ALU.mult))

        def p0adj(v):
            last = None
            for h in range(H):
                last = v.tensor_scalar(p_all[:, h * NT:(h + 1) * NT, 0:1],
                                       p_all[:, h * NT:(h + 1) * NT, 0:1],
                                       scrl_bc[:, h:h + 1], None, ALU.add)
            return last
        p.step("vector", p0adj)

        for b0 in range(0, NT, nb):
            def ld(e, b0=b0):
                out = []
                for i in range(nb):
                    tt = b0 + i
                    out += [e.dma_start(out=tmb["ls"][i][:, :], in_=ls_fm[:, 128 * tt:128 * (tt + 1)], transpose=True),
                            e.dma_start(out=tmb["ll"][i][:, :], in_=ll_fm[:, 128 * tt:128 * (tt + 1)], transpose=True),
                            e.dma_start(out=tmb["dl"][i][:, :], in_=delta_tm[128 * tt:128 * (tt + 1), :]),
                            e.dma_start(out=tmb["vv"][i][:, :], in_=v_tm[128 * tt:128 * (tt + 1), :])]
                return out
            p.dma_step("sync", ld)
            for i in range(nb):
                tt = b0 + i

                def comb(v, i=i, tt=tt):
                    last = None
                    for h in range(H):
                        seg = slice(256 * h, 256 * (h + 1))
                        col = lambda j: p_all[:, h * NT + tt, j:j + 1]
                        v.tensor_scalar(otile[:, seg], tmb["ls"][i][:, seg], col(0), None, ALU.mult)
                        v.scalar_tensor_tensor(otile[:, seg], tmb["ll"][i][:, seg], col(1),
                                               otile[:, seg], ALU.mult, ALU.add)
                        v.scalar_tensor_tensor(otile[:, seg], tmb["dl"][i][:, seg], col(2),
                                               otile[:, seg], ALU.mult, ALU.add)
                        last = v.scalar_tensor_tensor(otile[:, seg], tmb["vv"][i][:, seg], col(3),
                                                      otile[:, seg], ALU.mult, ALU.add)
                    return last
                def comb0(v, i=i, tt=tt):
                    last = None
                    for h in range(H):
                        seg = slice(256 * h, 256 * (h + 1))
                        last = v.tensor_scalar(otile[:, seg], tmb["ls"][i][:, seg],
                                               p_all[:, h * NT + tt, 0:1], None, ALU.mult)
                    return last
                p.step("vector", comb0)
                for j, nm in ((1, "ll"), (2, "dl"), (3, "vv")):
                    def combj(v, i=i, tt=tt, j=j, nm=nm):
                        last = None
                        for h in range(H):
                            seg = slice(256 * h, 256 * (h + 1))
                            last = v.scalar_tensor_tensor(otile[:, seg], tmb[nm][i][:, seg],
                                                          p_all[:, h * NT + tt, j:j + 1],
                                                          otile[:, seg], ALU.mult, ALU.add)
                        return last
                    p.step("vector", combj)

                def rms_acc(se, i=i):
                    last = None
                    for h in range(H):
                        last = se.activation(out=sscr[:, :], in_=otile[:, 256 * h:256 * (h + 1)],
                                             func=AF.Square, accum_out=psc[:, h:h + 1])
                    return last
                p.step("scalar", rms_acc)
                p.step("scalar", lambda se: se.activation(out=psc[:, 4:8], in_=psc[:, 0:4],
                                                          func=AF.Sqrt, scale=1.0 / 256,
                                                          bias=eps5))
                p.step("vector", lambda v: v.reciprocal(psc[:, 0:4], psc[:, 4:8]))

                def rms_scale(v, i=i):
                    last = None
                    for h in range(H):
                        last = v.tensor_scalar(obf[:, 256 * h:256 * (h + 1)],
                                               otile[:, 256 * h:256 * (h + 1)],
                                               psc[:, h:h + 1], None, ALU.mult)
                    return last
                p.step("vector", rms_scale)
                p.dma_step("scalar", lambda e, tt=tt: [
                    e.dma_start(out=o_tm[128 * tt:128 * (tt + 1), :], in_=obf[:, :])])
        p.step("gpsimd", lambda g: g.nop())
    return nc


def eps_ap(nc, p, val=1e-5, name="eps5"):
    if not hasattr(p, "_eps"):
        p._eps = {}
    if name not in p._eps:
        t = nc.alloc_sbuf_tensor(name, [128, 1], F32)
        p.step("vector", lambda v: v.memset(t[:, :], val))
        p._eps[name] = t
    return p._eps[name][:, :]


def host_l4a_inputs(inputs, l1_out, l2_out, l3_out):
    mlp1 = inputs["L0_attn_mlp1_w"]
    return {
        "x_bf": l1_out["x_bf"], "v_tm": l1_out["v_tm"], "delta_tm": l2_out["delta_tm"],
        "ls_fm": l3_out["ls_fm"], "ll_fm": l3_out["ll_fm"],
        "mlp1T": np.ascontiguousarray(mlp1.T),
        "mlp1b": np.ascontiguousarray(inputs["L0_attn_mlp1_b"].reshape(ND, 128).T),
        "mlp2T": np.ascontiguousarray(inputs["L0_attn_mlp2_w"].T),
        "mlp2b": np.ascontiguousarray(inputs["L0_attn_mlp2_b"].reshape(1, H)),
        "ltemp": np.ascontiguousarray(inputs["L0_attn_logit_temp"].reshape(1, 1)),
        "crl": np.ascontiguousarray(inputs["L0_attn_conv_res_logit"].reshape(1, H)),
        "ident": np.eye(128, dtype=np.float32),
    }


def ln_steps(p, nc, bufs, xa, out_f32, out_bf=None, width=D):
    stats6, mv, srt, rstd = bufs
    eps5 = p._eps["eps5"][:, :]
    nh = width // 512

    def st(v):
        last = None
        for i in range(nh):
            last = v.bn_stats(stats6[:, 6 * i:6 * (i + 1)], xa[:, 512 * i:512 * (i + 1)])
        return last
    p.step("vector", st)
    p.step("vector", lambda v: v.bn_aggr(mv[:, :], stats6[:, 0:6 * nh]))
    p.step("scalar", lambda se: se.activation(out=srt[:, :], in_=mv[:, 1:2],
                                              func=AF.Sqrt, bias=eps5))
    p.step("vector", lambda v: v.reciprocal(rstd[:, :], srt[:, :]))
    p.step("vector", lambda v: v.tensor_scalar(out_f32[:, :], xa[:, :], mv[:, 0:1],
                                               rstd[:, :], ALU.subtract, ALU.mult))
    if out_bf is not None:
        p.step("vector", lambda v: v.tensor_copy(out_bf[:, :], out_f32[:, :]))


def scale_steps(p, nc, amax8_sb, scr, qmax=32767.0):
    s1, onesf, psb = scr["s1"], scr["onesf"], scr["psb"]
    s_bc, inv_bc = scr["s_bc"], scr["inv_bc"]
    p.step("vector", lambda v: v.tensor_reduce(s1[0:1, 0:1], amax8_sb[:, :], AX.X, ALU.max))
    p.step("vector", lambda v: v.tensor_scalar(s1[0:1, 1:2], s1[0:1, 0:1], 1.0 / qmax, 1e-12,
                                               ALU.mult, ALU.max))
    p.step("vector", lambda v: v.reciprocal(s1[0:1, 2:3], s1[0:1, 1:2]))

    def bc(t):
        t.matmul(psb[:, 0:1], onesf[:, :], s1[0:1, 1:2], start=True, stop=True)
        return t.matmul(psb[:, 1:2], onesf[:, :], s1[0:1, 2:3], start=True, stop=True)
    p.step("tensor", bc)
    p.step("vector", lambda v: v.tensor_copy(s_bc[:, :], psb[:, 0:1]))
    p.step("vector", lambda v: v.tensor_copy(inv_bc[:, :], psb[:, 1:2]))


def amax_final_steps(p, nc, slots, width, scr, out_dram):
    a1, id_sb, pts = scr["s1"], scr["id_f32"], scr["pts"]
    p.step("vector", lambda v: v.tensor_reduce(a1[:, 3:4], slots[:, 0:width], AX.X, ALU.max))
    p.step("tensor", lambda t: t.transpose(pts[:, :], a1[:, 3:4], id_sb[:, :]))
    p.step("vector", lambda v: v.tensor_reduce(a1[0:1, 4:5], pts[0:1, :], AX.X, ALU.max))
    p.dma_step("scalar", lambda e: [e.dma_start(out=out_dram[:, :], in_=a1[0:1, 4:5])])


def build_ow():
    nc = bass.Bass("TRN2", target_bir_lowering=False)
    x = nc.dram_tensor("x", [T, D], F32, kind="ExternalInput").ap()
    o_tm = nc.dram_tensor("o_tm", [T, D], BF16, kind="ExternalInput").ap()
    owT = nc.dram_tensor("owT", [D, D], F32, kind="ExternalInput").ap()
    nw_bc = nc.dram_tensor("nw_bc", [128, ND], F32, kind="ExternalInput").ap()
    ident = nc.dram_tensor("ident", [128, 128], F32, kind="ExternalInput").ap()
    x1 = nc.dram_tensor("x1", [T, D], F32, kind="ExternalOutput").ap()
    x1_bf = nc.dram_tensor("x1_bf", [T, D], BF16, kind="ExternalOutput").ap()
    amax = nc.dram_tensor("amax", [1, 1], F32, kind="ExternalOutput").ap()

    with nc.Block() as block:
        p = P(nc, block)
        oT = [nc.alloc_sbuf_tensor(f"oT{i}", [128, T], BF16) for i in range(ND)]
        wf = nc.alloc_sbuf_tensor("wf", [128, D], F32)
        wb = [nc.alloc_sbuf_tensor(f"wb{i}", [128, D], BF16) for i in range(ND)]
        pss = [nc.alloc_psum_tensor(f"pss{i}", [128, 512], F32) for i in range(2)]
        pts = nc.alloc_psum_tensor("pts", [1, 128], F32)
        xt = nc.alloc_sbuf_tensor("xt", [128, D], F32)
        xa = nc.alloc_sbuf_tensor("xa", [128, D], F32)
        x1t = nc.alloc_sbuf_tensor("x1t", [128, D], F32)
        x1b = nc.alloc_sbuf_tensor("x1b", [128, D], BF16)
        stats6 = nc.alloc_sbuf_tensor("stats6", [128, 12], F32)
        mv = nc.alloc_sbuf_tensor("mv", [128, 2], F32)
        srt = nc.alloc_sbuf_tensor("srt", [128, 1], F32)
        rstd = nc.alloc_sbuf_tensor("rstd", [128, 1], F32)
        slots = nc.alloc_sbuf_tensor("slots", [128, NT], F32)
        s1 = nc.alloc_sbuf_tensor("s1", [128, 8], F32)
        id_sb = nc.alloc_sbuf_tensor("id_sb", [128, 128], F32)
        eps_ap(nc, p, 1e-5, "eps5")

        p.dma_step("sync", lambda e: [
            e.dma_start(out=oT[i][:, :], in_=o_tm[:, 128 * i:128 * (i + 1)], transpose=True)
            for i in range(ND)] + [e.dma_start(out=id_sb[:, :], in_=ident[:, :])])
        nw_sb = nc.alloc_sbuf_tensor("nw_sb", [128, ND], F32)
        p.dma_step("gpsimd", lambda g: [g.dma_start(out=nw_sb[:, :], in_=nw_bc[:, :])])
        for i in range(ND):
            p.dma_step("gpsimd", lambda g, i=i: [
                g.dma_start(out=wf[:, :], in_=owT[128 * i:128 * (i + 1), :])])
            p.step("vector", lambda v, i=i: v.tensor_scalar(wb[i][:, :], wf[:, :],
                                                            nw_sb[:, i:i + 1], None, ALU.mult))

        lnb = (stats6, mv, srt, rstd)
        for tt in range(NT):
            p.dma_step("sync", lambda e, tt=tt: [
                e.dma_start(out=xt[:, :], in_=x[128 * tt:128 * (tt + 1), :])])

            def mm(t, tt=tt):
                for half in range(2):
                    for k in range(ND):
                        last = t.matmul(pss[half][:, :], oT[k][:, 128 * tt:128 * (tt + 1)],
                                        wb[k][:, 512 * half:512 * (half + 1)],
                                        start=(k == 0), stop=(k == ND - 1))
                return last
            p.step("tensor", mm)

            def ev(v, tt=tt):
                v.tensor_add(xa[:, 0:512], pss[0][:, :], xt[:, 0:512])
                return v.tensor_add(xa[:, 512:1024], pss[1][:, :], xt[:, 512:1024])
            p.step("vector", ev)
            ln_steps(p, nc, lnb, xa, x1t, x1b)
            p.step("vector", lambda v, tt=tt: v.tensor_reduce(slots[:, tt:tt + 1], x1t[:, :],
                                                              AX.X, ALU.max,
                                                              apply_absolute_value=True))
            p.dma_step("scalar", lambda e, tt=tt: [
                e.dma_start(out=x1[128 * tt:128 * (tt + 1), :], in_=x1t[:, :]),
                e.dma_start(out=x1_bf[128 * tt:128 * (tt + 1), :], in_=x1b[:, :])])
        amax_final_steps(p, nc, slots, NT, {"s1": s1, "id_f32": id_sb, "pts": pts}, amax)
        p.step("gpsimd", lambda g: g.nop())
    return nc


def host_l4b_inputs(inputs, l4a_out, core):
    nw = inputs["L0_attn_norm_w"]
    nw_full = np.tile(nw, D // 256)
    return {
        "x": np.ascontiguousarray(inputs["L0_attn_hidden_states"][core]).astype(np.float32, copy=False),
        "o_tm": l4a_out["o_tm"],
        "owT": np.ascontiguousarray(inputs["L0_attn_o_w"].T),
        "nw_bc": np.ascontiguousarray(nw_full.reshape(ND, 128).T),
        "ident": np.eye(128, dtype=np.float32),
    }


def _dequant_steps(p, nc, w_dram, rows, I, bufs, out_dram):
    wf, wi, wbf, bmax, sw, inv = bufs
    nblk = I // 32
    for r in range(rows // 128):
        p.dma_step("gpsimd", lambda g, r=r: [
            g.dma_start(out=wf[:, 0:I], in_=w_dram[128 * r:128 * (r + 1), :])])
        p.step("vector", lambda v: v.tensor_reduce(
            bmax[:, 0:nblk], wf[:, 0:I].rearrange("p (b k) -> p b k", k=32), AX.X, ALU.max,
            apply_absolute_value=True))
        p.step("vector", lambda v: v.tensor_scalar(sw[:, 0:nblk], bmax[:, 0:nblk],
                                                   1.0 / 7, 1e-12, ALU.mult, ALU.max))
        p.step("vector", lambda v: v.reciprocal(inv[:, 0:nblk], sw[:, 0:nblk]))
        p.step("vector", lambda v: v.tensor_tensor(
            wi[:, 0:I].rearrange("p (b k) -> p b k", k=32),
            wf[:, 0:I].rearrange("p (b k) -> p b k", k=32),
            inv[:, 0:nblk].unsqueeze(2).broadcast_to([128, nblk, 32]), ALU.mult))
        p.step("vector", lambda v: v.tensor_tensor(
            wbf[:, 0:I].rearrange("p (b k) -> p b k", k=32),
            wi[:, 0:I].rearrange("p (b k) -> p b k", k=32),
            sw[:, 0:nblk].unsqueeze(2).broadcast_to([128, nblk, 32]), ALU.mult))
        p.dma_step("scalar", lambda e, r=r: [
            e.dma_start(out=out_dram[128 * r:128 * (r + 1), :], in_=wbf[:, 0:I])])


def build_ffn_up():
    nc = bass.Bass("TRN2", target_bir_lowering=False)
    x1_bf = nc.dram_tensor("x1_bf", [T, D], BF16, kind="ExternalInput").ap()
    w_up = nc.dram_tensor("w_up", [DFF, D], F32, kind="ExternalInput").ap()
    w_gate = nc.dram_tensor("w_gate", [DFF, D], F32, kind="ExternalInput").ap()
    w_down = nc.dram_tensor("w_down", [D, DFF], F32, kind="ExternalInput").ap()
    amax8 = nc.dram_tensor("amax8", [1, 8], F32, kind="ExternalInput").ap()
    ident = nc.dram_tensor("ident", [128, 128], F32, kind="ExternalInput").ap()
    up_st = nc.dram_tensor("up_st", [DFF, T], BF16, kind="ExternalOutput").ap()
    gate_st = nc.dram_tensor("gate_st", [DFF, T], BF16, kind="ExternalOutput").ap()
    wdq_down = nc.dram_tensor("wdq_down", [D, DFF], BF16, kind="ExternalOutput").ap()
    amax_up = nc.dram_tensor("amax_up", [1, 1], F32, kind="ExternalOutput").ap()
    amax_gate = nc.dram_tensor("amax_gate", [1, 1], F32, kind="ExternalOutput").ap()
    wdq_up = nc.dram_tensor("wdq_up", [DFF, D], BF16)
    wdq_gate = nc.dram_tensor("wdq_gate", [DFF, D], BF16)

    with nc.Block() as block:
        p = P(nc, block)
        xqT = [nc.alloc_sbuf_tensor(f"xqT{i}", [128, T], BF16) for i in range(ND)]
        wT = [nc.alloc_sbuf_tensor(f"wT{i}", [128, DFF], BF16) for i in range(ND)]
        wf = nc.alloc_sbuf_tensor("wf", [128, DFF], F32)
        wi = nc.alloc_sbuf_tensor("wi", [128, DFF], I32)
        wbf = nc.alloc_sbuf_tensor("wbf", [128, DFF], BF16)
        bmax = nc.alloc_sbuf_tensor("bmax", [128, 96], F32)
        sw = nc.alloc_sbuf_tensor("sw", [128, 96], F32)
        invw = nc.alloc_sbuf_tensor("invw", [128, 96], F32)
        xb = nc.alloc_sbuf_tensor("xb", [128, T], BF16)
        xi = nc.alloc_sbuf_tensor("xi", [128, T], I32)
        ev_f = nc.alloc_sbuf_tensor("ev_f", [128, 512], F32)
        ob = nc.alloc_sbuf_tensor("ob", [128, T], BF16)
        slots = nc.alloc_sbuf_tensor("slots", [128, 192], F32)
        slots2 = nc.alloc_sbuf_tensor("slots2", [128, 192], F32)
        s1 = nc.alloc_sbuf_tensor("s1", [128, 8], F32)
        onesf = nc.alloc_sbuf_tensor("onesf", [1, 128], F32)
        psb = nc.alloc_psum_tensor("psb", [128, 2], F32)
        pts = nc.alloc_psum_tensor("pts", [1, 128], F32)
        pss = [nc.alloc_psum_tensor(f"pss{i}", [128, 512], F32) for i in range(2)]
        s_bc = nc.alloc_sbuf_tensor("s_bc", [128, 1], F32)
        inv_bc = nc.alloc_sbuf_tensor("inv_bc", [128, 1], F32)
        am8 = nc.alloc_sbuf_tensor("am8", [1, 8], F32)
        id_sb = nc.alloc_sbuf_tensor("id_sb", [128, 128], F32)

        p.dma_step("gpsimd", lambda g: [
            g.dma_start(out=am8[:, :], in_=amax8[:, :]),
            g.dma_start(out=id_sb[:, :], in_=ident[:, :])])
        p.step("vector", lambda v: v.memset(onesf[:, :], 1.0))
        scr = {"s1": s1, "onesf": onesf, "psb": psb, "s_bc": s_bc, "inv_bc": inv_bc,
               "id_f32": id_sb, "pts": pts}
        scale_steps(p, nc, am8, scr)

        for i in range(ND):
            p.dma_step("sync", lambda e, i=i: [
                e.dma_start(out=xb[:, :], in_=x1_bf[:, 128 * i:128 * (i + 1)], transpose=True)])
            p.step("vector", lambda v: v.tensor_scalar(xi[:, :], xb[:, :], inv_bc[:, :],
                                                       None, ALU.mult))
            p.step("vector", lambda v, i=i: v.tensor_scalar(xqT[i][:, :], xi[:, :], s_bc[:, :],
                                                            None, ALU.mult))

        dq_bufs = (wf, wi, wbf, bmax, sw, invw)
        _dequant_steps(p, nc, w_up, DFF, D, dq_bufs, wdq_up)
        _dequant_steps(p, nc, w_gate, DFF, D, dq_bufs, wdq_gate)
        _dequant_steps(p, nc, w_down, D, DFF, dq_bufs, wdq_down)

        for wdq, st_out, slot, am_out in ((wdq_up, up_st, slots, amax_up),
                                          (wdq_gate, gate_st, slots2, amax_gate)):
            p.dma_step("sync", lambda e, wdq=wdq: [
                e.dma_start(out=wT[i][:, :], in_=wdq[:, 128 * i:128 * (i + 1)], transpose=True)
                for i in range(ND)])
            for ot in range(NF):
                def mms(t, tb, bank, ot=ot):
                    for k in range(ND):
                        last = t.matmul(pss[bank][:, :], wT[k][:, 128 * ot:128 * (ot + 1)],
                                        xqT[k][:, 512 * tb:512 * (tb + 1)],
                                        start=(k == 0), stop=(k == ND - 1))
                    return last

                def ev(v, tb, bank, ot=ot, slot=slot):
                    v.tensor_reduce(slot[:, 8 * ot + tb:8 * ot + tb + 1], pss[bank][:, :],
                                    AX.X, ALU.max, apply_absolute_value=True)
                    return v.tensor_copy(ob[:, 512 * tb:512 * (tb + 1)], pss[bank][:, :])
                p.mm_pipe(list(range(8)), mms, ev)
                p.dma_step("scalar", lambda e, ot=ot, st_out=st_out: [
                    e.dma_start(out=st_out[128 * ot:128 * (ot + 1), :], in_=ob[:, :])])
            amax_final_steps(p, nc, slot, 192, scr, am_out)
        p.step("gpsimd", lambda g: g.nop())
    return nc


def _amax8(outs_all, key):
    vals = [np.asarray(outs_all[c][key]).reshape(-1)[0] for c in range(len(outs_all))]
    vals = vals + [np.float32(0.0)] * (8 - len(vals))
    return np.ascontiguousarray(np.asarray(vals, np.float32).reshape(1, 8))


def host_l5_inputs(inputs, l4b_outs_all, core):
    return {
        "x1_bf": l4b_outs_all[core]["x1_bf"],
        "w_up": np.ascontiguousarray(inputs["L0_ffn_w_up"]),
        "w_gate": np.ascontiguousarray(inputs["L0_ffn_w_gate"]),
        "w_down": np.ascontiguousarray(inputs["L0_ffn_w_down"]),
        "amax8": _amax8(l4b_outs_all, "amax"),
        "ident": np.eye(128, dtype=np.float32),
    }


def build_qchain_a():
    nc = bass.Bass("TRN2", target_bir_lowering=False)
    up_st = nc.dram_tensor("up_st", [DFF, T], BF16, kind="ExternalInput").ap()
    gate_st = nc.dram_tensor("gate_st", [DFF, T], BF16, kind="ExternalInput").ap()
    am8u = nc.dram_tensor("am8u", [1, 8], F32, kind="ExternalInput").ap()
    am8g = nc.dram_tensor("am8g", [1, 8], F32, kind="ExternalInput").ap()
    ident = nc.dram_tensor("ident", [128, 128], F32, kind="ExternalInput").ap()
    upq_st = nc.dram_tensor("upq_st", [DFF, T], BF16, kind="ExternalOutput").ap()
    g_st = nc.dram_tensor("g_st", [DFF, T], BF16, kind="ExternalOutput").ap()
    amax_g = nc.dram_tensor("amax_g", [1, 1], F32, kind="ExternalOutput").ap()

    with nc.Block() as block:
        p = P(nc, block)
        tb = nc.alloc_sbuf_tensor("tb", [128, T], BF16)
        ti = nc.alloc_sbuf_tensor("ti", [128, T], I32)
        gq = nc.alloc_sbuf_tensor("gq", [128, T], F32)
        sg = nc.alloc_sbuf_tensor("sg", [128, T], F32)
        ob = nc.alloc_sbuf_tensor("ob", [128, T], BF16)
        slots = nc.alloc_sbuf_tensor("slots", [128, NF], F32)
        s1 = nc.alloc_sbuf_tensor("s1", [128, 8], F32)
        s1g = nc.alloc_sbuf_tensor("s1g", [128, 8], F32)
        onesf = nc.alloc_sbuf_tensor("onesf", [1, 128], F32)
        psb = nc.alloc_psum_tensor("psb", [128, 2], F32)
        pts = nc.alloc_psum_tensor("pts", [1, 128], F32)
        su_bc = nc.alloc_sbuf_tensor("su_bc", [128, 1], F32)
        iu_bc = nc.alloc_sbuf_tensor("iu_bc", [128, 1], F32)
        sg_bc = nc.alloc_sbuf_tensor("sg_bc", [128, 1], F32)
        ig_bc = nc.alloc_sbuf_tensor("ig_bc", [128, 1], F32)
        am8u_sb = nc.alloc_sbuf_tensor("am8u_sb", [1, 8], F32)
        am8g_sb = nc.alloc_sbuf_tensor("am8g_sb", [1, 8], F32)
        id_sb = nc.alloc_sbuf_tensor("id_sb", [128, 128], F32)

        p.dma_step("gpsimd", lambda g: [
            g.dma_start(out=am8u_sb[:, :], in_=am8u[:, :]),
            g.dma_start(out=am8g_sb[:, :], in_=am8g[:, :]),
            g.dma_start(out=id_sb[:, :], in_=ident[:, :])])
        p.step("vector", lambda v: v.memset(onesf[:, :], 1.0))
        scale_steps(p, nc, am8u_sb, {"s1": s1, "onesf": onesf, "psb": psb,
                                     "s_bc": su_bc, "inv_bc": iu_bc})
        scale_steps(p, nc, am8g_sb, {"s1": s1g, "onesf": onesf, "psb": psb,
                                     "s_bc": sg_bc, "inv_bc": ig_bc})
        for r in range(NF):
            p.dma_step("sync", lambda e, r=r: [
                e.dma_start(out=tb[:, :], in_=up_st[128 * r:128 * (r + 1), :])])
            p.step("vector", lambda v: v.tensor_scalar(ti[:, :], tb[:, :], iu_bc[:, :], None, ALU.mult))
            p.step("vector", lambda v: v.tensor_scalar(ob[:, :], ti[:, :], su_bc[:, :], None, ALU.mult))
            p.dma_step("scalar", lambda e, r=r: [
                e.dma_start(out=upq_st[128 * r:128 * (r + 1), :], in_=ob[:, :])])
            p.dma_step("sync", lambda e, r=r: [
                e.dma_start(out=tb[:, :], in_=gate_st[128 * r:128 * (r + 1), :])])
            p.step("vector", lambda v: v.tensor_scalar(ti[:, :], tb[:, :], ig_bc[:, :], None, ALU.mult))
            p.step("vector", lambda v: v.tensor_scalar(gq[:, :], ti[:, :], sg_bc[:, :], None, ALU.mult))
            p.step("scalar", lambda se: se.activation(out=sg[:, :], in_=gq[:, :], func=AF.Sigmoid))
            p.step("vector", lambda v: v.tensor_scalar(ti[:, :], sg[:, :], 65536.0, None, ALU.mult))
            p.step("vector", lambda v: v.tensor_scalar(sg[:, :], ti[:, :], 65535.0, 1.0 / 65536,
                                                       ALU.min, ALU.mult))
            p.step("vector", lambda v: v.tensor_mul(ob[:, :], gq[:, :], sg[:, :]))
            p.step("vector", lambda v, r=r: v.tensor_reduce(slots[:, r:r + 1], ob[:, :],
                                                            AX.X, ALU.max, apply_absolute_value=True))
            p.dma_step("scalar", lambda e, r=r: [
                e.dma_start(out=g_st[128 * r:128 * (r + 1), :], in_=ob[:, :])])
        amax_final_steps(p, nc, slots, NF, {"s1": s1, "id_f32": id_sb, "pts": pts}, amax_g)
        p.step("gpsimd", lambda g: g.nop())
    return nc


def build_qchain_b():
    nc = bass.Bass("TRN2", target_bir_lowering=False)
    g_st = nc.dram_tensor("g_st", [DFF, T], BF16, kind="ExternalInput").ap()
    upq_st = nc.dram_tensor("upq_st", [DFF, T], BF16, kind="ExternalInput").ap()
    am8 = nc.dram_tensor("am8", [1, 8], F32, kind="ExternalInput").ap()
    ident = nc.dram_tensor("ident", [128, 128], F32, kind="ExternalInput").ap()
    opre_st = nc.dram_tensor("opre_st", [DFF, T], BF16, kind="ExternalOutput").ap()
    amax_o = nc.dram_tensor("amax_o", [1, 1], F32, kind="ExternalOutput").ap()

    with nc.Block() as block:
        p = P(nc, block)
        tb = nc.alloc_sbuf_tensor("tb", [128, T], BF16)
        ub = nc.alloc_sbuf_tensor("ub", [128, T], BF16)
        ti = nc.alloc_sbuf_tensor("ti", [128, T], I32)
        g2 = nc.alloc_sbuf_tensor("g2", [128, T], F32)
        ob = nc.alloc_sbuf_tensor("ob", [128, T], BF16)
        slots = nc.alloc_sbuf_tensor("slots", [128, NF], F32)
        s1 = nc.alloc_sbuf_tensor("s1", [128, 8], F32)
        onesf = nc.alloc_sbuf_tensor("onesf", [1, 128], F32)
        psb = nc.alloc_psum_tensor("psb", [128, 2], F32)
        pts = nc.alloc_psum_tensor("pts", [1, 128], F32)
        s_bc = nc.alloc_sbuf_tensor("s_bc", [128, 1], F32)
        inv_bc = nc.alloc_sbuf_tensor("inv_bc", [128, 1], F32)
        am8_sb = nc.alloc_sbuf_tensor("am8_sb", [1, 8], F32)
        id_sb = nc.alloc_sbuf_tensor("id_sb", [128, 128], F32)

        p.dma_step("gpsimd", lambda g: [
            g.dma_start(out=am8_sb[:, :], in_=am8[:, :]),
            g.dma_start(out=id_sb[:, :], in_=ident[:, :])])
        p.step("vector", lambda v: v.memset(onesf[:, :], 1.0))
        scale_steps(p, nc, am8_sb, {"s1": s1, "onesf": onesf, "psb": psb,
                                    "s_bc": s_bc, "inv_bc": inv_bc})
        for r in range(NF):
            p.dma_step("sync", lambda e, r=r: [
                e.dma_start(out=tb[:, :], in_=g_st[128 * r:128 * (r + 1), :]),
                e.dma_start(out=ub[:, :], in_=upq_st[128 * r:128 * (r + 1), :])])
            p.step("vector", lambda v: v.tensor_scalar(ti[:, :], tb[:, :], inv_bc[:, :], None, ALU.mult))
            p.step("vector", lambda v: v.tensor_scalar(g2[:, :], ti[:, :], s_bc[:, :], None, ALU.mult))
            p.step("vector", lambda v: v.tensor_mul(ob[:, :], g2[:, :], ub[:, :]))
            p.step("vector", lambda v, r=r: v.tensor_reduce(slots[:, r:r + 1], ob[:, :],
                                                            AX.X, ALU.max, apply_absolute_value=True))
            p.dma_step("scalar", lambda e, r=r: [
                e.dma_start(out=opre_st[128 * r:128 * (r + 1), :], in_=ob[:, :])])
        amax_final_steps(p, nc, slots, NF, {"s1": s1, "id_f32": id_sb, "pts": pts}, amax_o)
        p.step("gpsimd", lambda g: g.nop())
    return nc


def build_qchain_c():
    nc = bass.Bass("TRN2", target_bir_lowering=False)
    opre_st = nc.dram_tensor("opre_st", [DFF, T], BF16, kind="ExternalInput").ap()
    wdq_down = nc.dram_tensor("wdq_down", [D, DFF], BF16, kind="ExternalInput").ap()
    x1 = nc.dram_tensor("x1", [T, D], F32, kind="ExternalInput").ap()
    am8 = nc.dram_tensor("am8", [1, 8], F32, kind="ExternalInput").ap()
    x2 = nc.dram_tensor("x2", [T, D], F32, kind="ExternalOutput").ap()
    x2_bf = nc.dram_tensor("x2_bf", [T, D], BF16, kind="ExternalOutput").ap()
    scaled_bf = nc.dram_tensor("scaled_bf", [T, D], BF16, kind="ExternalOutput").ap()
    norm_st = nc.dram_tensor("norm_st", [128, NT], F32, kind="ExternalOutput").ap()

    with nc.Block() as block:
        p = P(nc, block)
        wdT = [nc.alloc_sbuf_tensor(f"wdT{i}", [128, D], BF16) for i in range(NF)]
        oslc = nc.alloc_sbuf_tensor("oslc", [128, NF, 128], BF16)
        osli = nc.alloc_sbuf_tensor("osli", [128, NF, 128], I32)
        oslq = nc.alloc_sbuf_tensor("oslq", [128, NF, 128], BF16)
        xt = nc.alloc_sbuf_tensor("xt", [128, D], F32)
        xa = nc.alloc_sbuf_tensor("xa", [128, D], F32)
        x2t = nc.alloc_sbuf_tensor("x2t", [128, D], F32)
        x2b = nc.alloc_sbuf_tensor("x2b", [128, D], BF16)
        scb = nc.alloc_sbuf_tensor("scb", [128, D], BF16)
        stats6 = nc.alloc_sbuf_tensor("stats6", [128, 12], F32)
        mv = nc.alloc_sbuf_tensor("mv", [128, 2], F32)
        srt = nc.alloc_sbuf_tensor("srt", [128, 1], F32)
        rstd = nc.alloc_sbuf_tensor("rstd", [128, 1], F32)
        nrm = nc.alloc_sbuf_tensor("nrm", [128, NT], F32)
        sqs = nc.alloc_sbuf_tensor("sqs", [128, 1], F32)
        s1 = nc.alloc_sbuf_tensor("s1", [128, 8], F32)
        onesf = nc.alloc_sbuf_tensor("onesf", [1, 128], F32)
        psb = nc.alloc_psum_tensor("psb", [128, 2], F32)
        pss = [nc.alloc_psum_tensor(f"pss{i}", [128, 512], F32) for i in range(2)]
        s_bc = nc.alloc_sbuf_tensor("s_bc", [128, 1], F32)
        inv_bc = nc.alloc_sbuf_tensor("inv_bc", [128, 1], F32)
        am8_sb = nc.alloc_sbuf_tensor("am8_sb", [1, 8], F32)
        sscr = nc.alloc_sbuf_tensor("sscr", [128, D], F32)
        eps_ap(nc, p, 1e-5, "eps5")

        p.dma_step("gpsimd", lambda g: [g.dma_start(out=am8_sb[:, :], in_=am8[:, :])])
        p.step("vector", lambda v: v.memset(onesf[:, :], 1.0))
        scale_steps(p, nc, am8_sb, {"s1": s1, "onesf": onesf, "psb": psb,
                                    "s_bc": s_bc, "inv_bc": inv_bc})
        p.dma_step("sync", lambda e: [
            e.dma_start(out=wdT[i][:, :], in_=wdq_down[:, 128 * i:128 * (i + 1)], transpose=True)
            for i in range(NF)])
        lnb = (stats6, mv, srt, rstd)
        for tt in range(NT):
            p.dma_step("sync", lambda e, tt=tt: [
                e.dma_start(out=oslc[:, :, :],
                            in_=opre_st[:, 128 * tt:128 * (tt + 1)].rearrange("(a p) c -> p a c", p=128)),
                e.dma_start(out=xt[:, :], in_=x1[128 * tt:128 * (tt + 1), :])])
            p.step("vector", lambda v: v.tensor_scalar(osli[:, :, :], oslc[:, :, :],
                                                       inv_bc[:, :], None, ALU.mult))
            p.step("vector", lambda v: v.tensor_scalar(oslq[:, :, :], osli[:, :, :],
                                                       s_bc[:, :], None, ALU.mult))

            def mm(t, tt=tt):
                for half in range(2):
                    for k in range(NF):
                        last = t.matmul(pss[half][:, :], oslq[:, k, :],
                                        wdT[k][:, 512 * half:512 * (half + 1)],
                                        start=(k == 0), stop=(k == NF - 1))
                return last
            p.step("tensor", mm)

            def ev(v, tt=tt):
                v.tensor_add(xa[:, 0:512], pss[0][:, :], xt[:, 0:512])
                return v.tensor_add(xa[:, 512:1024], pss[1][:, :], xt[:, 512:1024])
            p.step("vector", ev)
            ln_steps(p, nc, lnb, xa, x2t, x2b)
            p.step("scalar", lambda se: se.activation(out=sscr[:, :], in_=x2t[:, :],
                                                      func=AF.Square, accum_out=sqs[:, :]))
            p.step("scalar", lambda se, tt=tt: se.activation(out=nrm[:, tt:tt + 1], in_=sqs[:, :],
                                                             func=AF.Sqrt))
            p.step("vector", lambda v, tt=tt: v.tensor_scalar(srt[:, :], nrm[:, tt:tt + 1],
                                                              1e-9, None, ALU.add))
            p.step("vector", lambda v: v.reciprocal(rstd[:, :], srt[:, :]))
            p.step("vector", lambda v: v.tensor_scalar(scb[:, :], x2t[:, :], rstd[:, :],
                                                       None, ALU.mult))
            p.dma_step("scalar", lambda e, tt=tt: [
                e.dma_start(out=x2[128 * tt:128 * (tt + 1), :], in_=x2t[:, :]),
                e.dma_start(out=x2_bf[128 * tt:128 * (tt + 1), :], in_=x2b[:, :]),
                e.dma_start(out=scaled_bf[128 * tt:128 * (tt + 1), :], in_=scb[:, :])])
        p.dma_step("scalar", lambda e: [e.dma_start(out=norm_st[:, :], in_=nrm[:, :])])
        p.step("gpsimd", lambda g: g.nop())
    return nc


def build_l1pre():
    nc = bass.Bass("TRN2", target_bir_lowering=False)
    scaled_bf = nc.dram_tensor("scaled_bf", [T, D], BF16, kind="ExternalInput").ap()
    WqT = nc.dram_tensor("WqT", [D, D], F32, kind="ExternalInput").ap()
    WvT = nc.dram_tensor("WvT", [D, D], F32, kind="ExternalInput").ap()
    bq = nc.dram_tensor("bq", [128, ND], F32, kind="ExternalInput").ap()
    bv1024 = nc.dram_tensor("bv1024", [1, D], F32, kind="ExternalInput").ap()
    Amat = nc.dram_tensor("Amat", [D, D], F32, kind="ExternalInput").ap()
    qT_st = nc.dram_tensor("qT_st", [D, T], BF16, kind="ExternalOutput").ap()
    tqT_st = nc.dram_tensor("tqT_st", [D, T], BF16, kind="ExternalOutput").ap()
    v_tm_st = nc.dram_tensor("v_tm_st", [T, D], BF16, kind="ExternalOutput").ap()

    with nc.Block() as block:
        p = P(nc, block)
        scT = [nc.alloc_sbuf_tensor(f"scT{i}", [128, T], BF16) for i in range(ND)]
        qT = [nc.alloc_sbuf_tensor(f"qT{i}", [128, T], BF16) for i in range(ND)]
        wb = [nc.alloc_sbuf_tensor(f"wb{i}", [128, D], BF16) for i in range(ND)]
        bq_sb = nc.alloc_sbuf_tensor("bq_sb", [128, ND], F32)
        bv_sb = nc.alloc_sbuf_tensor("bv_sb", [1, D], BF16)
        onesr = nc.alloc_sbuf_tensor("onesr", [1, 128], BF16)
        pss = [nc.alloc_psum_tensor(f"pss{i}", [128, 512], F32) for i in range(2)]
        ob = nc.alloc_sbuf_tensor("ob", [128, T], BF16)
        vtb = nc.alloc_sbuf_tensor("vtb", [128, D], BF16)

        p.dma_step("sync", lambda e: [
            e.dma_start(out=scT[i][:, :], in_=scaled_bf[:, 128 * i:128 * (i + 1)], transpose=True)
            for i in range(ND)])
        p.dma_step("gpsimd", lambda g: [
            g.dma_start(out=bq_sb[:, :], in_=bq[:, :]),
            g.dma_start(out=bv_sb[:, :], in_=bv1024[:, :])])
        p.step("vector", lambda v: v.memset(onesr[:, :], 1.0))

        p.dma_step("gpsimd", lambda g: [
            g.dma_start(out=wb[i][:, :], in_=WqT[128 * i:128 * (i + 1), :]) for i in range(ND)])
        for ot in range(ND):
            def mms(t, tb, bank, ot=ot):
                for k in range(ND):
                    last = t.matmul(pss[bank][:, :], wb[k][:, 128 * ot:128 * (ot + 1)],
                                    scT[k][:, 512 * tb:512 * (tb + 1)],
                                    start=(k == 0), stop=(k == ND - 1))
                return last

            def ev(v, tb, bank, ot=ot):
                return v.tensor_scalar(qT[ot][:, 512 * tb:512 * (tb + 1)], pss[bank][:, :],
                                       bq_sb[:, ot:ot + 1], None, ALU.add)
            p.mm_pipe(list(range(8)), mms, ev)
            p.dma_step("scalar", lambda e, ot=ot: [
                e.dma_start(out=qT_st[128 * ot:128 * (ot + 1), :], in_=qT[ot][:, :])])

        p.dma_step("gpsimd", lambda g: [
            g.dma_start(out=wb[i][:, :], in_=Amat[128 * i:128 * (i + 1), :]) for i in range(ND)])
        for ot in range(ND):
            def mms(t, tb, bank, ot=ot):
                for k in range(ND):
                    last = t.matmul(pss[bank][:, :], wb[k][:, 128 * ot:128 * (ot + 1)],
                                    qT[k][:, 512 * tb:512 * (tb + 1)],
                                    start=(k == 0), stop=(k == ND - 1))
                return last

            def ev(v, tb, bank, ot=ot):
                return v.tensor_copy(ob[:, 512 * tb:512 * (tb + 1)], pss[bank][:, :])
            p.mm_pipe(list(range(8)), mms, ev)
            p.dma_step("scalar", lambda e, ot=ot: [
                e.dma_start(out=tqT_st[128 * ot:128 * (ot + 1), :], in_=ob[:, :])])

        p.dma_step("gpsimd", lambda g: [
            g.dma_start(out=wb[i][:, :], in_=WvT[128 * i:128 * (i + 1), :]) for i in range(ND)])
        for tt in range(NT):
            def mmv(t, tt=tt):
                for half in range(2):
                    for k in range(ND):
                        t.matmul(pss[half][:, :], scT[k][:, 128 * tt:128 * (tt + 1)],
                                 wb[k][:, 512 * half:512 * (half + 1)],
                                 start=(k == 0), stop=False)
                    last = t.matmul(pss[half][:, :], onesr[:, :],
                                    bv_sb[:, 512 * half:512 * (half + 1)],
                                    start=False, stop=True)
                return last
            p.step("tensor", mmv)

            def evv(v, tt=tt):
                v.tensor_copy(vtb[:, 0:512], pss[0][:, :])
                return v.tensor_copy(vtb[:, 512:1024], pss[1][:, :])
            p.step("vector", evv)
            p.dma_step("scalar", lambda e, tt=tt: [
                e.dma_start(out=v_tm_st[128 * tt:128 * (tt + 1), :], in_=vtb[:, :])])
        p.step("gpsimd", lambda g: g.nop())
    return nc


def build_l1attn():
    nc = bass.Bass("TRN2", target_bir_lowering=False)
    qT_st = nc.dram_tensor("qT_st", [D, T], BF16, kind="ExternalInput").ap()
    tqT_st = nc.dram_tensor("tqT_st", [D, T], BF16, kind="ExternalInput").ap()
    v_tm_st = nc.dram_tensor("v_tm_st", [T, D], BF16, kind="ExternalInput").ap()
    x2 = nc.dram_tensor("x2", [T, D], F32, kind="ExternalInput").ap()
    norm_st = nc.dram_tensor("norm_st", [128, NT], F32, kind="ExternalInput").ap()
    ident_bf = nc.dram_tensor("ident_bf", [128, 128], BF16, kind="ExternalInput").ap()
    x3 = nc.dram_tensor("x3", [T, D], F32, kind="ExternalOutput").ap()
    x3_bf = nc.dram_tensor("x3_bf", [T, D], BF16, kind="ExternalOutput").ap()

    with nc.Block() as block:
        p = P(nc, block)
        tqT = [nc.alloc_sbuf_tensor(f"tqT{i}", [128, T], BF16) for i in range(ND)]
        vtm = [nc.alloc_sbuf_tensor(f"vtm{i}", [128, D], BF16) for i in range(NT)]
        qsl = nc.alloc_sbuf_tensor("qsl", [128, ND, 128], BF16)
        sco = nc.alloc_sbuf_tensor("sco", [128, T], F32)
        pexp = nc.alloc_sbuf_tensor("pexp", [128, T], BF16)
        pT = nc.alloc_sbuf_tensor("pT", [128, NT, 128], BF16)
        idb = nc.alloc_sbuf_tensor("idb", [128, 128], BF16)
        nrm = nc.alloc_sbuf_tensor("nrm", [128, NT], F32)
        x2t = nc.alloc_sbuf_tensor("x2t", [128, D], F32)
        xa = nc.alloc_sbuf_tensor("xa", [128, D], F32)
        x3t = nc.alloc_sbuf_tensor("x3t", [128, D], F32)
        x3b = nc.alloc_sbuf_tensor("x3b", [128, D], BF16)
        mrow = nc.alloc_sbuf_tensor("mrow", [128, 4], F32)
        stats6 = nc.alloc_sbuf_tensor("stats6", [128, 12], F32)
        mv = nc.alloc_sbuf_tensor("mv", [128, 2], F32)
        srt = nc.alloc_sbuf_tensor("srt", [128, 1], F32)
        rstd = nc.alloc_sbuf_tensor("rstd", [128, 1], F32)
        psc = [nc.alloc_psum_tensor(f"psc{i}", [128, 512], F32) for i in range(2)]
        ptr = nc.alloc_psum_tensor("ptr", [128, 128], BF16)
        pct = [nc.alloc_psum_tensor(f"pct{i}", [128, 512], F32) for i in range(2)]
        eps_ap(nc, p, 1e-5, "eps5")

        p.dma_step("sync", lambda e: [
            e.dma_start(out=tqT[i][:, :], in_=tqT_st[128 * i:128 * (i + 1), :])
            for i in range(ND)] + [
            e.dma_start(out=vtm[i][:, :], in_=v_tm_st[128 * i:128 * (i + 1), :])
            for i in range(NT)] + [
            e.dma_start(out=idb[:, :], in_=ident_bf[:, :]),
            e.dma_start(out=nrm[:, :], in_=norm_st[:, :])])

        for s in range(NT):
            p.dma_step("sync", lambda e, s=s: [
                e.dma_start(out=qsl[:, :, :],
                            in_=qT_st[:, 128 * s:128 * (s + 1)].rearrange("(a p) c -> p a c", p=128)),
                e.dma_start(out=x2t[:, :], in_=x2[128 * s:128 * (s + 1), :])])

            def mmsc(t, b2, bank, s=s):
                for k in range(ND):
                    last = t.matmul(psc[bank][:, :], qsl[:, k, :],
                                    tqT[k][:, 512 * b2:512 * (b2 + 1)],
                                    start=(k == 0), stop=(k == ND - 1))
                return last

            def evsc(v, b2, bank, s=s):
                return v.tensor_copy(sco[:, 512 * b2:512 * (b2 + 1)], psc[bank][:, :])
            p.mm_pipe(list(range(8)), mmsc, evsc)

            p.step("vector", lambda v: v.tensor_reduce(mrow[:, 0:1], sco[:, :], AX.X, ALU.max))
            p.step("vector", lambda v: v.tensor_scalar(mrow[:, 1:2], mrow[:, 0:1],
                                                       -1.0 / 32, None, ALU.mult))
            p.step("scalar", lambda se: se.activation(out=pexp[:, :], in_=sco[:, :], func=AF.Exp,
                                                      scale=1.0 / 32, bias=mrow[:, 1:2],
                                                      accum_out=mrow[:, 2:3]))
            p.step("vector", lambda v: v.reciprocal(mrow[:, 3:4], mrow[:, 2:3]))
            p.step("vector", lambda v, s=s: v.tensor_mul(mrow[:, 0:1], mrow[:, 3:4],
                                                         nrm[:, s:s + 1]))

            for k in range(NT):
                p.step("tensor", lambda t, k=k: t.transpose(ptr[:, :],
                                                            pexp[:, 128 * k:128 * (k + 1)],
                                                            idb[:, :]))
                p.step("vector", lambda v, k=k: v.tensor_copy(pT[:, k, :], ptr[:, :]))

            def mmct(t, s=s):
                for half in range(2):
                    for k in range(NT):
                        last = t.matmul(pct[half][:, :], pT[:, k, :],
                                        vtm[k][:, 512 * half:512 * (half + 1)],
                                        start=(k == 0), stop=(k == NT - 1))
                return last
            p.step("tensor", mmct)

            def evct(v, s=s):
                v.tensor_scalar(xa[:, 0:512], pct[0][:, :], mrow[:, 0:1], None, ALU.mult)
                return v.tensor_scalar(xa[:, 512:1024], pct[1][:, :], mrow[:, 0:1], None, ALU.mult)
            p.step("vector", evct)
            p.step("vector", lambda v: v.tensor_add(xa[:, :], xa[:, :], x2t[:, :]))
            ln_steps(p, nc, (stats6, mv, srt, rstd), xa, x3t, x3b)
            p.dma_step("scalar", lambda e, s=s: [
                e.dma_start(out=x3[128 * s:128 * (s + 1), :], in_=x3t[:, :]),
                e.dma_start(out=x3_bf[128 * s:128 * (s + 1), :], in_=x3b[:, :])])
        p.step("gpsimd", lambda g: g.nop())
    return nc


def build_moe():
    nc = bass.Bass("TRN2", target_bir_lowering=False)
    x3 = nc.dram_tensor("x3", [T, D], F32, kind="ExternalInput").ap()
    x3_bf = nc.dram_tensor("x3_bf", [T, D], BF16, kind="ExternalInput").ap()
    router = nc.dram_tensor("router", [D, 8], F32, kind="ExternalInput").ap()
    eg = nc.dram_tensor("eg", [8, D, 512], F32, kind="ExternalInput").ap()
    eu = nc.dram_tensor("eu", [8, D, 512], F32, kind="ExternalInput").ap()
    ed = nc.dram_tensor("ed", [8, 512, D], F32, kind="ExternalInput").ap()
    sgw = nc.dram_tensor("sgw", [D, D], F32, kind="ExternalInput").ap()
    suw = nc.dram_tensor("suw", [D, D], F32, kind="ExternalInput").ap()
    sdw = nc.dram_tensor("sdw", [D, D], F32, kind="ExternalInput").ap()
    out = nc.dram_tensor("out", [T, D], F32, kind="ExternalOutput").ap()
    sh_act = nc.dram_tensor("sh_act", [D, T], BF16)
    e_act = [nc.dram_tensor(f"e_act{e}", [512, T], BF16) for e in range(8)]

    with nc.Block() as block:
        p = P(nc, block)
        x3T = [nc.alloc_sbuf_tensor(f"x3T{i}", [128, T], BF16) for i in range(ND)]
        wb = [nc.alloc_sbuf_tensor(f"wb{i}", [128, D], BF16) for i in range(ND)]
        wb2 = [nc.alloc_sbuf_tensor(f"wb2{i}", [128, D], BF16) for i in range(ND)]
        rt = nc.alloc_sbuf_tensor("rt", [128, ND, 8], BF16)
        coef = nc.alloc_sbuf_tensor("coef", [128, NT, 8], F32)
        p8 = nc.alloc_sbuf_tensor("p8", [128, 8], F32)
        tmp8 = nc.alloc_sbuf_tensor("tmp8", [128, 8], F32)
        msk = nc.alloc_sbuf_tensor("msk", [128, 8], F32)
        sc8 = nc.alloc_sbuf_tensor("sc8", [128, 6], F32)
        af = nc.alloc_sbuf_tensor("af", [128, T], BF16)
        bf_ = nc.alloc_sbuf_tensor("bf_", [128, T], BF16)
        ob = nc.alloc_sbuf_tensor("ob", [128, T], BF16)
        acc = nc.alloc_sbuf_tensor("acc", [128, D], F32)
        x3t_f = nc.alloc_sbuf_tensor("x3t_f", [128, D], F32)
        o4 = nc.alloc_sbuf_tensor("o4", [128, D], F32)
        slc = nc.alloc_sbuf_tensor("slc", [128, ND, 128], BF16)
        eslc = nc.alloc_sbuf_tensor("eslc", [128, 4, 128], BF16)
        edw = [[nc.alloc_sbuf_tensor(f"edw{e}_{k}", [128, D], BF16) for k in range(4)]
               for e in range(8)]
        stats6 = nc.alloc_sbuf_tensor("stats6", [128, 12], F32)
        mv = nc.alloc_sbuf_tensor("mv", [128, 2], F32)
        srt = nc.alloc_sbuf_tensor("srt", [128, 1], F32)
        rstd = nc.alloc_sbuf_tensor("rstd", [128, 1], F32)
        pss = [nc.alloc_psum_tensor(f"pss{i}", [128, 512], F32) for i in range(2)]
        psr = nc.alloc_psum_tensor("psr", [128, 8], F32)
        eps_ap(nc, p, 1e-5, "eps5")

        p.dma_step("sync", lambda e: [
            e.dma_start(out=x3T[i][:, :], in_=x3_bf[:, 128 * i:128 * (i + 1)], transpose=True)
            for i in range(ND)])
        p.dma_step("gpsimd", lambda g: [
            g.dma_start(out=rt[:, :, :], in_=router.rearrange("(a p) c -> p a c", p=128))])

        for tt in range(NT):
            def mmr(t, tt=tt):
                for k in range(ND):
                    last = t.matmul(psr[:, :], x3T[k][:, 128 * tt:128 * (tt + 1)], rt[:, k, :],
                                    start=(k == 0), stop=(k == ND - 1))
                return last
            p.step("tensor", mmr)
            p.step("vector", lambda v: v.tensor_reduce(sc8[:, 0:1], psr[:, :], AX.X, ALU.max))
            p.step("vector", lambda v: v.tensor_scalar(sc8[:, 1:2], sc8[:, 0:1], -1.0, None, ALU.mult))
            p.step("vector", lambda v: v.tensor_copy(p8[:, :], psr[:, :]))
            p.step("scalar", lambda se: se.activation(out=p8[:, :], in_=p8[:, :], func=AF.Exp,
                                                      bias=sc8[:, 1:2], accum_out=sc8[:, 2:3]))
            p.step("vector", lambda v: v.reciprocal(sc8[:, 3:4], sc8[:, 2:3]))
            p.step("vector", lambda v: v.tensor_scalar(p8[:, :], p8[:, :], sc8[:, 3:4], None, ALU.mult))
            p.step("vector", lambda v: v.tensor_reduce(sc8[:, 4:5], p8[:, :], AX.X, ALU.max))
            p.step("vector", lambda v: v.tensor_scalar(msk[:, :], p8[:, :], sc8[:, 4:5],
                                                       None, ALU.is_ge))
            p.step("vector", lambda v: v.scalar_tensor_tensor(tmp8[:, :], msk[:, :], -2.0,
                                                              p8[:, :], ALU.mult, ALU.add))
            p.step("vector", lambda v: v.tensor_reduce(sc8[:, 5:6], tmp8[:, :], AX.X, ALU.max))
            p.step("vector", lambda v: v.tensor_scalar(msk[:, :], p8[:, :], sc8[:, 5:6],
                                                       None, ALU.is_ge))
            p.step("vector", lambda v, tt=tt: v.tensor_tensor(coef[:, tt, :], p8[:, :],
                                                              msk[:, :], ALU.mult))

        p.dma_step("gpsimd", lambda g: [
            g.dma_start(out=wb[i][:, :], in_=sgw[128 * i:128 * (i + 1), :]) for i in range(ND)] + [
            g.dma_start(out=wb2[i][:, :], in_=suw[128 * i:128 * (i + 1), :]) for i in range(ND)])
        for ot in range(ND):
            def mms(t, tb, bank, ot=ot):
                for k in range(ND):
                    last = t.matmul(pss[bank][:, :], wb[k][:, 128 * ot:128 * (ot + 1)],
                                    x3T[k][:, 512 * tb:512 * (tb + 1)],
                                    start=(k == 0), stop=(k == ND - 1))
                return last

            def ev(se, tb, bank, ot=ot):
                return se.activation(out=af[:, 512 * tb:512 * (tb + 1)], in_=pss[bank][:, :],
                                     func=AF.Silu)
            p.mm_pipe(list(range(8)), mms, ev, evac_engine="scalar")

            def mms2(t, tb, bank, ot=ot):
                for k in range(ND):
                    last = t.matmul(pss[bank][:, :], wb2[k][:, 128 * ot:128 * (ot + 1)],
                                    x3T[k][:, 512 * tb:512 * (tb + 1)],
                                    start=(k == 0), stop=(k == ND - 1))
                return last

            def ev2(v, tb, bank, ot=ot):
                return v.tensor_copy(bf_[:, 512 * tb:512 * (tb + 1)], pss[bank][:, :])
            p.mm_pipe(list(range(8)), mms2, ev2)
            p.step("vector", lambda v: v.tensor_mul(ob[:, :], af[:, :], bf_[:, :]))
            p.dma_step("scalar", lambda e, ot=ot: [
                e.dma_start(out=sh_act[128 * ot:128 * (ot + 1), :], in_=ob[:, :])])

        ew = [wb[i] for i in range(ND)]
        ew2 = [wb2[i] for i in range(ND)]
        for e in range(8):
            p.dma_step("gpsimd", lambda g, e=e: [
                g.dma_start(out=ew[i][:, 0:512], in_=eg[e, 128 * i:128 * (i + 1), :])
                for i in range(ND)] + [
                g.dma_start(out=ew2[i][:, 0:512], in_=eu[e, 128 * i:128 * (i + 1), :])
                for i in range(ND)])
            for ot in range(4):
                def mms(t, tb, bank, ot=ot):
                    for k in range(ND):
                        last = t.matmul(pss[bank][:, :], ew[k][:, 128 * ot:128 * (ot + 1)],
                                        x3T[k][:, 512 * tb:512 * (tb + 1)],
                                        start=(k == 0), stop=(k == ND - 1))
                    return last

                def ev(se, tb, bank, ot=ot):
                    return se.activation(out=af[:, 512 * tb:512 * (tb + 1)], in_=pss[bank][:, :],
                                         func=AF.Silu)
                p.mm_pipe(list(range(8)), mms, ev, evac_engine="scalar")

                def mms2(t, tb, bank, ot=ot):
                    for k in range(ND):
                        last = t.matmul(pss[bank][:, :], ew2[k][:, 128 * ot:128 * (ot + 1)],
                                        x3T[k][:, 512 * tb:512 * (tb + 1)],
                                        start=(k == 0), stop=(k == ND - 1))
                    return last

                def ev2(v, tb, bank, ot=ot):
                    return v.tensor_copy(bf_[:, 512 * tb:512 * (tb + 1)], pss[bank][:, :])
                p.mm_pipe(list(range(8)), mms2, ev2)
                p.step("vector", lambda v: v.tensor_mul(ob[:, :], af[:, :], bf_[:, :]))
                p.dma_step("scalar", lambda ee, ot=ot, e=e: [
                    ee.dma_start(out=e_act[e][128 * ot:128 * (ot + 1), :], in_=ob[:, :])])

        p.dma_step("gpsimd", lambda g: [
            g.dma_start(out=wb[i][:, :], in_=sdw[128 * i:128 * (i + 1), :]) for i in range(ND)])
        p.dma_step("gpsimd", lambda g: [
            g.dma_start(out=edw[e][k][:, :], in_=ed[e, 128 * k:128 * (k + 1), :])
            for e in range(8) for k in range(4)])
        for tt in range(NT):
            p.dma_step("sync", lambda e, tt=tt: [
                e.dma_start(out=slc[:, :, :],
                            in_=sh_act[:, 128 * tt:128 * (tt + 1)].rearrange("(a p) c -> p a c", p=128)),
                e.dma_start(out=x3t_f[:, :], in_=x3[128 * tt:128 * (tt + 1), :])])

            def mmsh(t, tt=tt):
                for half in range(2):
                    for k in range(ND):
                        last = t.matmul(pss[half][:, :], slc[:, k, :],
                                        wb[k][:, 512 * half:512 * (half + 1)],
                                        start=(k == 0), stop=(k == ND - 1))
                return last
            p.step("tensor", mmsh)

            def evsh(v, tt=tt):
                v.tensor_copy(acc[:, 0:512], pss[0][:, :])
                return v.tensor_copy(acc[:, 512:1024], pss[1][:, :])
            p.step("vector", evsh)
            for e in range(8):
                p.dma_step("sync", lambda en, tt=tt, e=e: [
                    en.dma_start(out=eslc[:, :, :],
                                 in_=e_act[e][:, 128 * tt:128 * (tt + 1)].rearrange("(a p) c -> p a c", p=128))])

                def mme(t, tt=tt, e=e):
                    for half in range(2):
                        for k in range(4):
                            last = t.matmul(pss[half][:, :], eslc[:, k, :],
                                            edw[e][k][:, 512 * half:512 * (half + 1)],
                                            start=(k == 0), stop=(k == 3))
                    return last
                p.step("tensor", mme)

                def eve(v, tt=tt, e=e):
                    v.scalar_tensor_tensor(acc[:, 0:512], pss[0][:, :], coef[:, tt, e:e + 1],
                                           acc[:, 0:512], ALU.mult, ALU.add)
                    return v.scalar_tensor_tensor(acc[:, 512:1024], pss[1][:, :],
                                                  coef[:, tt, e:e + 1], acc[:, 512:1024],
                                                  ALU.mult, ALU.add)
                p.step("vector", eve)
            p.step("vector", lambda v: v.tensor_add(acc[:, :], acc[:, :], x3t_f[:, :]))
            ln_steps(p, nc, (stats6, mv, srt, rstd), acc, o4)
            p.dma_step("scalar", lambda e, tt=tt: [
                e.dma_start(out=out[128 * tt:128 * (tt + 1), :], in_=o4[:, :])])
        p.step("gpsimd", lambda g: g.nop())
    return nc


def host_l1_inputs(inputs, core):
    f = np.ascontiguousarray
    return {
        "x": f(inputs["L0_attn_hidden_states"][core]).astype(np.float32, copy=False),
        "wqT": f(inputs["L0_attn_q_w"].T),
        "wkT": f(inputs["L0_attn_k_w"].T),
        "wvT": f(inputs["L0_attn_v_w"].T),
        "bwT": f(inputs["L0_attn_b_w"].T),
        "qcw": f(inputs["L0_attn_q_cw"].reshape(ND, 128, 4).transpose(1, 0, 2).reshape(128, ND * 4)),
        "kcw": f(inputs["L0_attn_k_cw"].reshape(ND, 128, 4).transpose(1, 0, 2).reshape(128, ND * 4)),
        "vcw": f(inputs["L0_attn_v_cw"].reshape(ND, 128, 4).transpose(1, 0, 2).reshape(128, ND * 4)),
    }


def host_l6a_inputs(l5_all, core):
    n = len(l5_all)
    return {
        "up_st": l5_all[core]["up_st"], "gate_st": l5_all[core]["gate_st"],
        "am8u": _amax8(l5_all, "amax_up"),
        "am8g": _amax8(l5_all, "amax_gate"),
        "ident": np.eye(128, dtype=np.float32),
    }


def host_l6b_inputs(l6a_all, core):
    n = len(l6a_all)
    return {
        "g_st": l6a_all[core]["g_st"], "upq_st": l6a_all[core]["upq_st"],
        "am8": _amax8(l6a_all, "amax_g"),
        "ident": np.eye(128, dtype=np.float32),
    }


def host_l6c_inputs(l5_all, l4b_all, l6b_all, core):
    n = len(l6b_all)
    return {
        "opre_st": l6b_all[core]["opre_st"],
        "wdq_down": l5_all[core]["wdq_down"],
        "x1": l4b_all[core]["x1"],
        "am8": _amax8(l6b_all, "amax_o"),
    }


def host_l7a_inputs(inputs, l6c_out):
    return {
        "scaled_bf": l6c_out["scaled_bf"],
        "WqT": np.ascontiguousarray(inputs["L1_attn_Wq"].T),
        "WvT": np.ascontiguousarray(inputs["L1_attn_Wv"].T),
        "bq": np.ascontiguousarray(inputs["L1_attn_bq"].reshape(ND, 128).T),
        "bv1024": np.ascontiguousarray(inputs["L1_attn_bv"].reshape(1, D)),
        "Amat": np.ascontiguousarray(inputs["L1_attn_anomaly_matrix"]),
    }


def host_l7b_inputs(l6c_out, l7a_out):
    return {
        "qT_st": l7a_out["qT_st"], "tqT_st": l7a_out["tqT_st"], "v_tm_st": l7a_out["v_tm_st"],
        "x2": l6c_out["x2"], "norm_st": l6c_out["norm_st"],
        "ident_bf": np.eye(128, dtype=BF),
    }


def host_l8_inputs(inputs, l7b_out):
    return {
        "x3": l7b_out["x3"], "x3_bf": l7b_out["x3_bf"],
        "router": np.ascontiguousarray(inputs["L1_ffn_router"]),
        "eg": np.ascontiguousarray(inputs["L1_ffn_experts_gate"]),
        "eu": np.ascontiguousarray(inputs["L1_ffn_experts_up"]),
        "ed": np.ascontiguousarray(inputs["L1_ffn_experts_down"]),
        "sgw": np.ascontiguousarray(inputs["L1_ffn_shared_gate"]),
        "suw": np.ascontiguousarray(inputs["L1_ffn_shared_up"]),
        "sdw": np.ascontiguousarray(inputs["L1_ffn_shared_down"]),
    }


def kernel(**inputs):
    n = B
    cores = list(range(n))

    def run(nc, maps):
        return run_bass_kernel_spmd(nc, maps, core_ids=cores).results

    r1 = run(build_qkvb(), [host_l1_inputs(inputs, c) for c in cores])
    r2 = run(build_delta(), [host_l2_inputs(r1[c]) for c in cores])
    r3 = run(build_fir(), [host_l3_inputs(inputs, r1[c]) for c in cores])
    r4a = run(build_gate(), [host_l4a_inputs(inputs, r1[c], r2[c], r3[c]) for c in cores])
    r4b = run(build_ow(), [host_l4b_inputs(inputs, r4a[c], c) for c in cores])
    r5 = run(build_ffn_up(), [host_l5_inputs(inputs, r4b, c) for c in cores])
    r6a = run(build_qchain_a(), [host_l6a_inputs(r5, c) for c in cores])
    r6b = run(build_qchain_b(), [host_l6b_inputs(r6a, c) for c in cores])
    r6c = run(build_qchain_c(), [host_l6c_inputs(r5, r4b, r6b, c) for c in cores])
    r7a = run(build_l1pre(), [host_l7a_inputs(inputs, r6c[c]) for c in cores])
    r7b = run(build_l1attn(), [host_l7b_inputs(r6c[c], r7a[c]) for c in cores])
    r8 = run(build_moe(), [host_l8_inputs(inputs, r7b[c]) for c in cores])
    return np.stack([np.asarray(r8[c]["out"]) for c in cores])


# version 02
# speedup vs baseline: 1.0073x; 1.0073x over previous
import math
import numpy as np
import ml_dtypes
import concourse.bass as bass
import concourse.mybir as mybir
from concourse.bass_utils import run_bass_kernel_spmd

F32 = mybir.dt.float32
BF16 = mybir.dt.bfloat16
I32 = mybir.dt.int32
AF = mybir.ActivationFunctionType
ALU = mybir.AluOpType
AX = mybir.AxisListType

B, T, D = 8, 4096, 1024
H, DK, DV = 4, 256, 256
NT = T // 128
ND = D // 128
CHUNK = 32
NCH = T // CHUNK
DFF = 3072
NF = DFF // 128

BF = ml_dtypes.bfloat16


class Sem:

    def __init__(self, nc, name):
        self.h = nc.alloc_semaphore(name)
        self.n = 0

    def inc(self, instr, k=1):
        instr.then_inc(self.h, k)
        self.n += k
        return instr

    def wait(self, eng, target=None):
        eng.wait_ge(self.h, self.n if target is None else target)


class P:

    def __init__(self, nc, block):
        self.nc = nc
        self.block = block
        self.bar = nc.alloc_semaphore("bar")
        self.cnt = 0
        self._pool = {}
        self.phase = ""
        self._stack = None

    def push_phase(self, name):
        from contextlib import ExitStack
        self.phase = name
        self._eps = {}
        self._stack = ExitStack()
        self._stack.__enter__()

    def pop_phase(self):
        self._stack.__exit__(None, None, None)
        self._stack = None

    def sb(self, name, shape, dtype):
        return self._stack.enter_context(
            self.nc.sbuf_tensor(f"{self.phase}_{name}", shape, dtype))

    def pt(self, name, shape, dtype=mybir.dt.float32):
        return self._stack.enter_context(
            self.nc.psum_tensor(f"{self.phase}_{name}", shape, dtype))

    def dram(self, name, shape, dtype):
        return self.nc.dram_tensor(f"{self.phase}_{name}", shape, dtype)

    def sem(self, name):
        if name not in self._pool:
            self._pool[name] = Sem(self.nc, name)
        return self._pool[name]

    def aux_sem(self):
        raise RuntimeError("use p.sem(name) — hardware semaphores are scarce")

    def step(self, engine, fn):
        need = self.cnt

        def body(eng):
            if need:
                eng.wait_ge(self.bar, need)
            last = fn(eng)
            assert last is not None, "step fn must return its last instruction"
            last.then_inc(self.bar, 1)
        getattr(self.block, engine)(body)
        self.cnt += 1

    def dma_step(self, engine, fn):
        need = self.cnt
        cnt_holder = [None]

        def body(eng):
            if need:
                eng.wait_ge(self.bar, need)
            insts = fn(eng)
            for ins in insts:
                ins.then_inc(self.bar, 16)
            cnt_holder[0] = len(insts)
        getattr(self.block, engine)(body)
        assert cnt_holder[0] and cnt_holder[0] > 0
        self.cnt += 16 * cnt_holder[0]

    def mm_pipe(self, jobs, emit_mms, emit_evac, evac_engine="vector", nbanks=2):
        s_mm = self.sem("mmp_mm")
        s_ev = self.sem("mmp_ev")
        mm0, ev0 = s_mm.n, s_ev.n
        start = self.cnt
        njobs = len(jobs)

        def body_t(t):
            if start:
                t.wait_ge(self.bar, start)
            for i, job in enumerate(jobs):
                if i >= nbanks:
                    t.wait_ge(s_ev.h, ev0 + i - nbanks + 1)
                emit_mms(t, job, i % nbanks).then_inc(s_mm.h, 1)

        def body_v(v):
            for i, job in enumerate(jobs):
                v.wait_ge(s_mm.h, mm0 + i + 1)
                emit_evac(v, job, i % nbanks).then_inc(s_ev.h, 1)
            v.wait_ge(s_ev.h, ev0 + njobs)
            v.nop().then_inc(self.bar, 1)
        self.block.tensor(body_t)
        getattr(self.block, evac_engine)(body_v)
        s_mm.n += njobs
        s_ev.n += njobs
        self.cnt += 1


def conv_taps_fm(p, src, dst, wcol_ap_fn, ntaps, width=T):
    n = len(src)

    def tap_al(v):
        last = None
        for ot in range(n):
            last = v.tensor_scalar(dst[ot][:, :], src[ot][:, :],
                                   wcol_ap_fn(ot, ntaps - 1), None, ALU.mult)
        return last
    p.step("vector", tap_al)
    for d in range(1, ntaps):
        def tap_d(v, d=d):
            last = None
            for ot in range(n):
                last = v.scalar_tensor_tensor(
                    dst[ot][:, d:width], src[ot][:, 0:width - d],
                    wcol_ap_fn(ot, ntaps - 1 - d), dst[ot][:, d:width],
                    ALU.mult, ALU.add)
            return last
        p.step("vector", tap_d)


def emit_qkvb(nc, p, dr):
    x = dr["x"]
    wT = dr["wT"]
    bwT = dr["bwT"]
    cw = dr["cw"]
    qn_tm = dr["qn_tm"]
    kn_tm = dr["kn_tm"]
    v_tm = dr["v_tm"]
    v_fm = dr["v_fm"]
    beta_tm = dr["beta_tm"]
    x_bf = dr["x_bf"]
    q_raw = nc.dram_tensor("q_raw", [D, T], BF16)
    k_raw = nc.dram_tensor("k_raw", [D, T], BF16)
    xT = [p.sb(f"xT{i}", [128, T], BF16) for i in range(ND)]
    w_sb = [p.sb(f"wsb{i}", [128, D], BF16) for i in range(ND)]
    bw_sb = [p.sb(f"bw{i}", [128, H], BF16) for i in range(ND)]
    cw_sb = {nm: p.sb(f"cw{nm}", [128, ND * 4], F32) for nm in "qkv"}
    xf32 = p.sb("xf32", [128, D], F32)
    xbf = p.sb("xbf", [128, D], BF16)
    pf32 = p.sb("pf32", [128, T], F32)
    pcv = p.sb("pcv", [128, T], F32)
    pbf = p.sb("pbf", [128, T], BF16)
    ps = [p.pt(f"ps{i}", [128, 512], F32) for i in range(2)]
    psb = p.pt("psb", [128, H], F32)
    btile = p.sb("btile", [128, H], F32)
    nb = 8
    tmi = [p.sb(f"tmi{i}", [128, D], BF16) for i in range(nb)]
    tmo = [p.sb(f"tmo{i}", [128, D], BF16) for i in range(nb)]
    sqs = p.sb("sqs", [128, 256], F32)
    ss = p.sb("ss", [128, nb * H], F32)
    rss = p.sb("rss", [128, nb * H], F32)
    eps = p.sb("eps", [128, 1], F32)

    def ld_cw(g):
        out = []
        for nm in "qkv":
            out.append(g.dma_start(out=cw_sb[nm][:, :], in_=cw[nm][:, :]))
        for i in range(ND):
            out.append(g.dma_start(out=bw_sb[i][:, :], in_=bwT[128 * i:128 * (i + 1), :]))
        return out
    p.dma_step("gpsimd", ld_cw)
    p.step("vector", lambda v: v.memset(eps[:, :], 1e-6))

    for i in range(NT):
        p.dma_step("sync", lambda e, i=i: [e.dma_start(out=xf32[:, :], in_=x[128 * i:128 * (i + 1), :])])
        p.step("vector", lambda v: v.tensor_copy(xbf[:, :], xf32[:, :]))
        p.dma_step("scalar", lambda e, i=i: [e.dma_start(out=x_bf[128 * i:128 * (i + 1), :], in_=xbf[:, :])])

    p.dma_step("sync", lambda e: [
        e.dma_start(out=xT[i][:, :], in_=x_bf[:, 128 * i:128 * (i + 1)], transpose=True)
        for i in range(ND)])

    btile2 = p.sb("btile2", [128, H], F32)
    s_mx = p.sem("s_mx").h
    s_sg = p.sem("s_sg").h
    s_sg2 = p.sem("s_sg2").h
    s_st = p.sem("s_st").h
    start0 = p.cnt

    def beta_t(t):
        t.wait_ge(p.bar, start0)
        for tt in range(NT):
            if tt >= 1:
                t.wait_ge(s_sg, tt)
            for k in range(ND):
                last = t.matmul(psb[:, :], xT[k][:, 128 * tt:128 * (tt + 1)],
                                bw_sb[k][:, :], start=(k == 0), stop=(k == ND - 1))
            last.then_inc(s_mx, 1)

    def beta_sc(se):
        for tt in range(NT):
            se.wait_ge(s_mx, tt + 1)
            if tt >= 1:
                se.wait_ge(s_sg2, tt)
            se.activation(out=btile[:, :], in_=psb[:, :], func=AF.Sigmoid).then_inc(s_sg, 1)

    def beta_v(v):
        for tt in range(NT):
            v.wait_ge(s_sg, tt + 1)
            if tt >= 1:
                v.wait_ge(s_st, 16 * tt)
            v.tensor_scalar(btile2[:, :], btile[:, :], 1e-6, None, ALU.max).then_inc(s_sg2, 1)

    def beta_g(g):
        for tt in range(NT):
            g.wait_ge(s_sg2, tt + 1)
            g.dma_start(out=beta_tm[128 * tt:128 * (tt + 1), :], in_=btile2[:, :]).then_inc(s_st, 16)
        g.wait_ge(s_st, 16 * NT)
        g.nop().then_inc(p.bar, 1)
    p.block.tensor(beta_t)
    p.block.scalar(beta_sc)
    p.block.vector(beta_v)
    p.block.gpsimd(beta_g)
    p.cnt += 1

    for nm, stash in (("q", q_raw), ("k", k_raw), ("v", v_fm)):
        p.dma_step("gpsimd", lambda g, nm=nm: [
            g.dma_start(out=w_sb[i][:, :], in_=wT[nm][128 * i:128 * (i + 1), :])
            for i in range(ND)])
        for ot in range(ND):
            def mms(t, tb, bank, ot=ot):
                for k in range(ND):
                    last = t.matmul(ps[bank][:, :], w_sb[k][:, 128 * ot:128 * (ot + 1)],
                                    xT[k][:, 512 * tb:512 * (tb + 1)],
                                    start=(k == 0), stop=(k == ND - 1))
                return last

            def ev(v, tb, bank, ot=ot):
                return v.tensor_copy(pf32[:, 512 * tb:512 * (tb + 1)], ps[bank][:, :])
            p.mm_pipe(list(range(8)), mms, ev)
            conv_taps_fm(p, [pf32], [pcv],
                         lambda o, j, nm=nm, ot=ot: cw_sb[nm][:, 4 * ot + j:4 * ot + j + 1], 4)
            p.step("scalar", lambda se: se.activation(out=pbf[:, :], in_=pcv[:, :], func=AF.Silu))
            p.dma_step("scalar", lambda e, ot=ot, stash=stash: [
                e.dma_start(out=stash[128 * ot:128 * (ot + 1), :], in_=pbf[:, :])])

    for src, dst, norm in ((q_raw, qn_tm, True), (k_raw, kn_tm, True), (v_fm, v_tm, False)):
        for b0 in range(0, NT, nb):
            p.dma_step("sync", lambda e, src=src, b0=b0: [
                e.dma_start(out=tmi[i][:, :], in_=src[:, 128 * (b0 + i):128 * (b0 + i + 1)],
                            transpose=True)
                for i in range(nb)])
            if norm:
                def sq_acc(se):
                    last = None
                    for i in range(nb):
                        for h in range(H):
                            last = se.activation(out=sqs[:, :], in_=tmi[i][:, 256 * h:256 * (h + 1)],
                                                 func=AF.Square,
                                                 accum_out=ss[:, i * H + h:i * H + h + 1])
                    return last
                p.step("scalar", sq_acc)
                p.step("scalar", lambda se: se.activation(out=rss[:, :], in_=ss[:, :],
                                                          func=AF.Sqrt, bias=eps[:, :]))
                p.step("vector", lambda v: v.reciprocal(ss[:, :], rss[:, :]))
                def scale(v):
                    last = None
                    for i in range(nb):
                        for h in range(H):
                            last = v.tensor_scalar(tmo[i][:, 256 * h:256 * (h + 1)],
                                                   tmi[i][:, 256 * h:256 * (h + 1)],
                                                   ss[:, i * H + h:i * H + h + 1], None, ALU.mult)
                    return last
                p.step("vector", scale)
                out_t = tmo
            else:
                out_t = tmi
            p.dma_step("scalar", lambda e, dst=dst, b0=b0, out_t=out_t: [
                e.dma_start(out=dst[128 * (b0 + i):128 * (b0 + i + 1), :], in_=out_t[i][:, :])
                for i in range(nb)])

    def fin(g):
        return g.nop()
    p.step("gpsimd", fin)


def build_qkvb():
    nc = bass.Bass("TRN2", target_bir_lowering=False)
    dr = {}
    dr['x'] = nc.dram_tensor("x", [T, D], F32, kind="ExternalInput").ap()
    dr['wT'] = {nm: nc.dram_tensor(f"w{nm}T", [D, D], F32, kind="ExternalInput").ap()
          for nm in "qkv"}
    dr['bwT'] = nc.dram_tensor("bwT", [D, H], F32, kind="ExternalInput").ap()
    dr['cw'] = {nm: nc.dram_tensor(f"{nm}cw", [128, ND * 4], F32, kind="ExternalInput").ap()
          for nm in "qkv"}
    dr['qn_tm'] = nc.dram_tensor("qn_tm", [T, D], BF16, kind="ExternalOutput").ap()
    dr['kn_tm'] = nc.dram_tensor("kn_tm", [T, D], BF16, kind="ExternalOutput").ap()
    dr['v_tm'] = nc.dram_tensor("v_tm", [T, D], BF16, kind="ExternalOutput").ap()
    dr['v_fm'] = nc.dram_tensor("v_fm", [D, T], BF16, kind="ExternalOutput").ap()
    dr['beta_tm'] = nc.dram_tensor("beta_tm", [T, H], F32, kind="ExternalOutput").ap()
    dr['x_bf'] = nc.dram_tensor("x_bf", [T, D], BF16, kind="ExternalOutput").ap()
    with nc.Block() as block:
        p = P(nc, block)
        p.push_phase("qkvb")
        emit_qkvb(nc, p, dr)
        p.pop_phase()
    return nc


C2 = 64
NC2 = T // C2


def emit_delta(nc, p, dr):
    qn_tm = dr["qn_tm"]
    kn_tm = dr["kn_tm"]
    v_tm = dr["v_tm"]
    beta_tm = dr["beta_tm"]
    eye64 = dr["eye64"]
    masks = dr["masks"]
    delta_tm = dr["delta_tm"]
    dbg = dr.get("dbg")
    kb_st = nc.dram_tensor("kb_st", [T, DK], BF16)
    vb_st = nc.dram_tensor("vb_st", [T, DK], BF16)
    qT = [p.sb(f"qT{i}", [128, T], BF16) for i in range(2)]
    kT = [p.sb(f"kT{i}", [128, T], BF16) for i in range(2)]
    kbT = [p.sb(f"kbT{i}", [128, T], BF16) for i in range(2)]
    wT = kbT
    k_cm = p.sb("k_cm", [C2, NC2, DK], BF16)
    kb_cm = p.sb("kb_cm", [C2, NC2, DK], BF16)
    vb_cm = p.sb("vb_cm", [C2, NC2, DK], BF16)
    u_cm = vb_cm
    ktmp = p.sb("ktmp", [128, 8, DK], BF16)
    beta_all = p.sb("beta_all", [128, NT, H], F32)
    kbtmp = p.sb("kbtmp", [128, 8, DK], BF16)
    N_all = p.sb("N_all", [C2, NC2, C2], BF16)
    M_all = p.sb("M_all", [C2, NC2, C2], BF16)
    NM = p.sb("NM", [C2, NC2, 2 * C2], BF16)
    P_c = p.sb("P_c", [C2, NC2, C2], BF16)
    attnT = p.sb("attnT", [C2, NC2, C2], BF16)
    eye_sb = p.sb("eye_sb", [C2, C2], BF16)
    msk_sb = p.sb("msk_sb", [C2, 3 * C2], F32)
    S_sb = p.sb("S_sb", [128, 2, DK], BF16)
    u_adj = p.sb("u_adj", [C2, DK], BF16)
    d_ring = [p.sb(f"d_ring{i}", [C2, DK], BF16) for i in range(4)]
    pp = [p.pt(f"pp{i}", [C2, 256], F32) for i in range(2)]
    pw = [p.pt(f"pw{i}", [128, 2 * C2], F32) for i in range(2)]
    ps_q = p.pt("ps_q", [C2, 256], F32)
    ps_w = p.pt("ps_w", [C2, 256], F32)
    ps_S = p.pt("ps_S", [128, 512], F32)

    p.dma_step("gpsimd", lambda g: [
        g.dma_start(out=eye_sb[:, :], in_=eye64[:, :]),
        g.dma_start(out=msk_sb[:, :], in_=masks[:, :]),
        g.dma_start(out=beta_all[:, :, :],
                    in_=beta_tm.rearrange("(n p) c -> p n c", p=128))])

    def ch(n, w=C2):
        return slice(C2 * n, C2 * n + w)

    m_su = msk_sb[:, 0:C2]
    m_sl = msk_sb[:, C2:2 * C2]
    m_iu = msk_sb[:, 2 * C2:3 * C2]

    for h in range(H):
        seg = slice(256 * h, 256 * (h + 1))
        for src, dst in ((kn_tm, kb_st), (v_tm, vb_st)):
            for r in range(4):
                rows = slice(1024 * r, 1024 * (r + 1))
                p.dma_step("sync", lambda e, src=src, rows=rows: [
                    e.dma_start(out=ktmp[:, :, :],
                                in_=src[rows, seg].rearrange("(n p) c -> p n c", p=128))])

                def bmul(v, r=r):
                    last = None
                    for n in range(8):
                        last = v.tensor_scalar(kbtmp[:, n, :], ktmp[:, n, :],
                                               beta_all[:, 8 * r + n, h:h + 1].opt(),
                                               None, ALU.mult)
                    return last
                p.step("vector", bmul)
                p.dma_step("scalar", lambda e, dst=dst, rows=rows: [
                    e.dma_start(out=dst[rows, :].rearrange("(n p) c -> p n c", p=128),
                                in_=kbtmp[:, :, :])])
        p.dma_step("sync", lambda e: [
            e.dma_start(out=qT[i][:, :],
                        in_=qn_tm[:, 256 * h + 128 * i:256 * h + 128 * (i + 1)],
                        transpose=True) for i in range(2)] + [
            e.dma_start(out=kT[i][:, :],
                        in_=kn_tm[:, 256 * h + 128 * i:256 * h + 128 * (i + 1)],
                        transpose=True) for i in range(2)] + [
            e.dma_start(out=kbT[i][:, :], in_=kb_st[:, 128 * i:128 * (i + 1)],
                        transpose=True) for i in range(2)] + [
            e.dma_start(out=k_cm[:, :, :],
                        in_=kn_tm[:, seg].rearrange("(n p) c -> p n c", p=C2)),
            e.dma_start(out=kb_cm[:, :, :],
                        in_=kb_st.rearrange("(n p) c -> p n c", p=C2)),
            e.dma_start(out=vb_cm[:, :, :],
                        in_=vb_st.rearrange("(n p) c -> p n c", p=C2))])

        CH = list(range(NC2))

        def s13_mm(t, n, b):
            t.matmul(pp[b][:, 0:C2], kT[0][:, ch(n)], qT[0][:, ch(n)], start=True, stop=False)
            return t.matmul(pp[b][:, 0:C2], kT[1][:, ch(n)], qT[1][:, ch(n)],
                            start=False, stop=True)
        p.mm_pipe(CH, s13_mm,
                  lambda v, n, b: v.tensor_tensor(attnT[:, n, :], pp[b][:, 0:C2], m_iu, ALU.mult))

        def s1_mm(t, n, b):
            t.matmul(pp[b][:, 0:C2], kT[0][:, ch(n)], kbT[0][:, ch(n)], start=True, stop=False)
            t.matmul(pp[b][:, 0:C2], kT[1][:, ch(n)], kbT[1][:, ch(n)], start=False, stop=True)
            t.matmul(pp[b][:, C2:2 * C2], kbT[0][:, ch(n)], kT[0][:, ch(n)], start=True, stop=False)
            return t.matmul(pp[b][:, C2:2 * C2], kbT[1][:, ch(n)], kT[1][:, ch(n)],
                            start=False, stop=True)

        def s1_ev(v, n, b):
            v.tensor_tensor(N_all[:, n, :], pp[b][:, 0:C2], m_su, ALU.mult)
            return v.tensor_tensor(M_all[:, n, :], pp[b][:, C2:2 * C2], m_sl, ALU.mult)
        p.mm_pipe(CH, s1_mm, s1_ev)

        def s6(v):
            return v.tensor_tensor(P_c[:, :, :],
                                   eye_sb[:, :].unsqueeze(1).broadcast_to([C2, NC2, C2]),
                                   N_all[:, :, :], ALU.subtract)
        p.step("vector", s6)

        def sq_stage(nsrc, msrc):
            def mm(t, n, b):
                t.matmul(pp[b][:, 0:C2], msrc(n), nsrc(n), start=True, stop=True)
                return t.matmul(pp[b][:, C2:2 * C2], nsrc(n), msrc(n), start=True, stop=True)

            def ev(v, n, b):
                return v.tensor_copy(NM[:, n, :], pp[b][:, 0:2 * C2])
            p.mm_pipe(CH, mm, ev)

        def prod_stage(msrc):
            def mm(t, n, b):
                t.matmul(pp[b][:, 0:C2], eye_sb[:, :], P_c[:, n, :], start=True, stop=False)
                return t.matmul(pp[b][:, 0:C2], msrc(n), P_c[:, n, :], start=False, stop=True)

            def ev(v, n, b):
                return v.tensor_copy(P_c[:, n, :], pp[b][:, 0:C2])
            p.mm_pipe(CH, mm, ev)

        sq_stage(lambda n: N_all[:, n, :], lambda n: M_all[:, n, :])
        prod_stage(lambda n: NM[:, n, C2:2 * C2])
        sq_stage(lambda n: NM[:, n, 0:C2], lambda n: NM[:, n, C2:2 * C2])
        prod_stage(lambda n: NM[:, n, C2:2 * C2])
        sq_stage(lambda n: NM[:, n, 0:C2], lambda n: NM[:, n, C2:2 * C2])
        prod_stage(lambda n: NM[:, n, C2:2 * C2])
        sq_stage(lambda n: NM[:, n, 0:C2], lambda n: NM[:, n, C2:2 * C2])
        prod_stage(lambda n: NM[:, n, C2:2 * C2])

        def s5_mm(t, n, b):
            return t.matmul(pp[b][:, 0:C2], NM[:, n, 0:C2], NM[:, n, C2:2 * C2],
                            start=True, stop=True)
        p.mm_pipe(CH, s5_mm, lambda v, n, b: v.tensor_copy(M_all[:, n, :], pp[b][:, 0:C2]))
        prod_stage(lambda n: M_all[:, n, :])

        def s11_mm(t, n, b):
            return t.matmul(pp[b][:, 0:256], P_c[:, n, :], vb_cm[:, n, :],
                            start=True, stop=True)
        p.mm_pipe(CH, s11_mm, lambda v, n, b: v.tensor_copy(u_cm[:, n, :], pp[b][:, 0:256]))

        def s12_mm(t, n, b):
            t.matmul(pw[b][:, 0:C2], kb_cm[:, n, 0:128], P_c[:, n, :], start=True, stop=True)
            return t.matmul(pw[b][:, C2:2 * C2], kb_cm[:, n, 128:256], P_c[:, n, :],
                            start=True, stop=True)

        def s12_ev(v, n, b):
            v.tensor_copy(wT[0][:, ch(n)], pw[b][:, 0:C2])
            return v.tensor_copy(wT[1][:, ch(n)], pw[b][:, C2:2 * C2])
        p.mm_pipe(CH, s12_mm, s12_ev)

        if dbg is not None and h == 0:
            p.dma_step("gpsimd", lambda g: [
                g.dma_start(out=dbg["TT"][:, :, :], in_=P_c[:, :, :]),
                g.dma_start(out=dbg["u"][:, :, :], in_=u_cm[:, :, :]),
                g.dma_start(out=dbg["attnT"][:, :, :], in_=attnT[:, :, :]),
                g.dma_start(out=dbg["N"][:, :, :], in_=N_all[:, :, :]),
                g.dma_start(out=dbg["wT0"][:, :], in_=wT[0][:, :]),
                g.dma_start(out=dbg["wT1"][:, :], in_=wT[1][:, :]),
                g.dma_start(out=dbg["kcm"][:, :, :], in_=k_cm[:, :, :]),
                g.dma_start(out=dbg["kbcm"][:, :, :], in_=kb_cm[:, :, :])])

        p.step("vector", lambda v: v.memset(S_sb[:, :, :], 0.0))
        S_pw, S_pq, S_pS = p.sem("s_pw"), p.sem("s_pq"), p.sem("s_pS")
        S_ua, S_do, S_Sc, S_dd = p.sem("s_ua"), p.sem("s_do"), p.sem("s_Sc"), p.sem("s_dd")
        pw0, pq0, pS0 = S_pw.n, S_pq.n, S_pS.n
        ua0, do0, Sc0, dd0 = S_ua.n, S_do.n, S_Sc.n, S_dd.n
        s_pw, s_pq, s_pS = S_pw.h, S_pq.h, S_pS.h
        s_ua, s_do, s_Sc, s_dd = S_ua.h, S_do.h, S_Sc.h, S_dd.h
        start2 = p.cnt

        def scan_pe(t):
            t.wait_ge(p.bar, start2)
            for n in range(NC2):
                t.wait_ge(s_Sc, Sc0 + n)
                t.wait_ge(s_ua, ua0 + n)
                t.matmul(ps_w[:, :], wT[0][:, ch(n)], S_sb[:, 0, :],
                         start=True, stop=False, skip_group_check=True)
                t.matmul(ps_w[:, :], wT[1][:, ch(n)], S_sb[:, 1, :],
                         start=False, stop=True, skip_group_check=True).then_inc(s_pw, 1)
                t.wait_ge(s_do, do0 + n)
                t.matmul(ps_q[:, :], qT[0][:, ch(n)], S_sb[:, 0, :],
                         start=True, stop=False, skip_group_check=True)
                t.matmul(ps_q[:, :], qT[1][:, ch(n)], S_sb[:, 1, :],
                         start=False, stop=False, skip_group_check=True)
                t.wait_ge(s_ua, ua0 + n + 1)
                t.matmul(ps_q[:, :], attnT[:, n, :], u_adj[:, :],
                         start=False, stop=True, skip_group_check=True).then_inc(s_pq, 1)
                t.matmul(ps_S[:, 0:256], k_cm[:, n, 0:128], u_adj[:, :],
                         start=(n == 0), stop=False, skip_group_check=True)
                t.matmul(ps_S[:, 256:512], k_cm[:, n, 128:256], u_adj[:, :],
                         start=False, stop=(n == NC2 - 1),
                         skip_group_check=True).then_inc(s_pS, 1)
            t.nop().then_inc(p.bar, 1)

        def scan_dve(v):
            for n in range(NC2):
                v.wait_ge(s_pw, pw0 + n + 1)
                v.tensor_tensor(u_adj[:, :], u_cm[:, n, :], ps_w[:, :],
                                ALU.subtract).then_inc(s_ua, 1)
                v.wait_ge(s_pq, pq0 + n + 1)
                if n >= 4:
                    v.wait_ge(s_dd, dd0 + 16 * (n - 3))
                v.tensor_copy(d_ring[n % 4][:, :], ps_q[:, :]).then_inc(s_do, 1)
            v.nop().then_inc(p.bar, 1)

        def scan_act(se):
            for n in range(NC2):
                se.wait_ge(s_pS, pS0 + n + 1)
                se.activation(out=S_sb[:, :, :], in_=ps_S[:, :], func=AF.Copy).then_inc(s_Sc, 1)
            se.nop().then_inc(p.bar, 1)

        def scan_g(g):
            for n in range(NC2):
                g.wait_ge(s_do, do0 + n + 1)
                g.dma_start(out=delta_tm[C2 * n:C2 * (n + 1), seg],
                            in_=d_ring[n % 4][:, :]).then_inc(s_dd, 16)
            g.wait_ge(s_dd, dd0 + 16 * NC2)
            g.nop().then_inc(p.bar, 1)
        S_pw.n += NC2; S_pq.n += NC2; S_pS.n += NC2
        S_ua.n += NC2; S_do.n += NC2; S_Sc.n += NC2; S_dd.n += 16 * NC2
        p.block.tensor(scan_pe)
        p.block.vector(scan_dve)
        p.block.scalar(scan_act)
        p.block.gpsimd(scan_g)
        p.cnt += 4

    p.step("gpsimd", lambda g: g.nop())


def build_delta(debug=False):
    nc = bass.Bass("TRN2", target_bir_lowering=False)
    dr = {}
    dr['qn_tm'] = nc.dram_tensor("qn_tm", [T, D], BF16, kind="ExternalInput").ap()
    dr['kn_tm'] = nc.dram_tensor("kn_tm", [T, D], BF16, kind="ExternalInput").ap()
    dr['v_tm'] = nc.dram_tensor("v_tm", [T, D], BF16, kind="ExternalInput").ap()
    dr['beta_tm'] = nc.dram_tensor("beta_tm", [T, H], F32, kind="ExternalInput").ap()
    dr['eye64'] = nc.dram_tensor("eye64", [C2, C2], BF16, kind="ExternalInput").ap()
    dr['masks'] = nc.dram_tensor("masks", [C2, 3 * C2], F32, kind="ExternalInput").ap()
    dr['delta_tm'] = nc.dram_tensor("delta_tm", [T, D], BF16, kind="ExternalOutput").ap()
    if debug:
        dr['dbg'] = {nm: nc.dram_tensor(f"dbg_{nm}", shp, BF16, kind="ExternalOutput").ap()
                     for nm, shp in [("TT", [C2, NC2, C2]), ("u", [C2, NC2, DK]),
                                     ("attnT", [C2, NC2, C2]), ("N", [C2, NC2, C2]),
                                     ("wT0", [128, T]), ("wT1", [128, T]),
                                     ("kcm", [C2, NC2, DK]), ("kbcm", [C2, NC2, DK])]}
    with nc.Block() as block:
        p = P(nc, block)
        p.push_phase("delta")
        emit_delta(nc, p, dr)
        p.pop_phase()
    return nc


def host_l2_inputs(l1_out):
    mask_su = np.triu(np.ones((C2, C2), np.float32), 1)
    mask_sl = mask_su.T.copy()
    mask_iu = np.triu(np.ones((C2, C2), np.float32), 0)
    return {
        "qn_tm": l1_out["qn_tm"], "kn_tm": l1_out["kn_tm"], "v_tm": l1_out["v_tm"],
        "beta_tm": l1_out["beta_tm"],
        "eye64": np.eye(C2, dtype=BF),
        "masks": np.ascontiguousarray(np.concatenate([mask_su, mask_sl, mask_iu], 1)),
    }


def emit_fir(nc, p, dr):
    v_fm = dr["v_fm"]
    fsw = dr["fsw"]
    flw = dr["flw"]
    ls_fm = dr["ls_fm"]
    ll_fm = dr["ll_fm"]
    vt = [p.sb(f"vt{i}", [128, T], BF16) for i in range(ND)]
    dst = [p.sb(f"dst{i}", [128, T], F32) for i in range(ND)]
    obf = p.sb("obf", [128, T], BF16)
    fsw_sb = p.sb("fsw_sb", [128, ND * 5], F32)
    flw_sb = p.sb("flw_sb", [128, ND * 64], F32)

    p.dma_step("gpsimd", lambda g: [
        g.dma_start(out=fsw_sb[:, :], in_=fsw[:, :]),
        g.dma_start(out=flw_sb[:, :], in_=flw[:, :])])
    p.dma_step("sync", lambda e: [
        e.dma_start(out=vt[i][:, :], in_=v_fm[128 * i:128 * (i + 1), :])
        for i in range(ND)])
    for wsb, ntaps, out in ((fsw_sb, 5, ls_fm), (flw_sb, 64, ll_fm)):
        conv_taps_fm(p, vt, dst,
                     lambda ot, j, wsb=wsb, ntaps=ntaps: wsb[:, ntaps * ot + j:ntaps * ot + j + 1],
                     ntaps)
        for i in range(ND):
            p.step("vector", lambda v, i=i: v.tensor_copy(obf[:, :], dst[i][:, :]))
            p.dma_step("scalar", lambda e, i=i, out=out: [
                e.dma_start(out=out[128 * i:128 * (i + 1), :], in_=obf[:, :])])
    p.step("gpsimd", lambda g: g.nop())


def build_fir():
    nc = bass.Bass("TRN2", target_bir_lowering=False)
    dr = {}
    dr['v_fm'] = nc.dram_tensor("v_fm", [D, T], BF16, kind="ExternalInput").ap()
    dr['fsw'] = nc.dram_tensor("fsw", [128, ND * 5], F32, kind="ExternalInput").ap()
    dr['flw'] = nc.dram_tensor("flw", [128, ND * 64], F32, kind="ExternalInput").ap()
    dr['ls_fm'] = nc.dram_tensor("ls_fm", [D, T], BF16, kind="ExternalOutput").ap()
    dr['ll_fm'] = nc.dram_tensor("ll_fm", [D, T], BF16, kind="ExternalOutput").ap()
    with nc.Block() as block:
        p = P(nc, block)
        p.push_phase("fir")
        emit_fir(nc, p, dr)
        p.pop_phase()
    return nc


def host_l3_inputs(inputs, l1_out):
    def lay(w, k):
        return np.ascontiguousarray(
            w.reshape(1024, k).reshape(ND, 128, k).transpose(1, 0, 2).reshape(128, ND * k))
    return {"v_fm": l1_out["v_fm"],
            "fsw": lay(inputs["L0_attn_fir_short_w"], 5),
            "flw": lay(inputs["L0_attn_fir_long_w"], 64)}


def emit_gate(nc, p, dr):
    x_bf = dr["x_bf"]
    v_tm = dr["v_tm"]
    delta_tm = dr["delta_tm"]
    ls_fm = dr["ls_fm"]
    ll_fm = dr["ll_fm"]
    mlp1T = dr["mlp1T"]
    mlp1b = dr["mlp1b"]
    mlp2T = dr["mlp2T"]
    mlp2b = dr["mlp2b"]
    ltemp = dr["ltemp"]
    crl = dr["crl"]
    ident = dr["ident"]
    o_tm = dr["o_tm"]
    GB = 512
    xT = [p.sb(f"xT{i}", [128, T], BF16) for i in range(ND)]
    w1 = [p.sb(f"w1_{i}", [128, D], BF16) for i in range(ND)]
    w1s = p.sb("w1s", [16, D], BF16)
    w2 = [p.sb(f"w2_{i}", [128, H], BF16) for i in range(ND)]
    b1 = p.sb("b1", [128, ND], F32)
    b2 = p.sb("b2", [1, H], BF16)
    onesr = p.sb("onesr", [1, 128], BF16)
    id_sb = p.sb("id_sb", [128, 128], F32)
    ltp = p.sb("ltp", [1, 8], F32)
    crl_sb = p.sb("crl_sb", [1, H], F32)
    scrl_bc = p.sb("scrl_bc", [128, H], F32)
    itmp_bc = p.sb("itmp_bc", [128, 1], F32)
    one1 = p.sb("one1", [1, 1], F32)
    onesf = p.sb("onesf", [1, 128], F32)
    nb = 2
    tmb = {nm: [p.sb(f"tm_{nm}{i}", [128, D], BF16) for i in range(nb)]
           for nm in ("ls", "ll", "dl", "vv")}
    acc3 = p.sb("acc3", [128, 3], F32)
    sscr = p.sb("sscr", [128, 256], F32)
    dstat = p.sb("dstat", [128, 16], F32)
    statsT = [p.sb(f"statsT{h}", [16, T], BF16) for h in range(H)]
    pstat = p.pt("pstat", [16, 128], F32)
    hps = [p.pt(f"hps{i}", [128, GB], F32) for i in range(4)]
    z = p.sb("z", [128, GB], F32)
    sq = p.sb("sq", [128, GB], F32)
    v2 = p.sb("v2", [128, GB], F32)
    ug = p.sb("ug", [128, GB], F32)
    h1b = [p.sb(f"h1b{i}", [128, GB], BF16) for i in range(ND)]
    plog = p.pt("plog", [128, H], F32)
    lgt = p.sb("lgt", [128, H], F32)
    p_all = p.sb("p_all", [128, NT * H, 1 * H], F32)
    psc = p.sb("psc", [128, 8], F32)
    otile = p.sb("otile", [128, D], F32)
    obf = p.sb("obf", [128, D], BF16)

    p.dma_step("gpsimd", lambda g: [
        g.dma_start(out=w1[i][:, :], in_=mlp1T[128 * i:128 * (i + 1), :]) for i in range(ND)] + [
        g.dma_start(out=w1s[:, :], in_=mlp1T[1024:1040, :]),
        g.dma_start(out=b1[:, :], in_=mlp1b[:, :]),
        g.dma_start(out=b2[:, :], in_=mlp2b[:, :]),
        g.dma_start(out=ltp[0:1, 0:1], in_=ltemp[:, :]),
        g.dma_start(out=crl_sb[:, :], in_=crl[:, :]),
        g.dma_start(out=id_sb[:, :], in_=ident[:, :])])
    p.dma_step("gpsimd", lambda g: [
        g.dma_start(out=w2[i][:, :], in_=mlp2T[128 * i:128 * (i + 1), :]) for i in range(ND)])
    p.dma_step("sync", lambda e: [
        e.dma_start(out=xT[i][:, :], in_=x_bf[:, 128 * i:128 * (i + 1)], transpose=True)
        for i in range(ND)])
    p.step("vector", lambda v: v.memset(onesr[:, :], 1.0))
    p.step("vector", lambda v: v.memset(onesf[:, :], 1.0))
    p.step("vector", lambda v: v.memset(one1[:, :], 1.0))

    def scl1(se):
        se.activation(out=ltp[0:1, 1:2], in_=ltp[0:1, 0:1], func=AF.Abs)
        return se.activation(out=ltp[0:1, 2:3], in_=ltp[0:1, 1:2], func=AF.Exp, scale=-1.0)
    p.step("scalar", scl1)
    p.step("scalar", lambda se: se.activation(out=ltp[0:1, 3:4], in_=ltp[0:1, 2:3],
                                              func=AF.Ln, bias=one1[0:1, 0:1]))
    p.step("vector", lambda v: v.tensor_scalar(ltp[0:1, 4:5], ltp[0:1, 0:1], 0.0, None, ALU.max))
    p.step("vector", lambda v: v.tensor_add(ltp[0:1, 5:6], ltp[0:1, 3:4], ltp[0:1, 4:5]))
    p.step("vector", lambda v: v.reciprocal(ltp[0:1, 6:7], ltp[0:1, 5:6]))
    p.step("scalar", lambda se: se.activation(out=ltp[0:1, 7:8], in_=ltp[0:1, 6:7], func=AF.Copy))
    p.step("scalar", lambda se: se.activation(out=crl_sb[0:1, :], in_=crl_sb[0:1, :], func=AF.Sigmoid))

    def bc_mm(t):
        t.matmul(itmp_bc[:, :] if False else pstat[0:128, 0:1], onesf[:, :], ltp[0:1, 6:7],
                 start=True, stop=True)
        return t.matmul(pstat[0:128, 1:5] if False else plog[:, 0:4], onesf[:, :],
                        crl_sb[0:1, :], start=True, stop=True)
    def bc_mm2(t):
        t.matmul(hps[0][:, 0:1], onesf[:, :], ltp[0:1, 6:7], start=True, stop=True)
        return t.matmul(plog[:, :], onesf[:, :], crl_sb[0:1, :], start=True, stop=True)
    p.step("tensor", bc_mm2)
    p.step("vector", lambda v: v.tensor_copy(itmp_bc[:, :], hps[0][:, 0:1]))
    p.step("vector", lambda v: v.tensor_copy(scrl_bc[:, :], plog[:, :]))

    eps5 = eps_ap(nc, p, 1e-5, "eps5")

    for b0 in range(0, NT, nb):
        def ld(e, b0=b0):
            out = []
            for i in range(nb):
                tt = b0 + i
                out += [e.dma_start(out=tmb["ls"][i][:, :], in_=ls_fm[:, 128 * tt:128 * (tt + 1)], transpose=True),
                        e.dma_start(out=tmb["ll"][i][:, :], in_=ll_fm[:, 128 * tt:128 * (tt + 1)], transpose=True),
                        e.dma_start(out=tmb["dl"][i][:, :], in_=delta_tm[128 * tt:128 * (tt + 1), :]),
                        e.dma_start(out=tmb["vv"][i][:, :], in_=v_tm[128 * tt:128 * (tt + 1), :])]
            return out
        p.dma_step("sync", ld)
        for i in range(nb):
            tt = b0 + i
            for h in range(H):
                def stat_acc(se, i=i, h=h):
                    last = None
                    for ti, nm in enumerate(("ls", "ll", "dl", "vv")):
                        seg = tmb[nm][i][:, 256 * h:256 * (h + 1)]
                        se.activation(out=sscr[:, :], in_=seg, func=AF.Copy,
                                      accum_out=acc3[:, 0:1])
                        se.activation(out=sscr[:, :], in_=seg, func=AF.Square,
                                      accum_out=acc3[:, 1:2])
                        last = se.activation(out=sscr[:, :], in_=seg, func=AF.Abs,
                                             accum_out=acc3[:, 2:3])
                    return last
                def stat_one(p=p, i=i, h=h):
                    for ti, nm in enumerate(("ls", "ll", "dl", "vv")):
                        seg_ap = (nm, ti)

                        def acc_f(se, i=i, h=h, nm=nm):
                            seg = tmb[nm][i][:, 256 * h:256 * (h + 1)]
                            se.activation(out=sscr[:, :], in_=seg, func=AF.Copy,
                                          accum_out=acc3[:, 0:1])
                            se.activation(out=sscr[:, :], in_=seg, func=AF.Square,
                                          accum_out=acc3[:, 1:2])
                            return se.activation(out=sscr[:, :], in_=seg, func=AF.Abs,
                                                 accum_out=acc3[:, 2:3])
                        p.step("scalar", acc_f)

                        def der_f(v, ti=ti):
                            v.tensor_scalar(dstat[:, 4 * ti + 0:4 * ti + 1], acc3[:, 0:1],
                                            1.0 / 256, None, ALU.mult)
                            v.tensor_scalar(dstat[:, 4 * ti + 2:4 * ti + 3], acc3[:, 2:3],
                                            1.0 / 256, None, ALU.mult)
                            return v.tensor_scalar(dstat[:, 4 * ti + 1:4 * ti + 2], acc3[:, 1:2],
                                                   1.0 / 256, None, ALU.mult)
                        p.step("vector", der_f)

                        def der2_f(v, ti=ti):
                            v.scalar_tensor_tensor(dstat[:, 4 * ti + 1:4 * ti + 2],
                                                   dstat[:, 4 * ti + 0:4 * ti + 1],
                                                   dstat[:, 4 * ti + 0:4 * ti + 1],
                                                   dstat[:, 4 * ti + 1:4 * ti + 2],
                                                   ALU.mult, ALU.subtract)
                            return v.nop()
                stat_one()

                def fix_var(v, i=i, h=h):
                    last = None
                    for ti in range(4):
                        v.tensor_mul(acc3[:, 0:1], dstat[:, 4 * ti:4 * ti + 1],
                                     dstat[:, 4 * ti:4 * ti + 1])
                        last = v.tensor_sub(dstat[:, 4 * ti + 1:4 * ti + 2],
                                            dstat[:, 4 * ti + 1:4 * ti + 2], acc3[:, 0:1])
                    return last
                def mk_m2(v, i=i, h=h):
                    last = None
                    for ti in range(4):
                        last = v.tensor_mul(psc[:, ti:ti + 1], dstat[:, 4 * ti:4 * ti + 1],
                                            dstat[:, 4 * ti:4 * ti + 1])
                    return last
                p.step("vector", mk_m2)

                def mk_var(v, i=i, h=h):
                    last = None
                    for ti in range(4):
                        last = v.tensor_sub(dstat[:, 4 * ti + 1:4 * ti + 2],
                                            dstat[:, 4 * ti + 1:4 * ti + 2], psc[:, ti:ti + 1])
                    return last
                p.step("vector", mk_var)

                def mk_l2(se, i=i, h=h):
                    last = None
                    for ti in range(4):
                        last = se.activation(out=dstat[:, 4 * ti + 3:4 * ti + 4],
                                             in_=dstat[:, 4 * ti + 1:4 * ti + 2]
                                             if False else sscr[:, 0:1], func=AF.Copy)
                    return last
                def mk_l2b(v, i=i, h=h):
                    last = None
                    for ti in range(4):
                        last = v.tensor_add(acc3[:, 0:1] if False else psc[:, 4 + (ti % 4):5 + (ti % 4)],
                                            dstat[:, 4 * ti + 1:4 * ti + 2], psc[:, ti:ti + 1])
                    return last
                def mk_l2c(v, i=i, h=h):
                    last = None
                    for ti in range(4):
                        last = v.scalar_tensor_tensor(psc[:, 4 + ti:5 + ti],
                                                      dstat[:, 4 * ti + 1:4 * ti + 2], 256.0,
                                                      psc[:, ti:ti + 1], ALU.bypass, ALU.add)
                    return last
                p.step("vector", mk_l2c)

                def mk_l2d(se, i=i, h=h):
                    last = None
                    for ti in range(4):
                        last = se.activation(out=dstat[:, 4 * ti + 3:4 * ti + 4],
                                             in_=psc[:, 4 + ti:5 + ti], func=AF.Sqrt,
                                             scale=256.0)
                    return last
                p.step("scalar", mk_l2d)

                def tr(t, i=i, h=h):
                    return t.transpose(pstat[:, :], dstat[:, :], id_sb[:, :])
                p.step("tensor", tr)

                def evst(v, tt=tt, h=h):
                    return v.tensor_copy(statsT[h][:, 128 * tt:128 * (tt + 1)], pstat[:, :])
                p.step("vector", evst)

    for h in range(H):
        for tb in range(8):
            t0 = GB * tb

            for half in range(2):
                def mm1(t, h=h, t0=t0, half=half):
                    for oi in range(4):
                        ot = 4 * half + oi
                        for k in range(ND):
                            t.matmul(hps[oi][:, :], w1[k][:, 128 * ot:128 * (ot + 1)],
                                     xT[k][:, t0:t0 + GB], start=(k == 0), stop=False)
                        last = t.matmul(hps[oi][:, :], w1s[:, 128 * ot:128 * (ot + 1)],
                                        statsT[h][:, t0:t0 + GB], start=False, stop=True)
                    return last
                p.step("tensor", mm1)

                for oi in range(4):
                    ot = 4 * half + oi
                    p.step("vector", lambda v, oi=oi, ot=ot: v.tensor_scalar(
                        z[:, :], hps[oi][:, :], b1[:, ot:ot + 1], None, ALU.add))
                    p.step("scalar", lambda se: se.activation(out=sq[:, :], in_=z[:, :], func=AF.Square))
                    p.step("vector", lambda v: v.tensor_scalar(sq[:, :], sq[:, :], 0.044715, 1.0,
                                                               ALU.mult, ALU.add))
                    p.step("vector", lambda v: v.tensor_mul(v2[:, :], z[:, :], sq[:, :]))
                    p.step("scalar", lambda se: se.activation(out=ug[:, :], in_=v2[:, :],
                                                              func=AF.Sigmoid, scale=1.5957691216057308))
                    p.step("vector", lambda v, ot=ot: v.tensor_mul(h1b[ot][:, :], z[:, :], ug[:, :]))

            for g in range(4):
                def mm2(t, h=h, t0=t0, g=g):
                    for k in range(ND):
                        t.matmul(plog[:, :], h1b[k][:, 128 * g:128 * (g + 1)], w2[k][:, :],
                                 start=(k == 0), stop=False)
                    return t.matmul(plog[:, :], onesr[:, :], b2[:, :], start=False, stop=True)
                p.step("tensor", mm2)
                p.step("vector", lambda v: v.tensor_copy(lgt[:, :], plog[:, :]))
                p.step("vector", lambda v: v.tensor_reduce(psc[:, 0:1], lgt[:, :], AX.X, ALU.max))
                p.step("vector", lambda v: v.tensor_scalar(psc[:, 6:7], psc[:, 0:1],
                                                           itmp_bc[:, :], None, ALU.mult))
                p.step("vector", lambda v: v.tensor_scalar(psc[:, 1:2], psc[:, 6:7],
                                                           -1.0, None, ALU.mult))
                p.step("scalar", lambda se: se.activation(out=lgt[:, :], in_=lgt[:, :], func=AF.Exp,
                                                          scale=itmp_bc[:, :], bias=psc[:, 1:2],
                                                          accum_out=psc[:, 2:3]))
                p.step("vector", lambda v: v.reciprocal(psc[:, 3:4], psc[:, 2:3]))
                p.step("vector", lambda v: v.tensor_scalar(lgt[:, :], lgt[:, :], psc[:, 3:4],
                                                           None, ALU.mult))
                p.step("vector", lambda v: v.tensor_scalar(lgt[:, :], lgt[:, :], 0.02, None, ALU.max))
                p.step("vector", lambda v: v.tensor_reduce(psc[:, 4:5], lgt[:, :], AX.X, ALU.add))
                p.step("vector", lambda v: v.reciprocal(psc[:, 5:6], psc[:, 4:5]))
                p.step("vector", lambda v, h=h, t0=t0, g=g: v.tensor_scalar(
                    p_all[:, h * NT + (t0 // 128 + g), :], lgt[:, :], psc[:, 5:6], None, ALU.mult))

    def p0adj(v):
        last = None
        for h in range(H):
            last = v.tensor_scalar(p_all[:, h * NT:(h + 1) * NT, 0:1],
                                   p_all[:, h * NT:(h + 1) * NT, 0:1],
                                   scrl_bc[:, h:h + 1], None, ALU.add)
        return last
    p.step("vector", p0adj)

    for b0 in range(0, NT, nb):
        def ld(e, b0=b0):
            out = []
            for i in range(nb):
                tt = b0 + i
                out += [e.dma_start(out=tmb["ls"][i][:, :], in_=ls_fm[:, 128 * tt:128 * (tt + 1)], transpose=True),
                        e.dma_start(out=tmb["ll"][i][:, :], in_=ll_fm[:, 128 * tt:128 * (tt + 1)], transpose=True),
                        e.dma_start(out=tmb["dl"][i][:, :], in_=delta_tm[128 * tt:128 * (tt + 1), :]),
                        e.dma_start(out=tmb["vv"][i][:, :], in_=v_tm[128 * tt:128 * (tt + 1), :])]
            return out
        p.dma_step("sync", ld)
        for i in range(nb):
            tt = b0 + i

            def comb(v, i=i, tt=tt):
                last = None
                for h in range(H):
                    seg = slice(256 * h, 256 * (h + 1))
                    col = lambda j: p_all[:, h * NT + tt, j:j + 1]
                    v.tensor_scalar(otile[:, seg], tmb["ls"][i][:, seg], col(0), None, ALU.mult)
                    v.scalar_tensor_tensor(otile[:, seg], tmb["ll"][i][:, seg], col(1),
                                           otile[:, seg], ALU.mult, ALU.add)
                    v.scalar_tensor_tensor(otile[:, seg], tmb["dl"][i][:, seg], col(2),
                                           otile[:, seg], ALU.mult, ALU.add)
                    last = v.scalar_tensor_tensor(otile[:, seg], tmb["vv"][i][:, seg], col(3),
                                                  otile[:, seg], ALU.mult, ALU.add)
                return last
            def comb0(v, i=i, tt=tt):
                last = None
                for h in range(H):
                    seg = slice(256 * h, 256 * (h + 1))
                    last = v.tensor_scalar(otile[:, seg], tmb["ls"][i][:, seg],
                                           p_all[:, h * NT + tt, 0:1], None, ALU.mult)
                return last
            p.step("vector", comb0)
            for j, nm in ((1, "ll"), (2, "dl"), (3, "vv")):
                def combj(v, i=i, tt=tt, j=j, nm=nm):
                    last = None
                    for h in range(H):
                        seg = slice(256 * h, 256 * (h + 1))
                        last = v.scalar_tensor_tensor(otile[:, seg], tmb[nm][i][:, seg],
                                                      p_all[:, h * NT + tt, j:j + 1],
                                                      otile[:, seg], ALU.mult, ALU.add)
                    return last
                p.step("vector", combj)

            def rms_acc(se, i=i):
                last = None
                for h in range(H):
                    last = se.activation(out=sscr[:, :], in_=otile[:, 256 * h:256 * (h + 1)],
                                         func=AF.Square, accum_out=psc[:, h:h + 1])
                return last
            p.step("scalar", rms_acc)
            p.step("scalar", lambda se: se.activation(out=psc[:, 4:8], in_=psc[:, 0:4],
                                                      func=AF.Sqrt, scale=1.0 / 256,
                                                      bias=eps5))
            p.step("vector", lambda v: v.reciprocal(psc[:, 0:4], psc[:, 4:8]))

            def rms_scale(v, i=i):
                last = None
                for h in range(H):
                    last = v.tensor_scalar(obf[:, 256 * h:256 * (h + 1)],
                                           otile[:, 256 * h:256 * (h + 1)],
                                           psc[:, h:h + 1], None, ALU.mult)
                return last
            p.step("vector", rms_scale)
            p.dma_step("scalar", lambda e, tt=tt: [
                e.dma_start(out=o_tm[128 * tt:128 * (tt + 1), :], in_=obf[:, :])])
    p.step("gpsimd", lambda g: g.nop())


def build_gate():
    nc = bass.Bass("TRN2", target_bir_lowering=False)
    dr = {}
    dr['x_bf'] = nc.dram_tensor("x_bf", [T, D], BF16, kind="ExternalInput").ap()
    dr['v_tm'] = nc.dram_tensor("v_tm", [T, D], BF16, kind="ExternalInput").ap()
    dr['delta_tm'] = nc.dram_tensor("delta_tm", [T, D], BF16, kind="ExternalInput").ap()
    dr['ls_fm'] = nc.dram_tensor("ls_fm", [D, T], BF16, kind="ExternalInput").ap()
    dr['ll_fm'] = nc.dram_tensor("ll_fm", [D, T], BF16, kind="ExternalInput").ap()
    dr['mlp1T'] = nc.dram_tensor("mlp1T", [1040, D], F32, kind="ExternalInput").ap()
    dr['mlp1b'] = nc.dram_tensor("mlp1b", [128, ND], F32, kind="ExternalInput").ap()
    dr['mlp2T'] = nc.dram_tensor("mlp2T", [D, H], F32, kind="ExternalInput").ap()
    dr['mlp2b'] = nc.dram_tensor("mlp2b", [1, H], F32, kind="ExternalInput").ap()
    dr['ltemp'] = nc.dram_tensor("ltemp", [1, 1], F32, kind="ExternalInput").ap()
    dr['crl'] = nc.dram_tensor("crl", [1, H], F32, kind="ExternalInput").ap()
    dr['ident'] = nc.dram_tensor("ident", [128, 128], F32, kind="ExternalInput").ap()
    dr['o_tm'] = nc.dram_tensor("o_tm", [T, D], BF16, kind="ExternalOutput").ap()
    with nc.Block() as block:
        p = P(nc, block)
        p.push_phase("gate")
        emit_gate(nc, p, dr)
        p.pop_phase()
    return nc


def eps_ap(nc, p, val=1e-5, name="eps5"):
    if not hasattr(p, "_eps") or p._eps is None:
        p._eps = {}
    if name not in p._eps:
        t = p.sb(name, [128, 1], F32)
        p.step("vector", lambda v: v.memset(t[:, :], val))
        p._eps[name] = t
    return p._eps[name][:, :]


def host_l4a_inputs(inputs, l1_out, l2_out, l3_out):
    mlp1 = inputs["L0_attn_mlp1_w"]
    return {
        "x_bf": l1_out["x_bf"], "v_tm": l1_out["v_tm"], "delta_tm": l2_out["delta_tm"],
        "ls_fm": l3_out["ls_fm"], "ll_fm": l3_out["ll_fm"],
        "mlp1T": np.ascontiguousarray(mlp1.T),
        "mlp1b": np.ascontiguousarray(inputs["L0_attn_mlp1_b"].reshape(ND, 128).T),
        "mlp2T": np.ascontiguousarray(inputs["L0_attn_mlp2_w"].T),
        "mlp2b": np.ascontiguousarray(inputs["L0_attn_mlp2_b"].reshape(1, H)),
        "ltemp": np.ascontiguousarray(inputs["L0_attn_logit_temp"].reshape(1, 1)),
        "crl": np.ascontiguousarray(inputs["L0_attn_conv_res_logit"].reshape(1, H)),
        "ident": np.eye(128, dtype=np.float32),
    }


def ln_steps(p, nc, bufs, xa, out_f32, out_bf=None, width=D):
    stats6, mv, srt, rstd = bufs
    eps5 = p._eps["eps5"][:, :]
    nh = width // 512

    def st(v):
        last = None
        for i in range(nh):
            last = v.bn_stats(stats6[:, 6 * i:6 * (i + 1)], xa[:, 512 * i:512 * (i + 1)])
        return last
    p.step("vector", st)
    p.step("vector", lambda v: v.bn_aggr(mv[:, :], stats6[:, 0:6 * nh]))
    p.step("scalar", lambda se: se.activation(out=srt[:, :], in_=mv[:, 1:2],
                                              func=AF.Sqrt, bias=eps5))
    p.step("vector", lambda v: v.reciprocal(rstd[:, :], srt[:, :]))
    p.step("vector", lambda v: v.tensor_scalar(out_f32[:, :], xa[:, :], mv[:, 0:1],
                                               rstd[:, :], ALU.subtract, ALU.mult))
    if out_bf is not None:
        p.step("vector", lambda v: v.tensor_copy(out_bf[:, :], out_f32[:, :]))


def scale_steps(p, nc, amax8_sb, scr, qmax=32767.0):
    s1, onesf, psb = scr["s1"], scr["onesf"], scr["psb"]
    s_bc, inv_bc = scr["s_bc"], scr["inv_bc"]
    p.step("vector", lambda v: v.tensor_reduce(s1[0:1, 0:1], amax8_sb[:, :], AX.X, ALU.max))
    p.step("vector", lambda v: v.tensor_scalar(s1[0:1, 1:2], s1[0:1, 0:1], 1.0 / qmax, 1e-12,
                                               ALU.mult, ALU.max))
    p.step("vector", lambda v: v.reciprocal(s1[0:1, 2:3], s1[0:1, 1:2]))

    def bc(t):
        t.matmul(psb[:, 0:1], onesf[:, :], s1[0:1, 1:2], start=True, stop=True)
        return t.matmul(psb[:, 1:2], onesf[:, :], s1[0:1, 2:3], start=True, stop=True)
    p.step("tensor", bc)
    p.step("vector", lambda v: v.tensor_copy(s_bc[:, :], psb[:, 0:1]))
    p.step("vector", lambda v: v.tensor_copy(inv_bc[:, :], psb[:, 1:2]))


def amax_final_steps(p, nc, slots, width, scr, out_dram):
    a1, id_sb, pts = scr["s1"], scr["id_f32"], scr["pts"]
    p.step("vector", lambda v: v.tensor_reduce(a1[:, 3:4], slots[:, 0:width], AX.X, ALU.max))
    p.step("tensor", lambda t: t.transpose(pts[:, :], a1[:, 3:4], id_sb[:, :]))
    p.step("vector", lambda v: v.tensor_reduce(a1[0:1, 4:5], pts[0:1, :], AX.X, ALU.max))
    p.dma_step("scalar", lambda e: [e.dma_start(out=out_dram[:, :], in_=a1[0:1, 4:5])])


def emit_ow(nc, p, dr):
    x = dr["x"]
    o_tm = dr["o_tm"]
    owT = dr["owT"]
    nw_bc = dr["nw_bc"]
    ident = dr["ident"]
    x1 = dr["x1"]
    x1_bf = dr["x1_bf"]
    amax = dr["amax"]
    oT = [p.sb(f"oT{i}", [128, T], BF16) for i in range(ND)]
    wf = p.sb("wf", [128, D], F32)
    wb = [p.sb(f"wb{i}", [128, D], BF16) for i in range(ND)]
    pss = [p.pt(f"pss{i}", [128, 512], F32) for i in range(2)]
    pts = p.pt("pts", [1, 128], F32)
    xt = p.sb("xt", [128, D], F32)
    xa = p.sb("xa", [128, D], F32)
    x1t = p.sb("x1t", [128, D], F32)
    x1b = p.sb("x1b", [128, D], BF16)
    stats6 = p.sb("stats6", [128, 12], F32)
    mv = p.sb("mv", [128, 2], F32)
    srt = p.sb("srt", [128, 1], F32)
    rstd = p.sb("rstd", [128, 1], F32)
    slots = p.sb("slots", [128, NT], F32)
    s1 = p.sb("s1", [128, 8], F32)
    id_sb = p.sb("id_sb", [128, 128], F32)
    eps_ap(nc, p, 1e-5, "eps5")

    p.dma_step("sync", lambda e: [
        e.dma_start(out=oT[i][:, :], in_=o_tm[:, 128 * i:128 * (i + 1)], transpose=True)
        for i in range(ND)] + [e.dma_start(out=id_sb[:, :], in_=ident[:, :])])
    nw_sb = p.sb("nw_sb", [128, ND], F32)
    p.dma_step("gpsimd", lambda g: [g.dma_start(out=nw_sb[:, :], in_=nw_bc[:, :])])
    for i in range(ND):
        p.dma_step("gpsimd", lambda g, i=i: [
            g.dma_start(out=wf[:, :], in_=owT[128 * i:128 * (i + 1), :])])
        p.step("vector", lambda v, i=i: v.tensor_scalar(wb[i][:, :], wf[:, :],
                                                        nw_sb[:, i:i + 1], None, ALU.mult))

    lnb = (stats6, mv, srt, rstd)
    for tt in range(NT):
        p.dma_step("sync", lambda e, tt=tt: [
            e.dma_start(out=xt[:, :], in_=x[128 * tt:128 * (tt + 1), :])])

        def mm(t, tt=tt):
            for half in range(2):
                for k in range(ND):
                    last = t.matmul(pss[half][:, :], oT[k][:, 128 * tt:128 * (tt + 1)],
                                    wb[k][:, 512 * half:512 * (half + 1)],
                                    start=(k == 0), stop=(k == ND - 1))
            return last
        p.step("tensor", mm)

        def ev(v, tt=tt):
            v.tensor_add(xa[:, 0:512], pss[0][:, :], xt[:, 0:512])
            return v.tensor_add(xa[:, 512:1024], pss[1][:, :], xt[:, 512:1024])
        p.step("vector", ev)
        ln_steps(p, nc, lnb, xa, x1t, x1b)
        p.step("vector", lambda v, tt=tt: v.tensor_reduce(slots[:, tt:tt + 1], x1t[:, :],
                                                          AX.X, ALU.max,
                                                          apply_absolute_value=True))
        p.dma_step("scalar", lambda e, tt=tt: [
            e.dma_start(out=x1[128 * tt:128 * (tt + 1), :], in_=x1t[:, :]),
            e.dma_start(out=x1_bf[128 * tt:128 * (tt + 1), :], in_=x1b[:, :])])
    amax_final_steps(p, nc, slots, NT, {"s1": s1, "id_f32": id_sb, "pts": pts}, amax)
    p.step("gpsimd", lambda g: g.nop())


def build_ow():
    nc = bass.Bass("TRN2", target_bir_lowering=False)
    dr = {}
    dr['x'] = nc.dram_tensor("x", [T, D], F32, kind="ExternalInput").ap()
    dr['o_tm'] = nc.dram_tensor("o_tm", [T, D], BF16, kind="ExternalInput").ap()
    dr['owT'] = nc.dram_tensor("owT", [D, D], F32, kind="ExternalInput").ap()
    dr['nw_bc'] = nc.dram_tensor("nw_bc", [128, ND], F32, kind="ExternalInput").ap()
    dr['ident'] = nc.dram_tensor("ident", [128, 128], F32, kind="ExternalInput").ap()
    dr['x1'] = nc.dram_tensor("x1", [T, D], F32, kind="ExternalOutput").ap()
    dr['x1_bf'] = nc.dram_tensor("x1_bf", [T, D], BF16, kind="ExternalOutput").ap()
    dr['amax'] = nc.dram_tensor("amax", [1, 1], F32, kind="ExternalOutput").ap()
    with nc.Block() as block:
        p = P(nc, block)
        p.push_phase("ow")
        emit_ow(nc, p, dr)
        p.pop_phase()
    return nc


def host_l4b_inputs(inputs, l4a_out, core):
    nw = inputs["L0_attn_norm_w"]
    nw_full = np.tile(nw, D // 256)
    return {
        "x": np.ascontiguousarray(inputs["L0_attn_hidden_states"][core]).astype(np.float32, copy=False),
        "o_tm": l4a_out["o_tm"],
        "owT": np.ascontiguousarray(inputs["L0_attn_o_w"].T),
        "nw_bc": np.ascontiguousarray(nw_full.reshape(ND, 128).T),
        "ident": np.eye(128, dtype=np.float32),
    }


def _dequant_steps(p, nc, w_dram, rows, I, bufs, out_dram):
    wf, wi, wbf, bmax, sw, inv = bufs
    nblk = I // 32
    for r in range(rows // 128):
        p.dma_step("gpsimd", lambda g, r=r: [
            g.dma_start(out=wf[:, 0:I], in_=w_dram[128 * r:128 * (r + 1), :])])
        p.step("vector", lambda v: v.tensor_reduce(
            bmax[:, 0:nblk], wf[:, 0:I].rearrange("p (b k) -> p b k", k=32), AX.X, ALU.max,
            apply_absolute_value=True))
        p.step("vector", lambda v: v.tensor_scalar(sw[:, 0:nblk], bmax[:, 0:nblk],
                                                   1.0 / 7, 1e-12, ALU.mult, ALU.max))
        p.step("vector", lambda v: v.reciprocal(inv[:, 0:nblk], sw[:, 0:nblk]))
        p.step("vector", lambda v: v.tensor_tensor(
            wi[:, 0:I].rearrange("p (b k) -> p b k", k=32),
            wf[:, 0:I].rearrange("p (b k) -> p b k", k=32),
            inv[:, 0:nblk].unsqueeze(2).broadcast_to([128, nblk, 32]), ALU.mult))
        p.step("vector", lambda v: v.tensor_tensor(
            wbf[:, 0:I].rearrange("p (b k) -> p b k", k=32),
            wi[:, 0:I].rearrange("p (b k) -> p b k", k=32),
            sw[:, 0:nblk].unsqueeze(2).broadcast_to([128, nblk, 32]), ALU.mult))
        p.dma_step("scalar", lambda e, r=r: [
            e.dma_start(out=out_dram[128 * r:128 * (r + 1), :], in_=wbf[:, 0:I])])


def emit_ffn_up(nc, p, dr):
    x1_bf = dr["x1_bf"]
    w_up = dr["w_up"]
    w_gate = dr["w_gate"]
    w_down = dr["w_down"]
    amax8 = dr["amax8"]
    ident = dr["ident"]
    up_st = dr["up_st"]
    gate_st = dr["gate_st"]
    wdq_down = dr["wdq_down"]
    amax_up = dr["amax_up"]
    amax_gate = dr["amax_gate"]
    wdq_up = nc.dram_tensor("wdq_up", [DFF, D], BF16)
    wdq_gate = nc.dram_tensor("wdq_gate", [DFF, D], BF16)
    xqT = [p.sb(f"xqT{i}", [128, T], BF16) for i in range(ND)]
    wT = [p.sb(f"wT{i}", [128, DFF], BF16) for i in range(ND)]
    wf = p.sb("wf", [128, DFF], F32)
    wi = p.sb("wi", [128, DFF], I32)
    wbf = p.sb("wbf", [128, DFF], BF16)
    bmax = p.sb("bmax", [128, 96], F32)
    sw = p.sb("sw", [128, 96], F32)
    invw = p.sb("invw", [128, 96], F32)
    xb = p.sb("xb", [128, T], BF16)
    xi = p.sb("xi", [128, T], I32)
    ev_f = p.sb("ev_f", [128, 512], F32)
    ob = p.sb("ob", [128, T], BF16)
    slots = p.sb("slots", [128, 192], F32)
    slots2 = p.sb("slots2", [128, 192], F32)
    s1 = p.sb("s1", [128, 8], F32)
    onesf = p.sb("onesf", [1, 128], F32)
    psb = p.pt("psb", [128, 2], F32)
    pts = p.pt("pts", [1, 128], F32)
    pss = [p.pt(f"pss{i}", [128, 512], F32) for i in range(2)]
    s_bc = p.sb("s_bc", [128, 1], F32)
    inv_bc = p.sb("inv_bc", [128, 1], F32)
    am8 = p.sb("am8", [1, 8], F32)
    id_sb = p.sb("id_sb", [128, 128], F32)

    p.dma_step("gpsimd", lambda g: [
        g.dma_start(out=am8[:, :], in_=amax8[:, :]),
        g.dma_start(out=id_sb[:, :], in_=ident[:, :])])
    p.step("vector", lambda v: v.memset(onesf[:, :], 1.0))
    scr = {"s1": s1, "onesf": onesf, "psb": psb, "s_bc": s_bc, "inv_bc": inv_bc,
           "id_f32": id_sb, "pts": pts}
    scale_steps(p, nc, am8[0:1, 0:dr.get("am8_n", 8)], scr)

    for i in range(ND):
        p.dma_step("sync", lambda e, i=i: [
            e.dma_start(out=xb[:, :], in_=x1_bf[:, 128 * i:128 * (i + 1)], transpose=True)])
        p.step("vector", lambda v: v.tensor_scalar(xi[:, :], xb[:, :], inv_bc[:, :],
                                                   None, ALU.mult))
        p.step("vector", lambda v, i=i: v.tensor_scalar(xqT[i][:, :], xi[:, :], s_bc[:, :],
                                                        None, ALU.mult))

    dq_bufs = (wf, wi, wbf, bmax, sw, invw)
    _dequant_steps(p, nc, w_up, DFF, D, dq_bufs, wdq_up)
    _dequant_steps(p, nc, w_gate, DFF, D, dq_bufs, wdq_gate)
    _dequant_steps(p, nc, w_down, D, DFF, dq_bufs, wdq_down)

    for wdq, st_out, slot, am_out in ((wdq_up, up_st, slots, amax_up),
                                      (wdq_gate, gate_st, slots2, amax_gate)):
        p.dma_step("sync", lambda e, wdq=wdq: [
            e.dma_start(out=wT[i][:, :], in_=wdq[:, 128 * i:128 * (i + 1)], transpose=True)
            for i in range(ND)])
        for ot in range(NF):
            def mms(t, tb, bank, ot=ot):
                for k in range(ND):
                    last = t.matmul(pss[bank][:, :], wT[k][:, 128 * ot:128 * (ot + 1)],
                                    xqT[k][:, 512 * tb:512 * (tb + 1)],
                                    start=(k == 0), stop=(k == ND - 1))
                return last

            def ev(v, tb, bank, ot=ot, slot=slot):
                v.tensor_reduce(slot[:, 8 * ot + tb:8 * ot + tb + 1], pss[bank][:, :],
                                AX.X, ALU.max, apply_absolute_value=True)
                return v.tensor_copy(ob[:, 512 * tb:512 * (tb + 1)], pss[bank][:, :])
            p.mm_pipe(list(range(8)), mms, ev)
            p.dma_step("scalar", lambda e, ot=ot, st_out=st_out: [
                e.dma_start(out=st_out[128 * ot:128 * (ot + 1), :], in_=ob[:, :])])
        amax_final_steps(p, nc, slot, 192, scr, am_out)
    p.step("gpsimd", lambda g: g.nop())


def build_ffn_up():
    nc = bass.Bass("TRN2", target_bir_lowering=False)
    dr = {}
    dr['x1_bf'] = nc.dram_tensor("x1_bf", [T, D], BF16, kind="ExternalInput").ap()
    dr['w_up'] = nc.dram_tensor("w_up", [DFF, D], F32, kind="ExternalInput").ap()
    dr['w_gate'] = nc.dram_tensor("w_gate", [DFF, D], F32, kind="ExternalInput").ap()
    dr['w_down'] = nc.dram_tensor("w_down", [D, DFF], F32, kind="ExternalInput").ap()
    dr['amax8'] = nc.dram_tensor("amax8", [1, 8], F32, kind="ExternalInput").ap()
    dr['ident'] = nc.dram_tensor("ident", [128, 128], F32, kind="ExternalInput").ap()
    dr['up_st'] = nc.dram_tensor("up_st", [DFF, T], BF16, kind="ExternalOutput").ap()
    dr['gate_st'] = nc.dram_tensor("gate_st", [DFF, T], BF16, kind="ExternalOutput").ap()
    dr['wdq_down'] = nc.dram_tensor("wdq_down", [D, DFF], BF16, kind="ExternalOutput").ap()
    dr['amax_up'] = nc.dram_tensor("amax_up", [1, 1], F32, kind="ExternalOutput").ap()
    dr['amax_gate'] = nc.dram_tensor("amax_gate", [1, 1], F32, kind="ExternalOutput").ap()
    with nc.Block() as block:
        p = P(nc, block)
        p.push_phase("ffn_up")
        emit_ffn_up(nc, p, dr)
        p.pop_phase()
    return nc


def _amax8(outs_all, key):
    vals = [np.asarray(outs_all[c][key]).reshape(-1)[0] for c in range(len(outs_all))]
    vals = vals + [np.float32(0.0)] * (8 - len(vals))
    return np.ascontiguousarray(np.asarray(vals, np.float32).reshape(1, 8))


def host_l5_inputs(inputs, l4b_outs_all, core):
    return {
        "x1_bf": l4b_outs_all[core]["x1_bf"],
        "w_up": np.ascontiguousarray(inputs["L0_ffn_w_up"]),
        "w_gate": np.ascontiguousarray(inputs["L0_ffn_w_gate"]),
        "w_down": np.ascontiguousarray(inputs["L0_ffn_w_down"]),
        "amax8": _amax8(l4b_outs_all, "amax"),
        "ident": np.eye(128, dtype=np.float32),
    }


def emit_qchain_a(nc, p, dr):
    up_st = dr["up_st"]
    gate_st = dr["gate_st"]
    am8u = dr["am8u"]
    am8g = dr["am8g"]
    ident = dr["ident"]
    upq_st = dr["upq_st"]
    g_st = dr["g_st"]
    amax_g = dr["amax_g"]
    tb = p.sb("tb", [128, T], BF16)
    ti = p.sb("ti", [128, T], I32)
    gq = p.sb("gq", [128, T], F32)
    sg = p.sb("sg", [128, T], F32)
    ob = p.sb("ob", [128, T], BF16)
    slots = p.sb("slots", [128, NF], F32)
    s1 = p.sb("s1", [128, 8], F32)
    s1g = p.sb("s1g", [128, 8], F32)
    onesf = p.sb("onesf", [1, 128], F32)
    psb = p.pt("psb", [128, 2], F32)
    pts = p.pt("pts", [1, 128], F32)
    su_bc = p.sb("su_bc", [128, 1], F32)
    iu_bc = p.sb("iu_bc", [128, 1], F32)
    sg_bc = p.sb("sg_bc", [128, 1], F32)
    ig_bc = p.sb("ig_bc", [128, 1], F32)
    am8u_sb = p.sb("am8u_sb", [1, 8], F32)
    am8g_sb = p.sb("am8g_sb", [1, 8], F32)
    id_sb = p.sb("id_sb", [128, 128], F32)

    p.dma_step("gpsimd", lambda g: [
        g.dma_start(out=am8u_sb[:, :], in_=am8u[:, :]),
        g.dma_start(out=am8g_sb[:, :], in_=am8g[:, :]),
        g.dma_start(out=id_sb[:, :], in_=ident[:, :])])
    p.step("vector", lambda v: v.memset(onesf[:, :], 1.0))
    scale_steps(p, nc, am8u_sb[0:1, 0:dr.get("am8_n", 8)], {"s1": s1, "onesf": onesf, "psb": psb,
                                 "s_bc": su_bc, "inv_bc": iu_bc})
    scale_steps(p, nc, am8g_sb[0:1, 0:dr.get("am8_n", 8)], {"s1": s1g, "onesf": onesf, "psb": psb,
                                 "s_bc": sg_bc, "inv_bc": ig_bc})
    for r in range(NF):
        p.dma_step("sync", lambda e, r=r: [
            e.dma_start(out=tb[:, :], in_=up_st[128 * r:128 * (r + 1), :])])
        p.step("vector", lambda v: v.tensor_scalar(ti[:, :], tb[:, :], iu_bc[:, :], None, ALU.mult))
        p.step("vector", lambda v: v.tensor_scalar(ob[:, :], ti[:, :], su_bc[:, :], None, ALU.mult))
        p.dma_step("scalar", lambda e, r=r: [
            e.dma_start(out=upq_st[128 * r:128 * (r + 1), :], in_=ob[:, :])])
        p.dma_step("sync", lambda e, r=r: [
            e.dma_start(out=tb[:, :], in_=gate_st[128 * r:128 * (r + 1), :])])
        p.step("vector", lambda v: v.tensor_scalar(ti[:, :], tb[:, :], ig_bc[:, :], None, ALU.mult))
        p.step("vector", lambda v: v.tensor_scalar(gq[:, :], ti[:, :], sg_bc[:, :], None, ALU.mult))
        p.step("scalar", lambda se: se.activation(out=sg[:, :], in_=gq[:, :], func=AF.Sigmoid))
        p.step("vector", lambda v: v.tensor_scalar(ti[:, :], sg[:, :], 65536.0, None, ALU.mult))
        p.step("vector", lambda v: v.tensor_scalar(sg[:, :], ti[:, :], 65535.0, 1.0 / 65536,
                                                   ALU.min, ALU.mult))
        p.step("vector", lambda v: v.tensor_mul(ob[:, :], gq[:, :], sg[:, :]))
        p.step("vector", lambda v, r=r: v.tensor_reduce(slots[:, r:r + 1], ob[:, :],
                                                        AX.X, ALU.max, apply_absolute_value=True))
        p.dma_step("scalar", lambda e, r=r: [
            e.dma_start(out=g_st[128 * r:128 * (r + 1), :], in_=ob[:, :])])
    amax_final_steps(p, nc, slots, NF, {"s1": s1, "id_f32": id_sb, "pts": pts}, amax_g)
    p.step("gpsimd", lambda g: g.nop())


def build_qchain_a():
    nc = bass.Bass("TRN2", target_bir_lowering=False)
    dr = {}
    dr['up_st'] = nc.dram_tensor("up_st", [DFF, T], BF16, kind="ExternalInput").ap()
    dr['gate_st'] = nc.dram_tensor("gate_st", [DFF, T], BF16, kind="ExternalInput").ap()
    dr['am8u'] = nc.dram_tensor("am8u", [1, 8], F32, kind="ExternalInput").ap()
    dr['am8g'] = nc.dram_tensor("am8g", [1, 8], F32, kind="ExternalInput").ap()
    dr['ident'] = nc.dram_tensor("ident", [128, 128], F32, kind="ExternalInput").ap()
    dr['upq_st'] = nc.dram_tensor("upq_st", [DFF, T], BF16, kind="ExternalOutput").ap()
    dr['g_st'] = nc.dram_tensor("g_st", [DFF, T], BF16, kind="ExternalOutput").ap()
    dr['amax_g'] = nc.dram_tensor("amax_g", [1, 1], F32, kind="ExternalOutput").ap()
    with nc.Block() as block:
        p = P(nc, block)
        p.push_phase("qchain_a")
        emit_qchain_a(nc, p, dr)
        p.pop_phase()
    return nc


def emit_qchain_b(nc, p, dr):
    g_st = dr["g_st"]
    upq_st = dr["upq_st"]
    am8 = dr["am8"]
    ident = dr["ident"]
    opre_st = dr["opre_st"]
    amax_o = dr["amax_o"]
    tb = p.sb("tb", [128, T], BF16)
    ub = p.sb("ub", [128, T], BF16)
    ti = p.sb("ti", [128, T], I32)
    g2 = p.sb("g2", [128, T], F32)
    ob = p.sb("ob", [128, T], BF16)
    slots = p.sb("slots", [128, NF], F32)
    s1 = p.sb("s1", [128, 8], F32)
    onesf = p.sb("onesf", [1, 128], F32)
    psb = p.pt("psb", [128, 2], F32)
    pts = p.pt("pts", [1, 128], F32)
    s_bc = p.sb("s_bc", [128, 1], F32)
    inv_bc = p.sb("inv_bc", [128, 1], F32)
    am8_sb = p.sb("am8_sb", [1, 8], F32)
    id_sb = p.sb("id_sb", [128, 128], F32)

    p.dma_step("gpsimd", lambda g: [
        g.dma_start(out=am8_sb[:, :], in_=am8[:, :]),
        g.dma_start(out=id_sb[:, :], in_=ident[:, :])])
    p.step("vector", lambda v: v.memset(onesf[:, :], 1.0))
    scale_steps(p, nc, am8_sb[0:1, 0:dr.get("am8_n", 8)], {"s1": s1, "onesf": onesf, "psb": psb,
                                "s_bc": s_bc, "inv_bc": inv_bc})
    for r in range(NF):
        p.dma_step("sync", lambda e, r=r: [
            e.dma_start(out=tb[:, :], in_=g_st[128 * r:128 * (r + 1), :]),
            e.dma_start(out=ub[:, :], in_=upq_st[128 * r:128 * (r + 1), :])])
        p.step("vector", lambda v: v.tensor_scalar(ti[:, :], tb[:, :], inv_bc[:, :], None, ALU.mult))
        p.step("vector", lambda v: v.tensor_scalar(g2[:, :], ti[:, :], s_bc[:, :], None, ALU.mult))
        p.step("vector", lambda v: v.tensor_mul(ob[:, :], g2[:, :], ub[:, :]))
        p.step("vector", lambda v, r=r: v.tensor_reduce(slots[:, r:r + 1], ob[:, :],
                                                        AX.X, ALU.max, apply_absolute_value=True))
        p.dma_step("scalar", lambda e, r=r: [
            e.dma_start(out=opre_st[128 * r:128 * (r + 1), :], in_=ob[:, :])])
    amax_final_steps(p, nc, slots, NF, {"s1": s1, "id_f32": id_sb, "pts": pts}, amax_o)
    p.step("gpsimd", lambda g: g.nop())


def build_qchain_b():
    nc = bass.Bass("TRN2", target_bir_lowering=False)
    dr = {}
    dr['g_st'] = nc.dram_tensor("g_st", [DFF, T], BF16, kind="ExternalInput").ap()
    dr['upq_st'] = nc.dram_tensor("upq_st", [DFF, T], BF16, kind="ExternalInput").ap()
    dr['am8'] = nc.dram_tensor("am8", [1, 8], F32, kind="ExternalInput").ap()
    dr['ident'] = nc.dram_tensor("ident", [128, 128], F32, kind="ExternalInput").ap()
    dr['opre_st'] = nc.dram_tensor("opre_st", [DFF, T], BF16, kind="ExternalOutput").ap()
    dr['amax_o'] = nc.dram_tensor("amax_o", [1, 1], F32, kind="ExternalOutput").ap()
    with nc.Block() as block:
        p = P(nc, block)
        p.push_phase("qchain_b")
        emit_qchain_b(nc, p, dr)
        p.pop_phase()
    return nc


def emit_qchain_c(nc, p, dr):
    opre_st = dr["opre_st"]
    wdq_down = dr["wdq_down"]
    x1 = dr["x1"]
    am8 = dr["am8"]
    x2 = dr["x2"]
    x2_bf = dr["x2_bf"]
    scaled_bf = dr["scaled_bf"]
    norm_st = dr["norm_st"]
    wdT = [p.sb(f"wdT{i}", [128, D], BF16) for i in range(NF)]
    oslc = p.sb("oslc", [128, NF, 128], BF16)
    osli = p.sb("osli", [128, NF, 128], I32)
    oslq = p.sb("oslq", [128, NF, 128], BF16)
    xt = p.sb("xt", [128, D], F32)
    xa = p.sb("xa", [128, D], F32)
    x2t = p.sb("x2t", [128, D], F32)
    x2b = p.sb("x2b", [128, D], BF16)
    scb = p.sb("scb", [128, D], BF16)
    stats6 = p.sb("stats6", [128, 12], F32)
    mv = p.sb("mv", [128, 2], F32)
    srt = p.sb("srt", [128, 1], F32)
    rstd = p.sb("rstd", [128, 1], F32)
    nrm = p.sb("nrm", [128, NT], F32)
    sqs = p.sb("sqs", [128, 1], F32)
    s1 = p.sb("s1", [128, 8], F32)
    onesf = p.sb("onesf", [1, 128], F32)
    psb = p.pt("psb", [128, 2], F32)
    pss = [p.pt(f"pss{i}", [128, 512], F32) for i in range(2)]
    s_bc = p.sb("s_bc", [128, 1], F32)
    inv_bc = p.sb("inv_bc", [128, 1], F32)
    am8_sb = p.sb("am8_sb", [1, 8], F32)
    sscr = p.sb("sscr", [128, D], F32)
    eps_ap(nc, p, 1e-5, "eps5")

    p.dma_step("gpsimd", lambda g: [g.dma_start(out=am8_sb[:, :], in_=am8[:, :])])
    p.step("vector", lambda v: v.memset(onesf[:, :], 1.0))
    scale_steps(p, nc, am8_sb[0:1, 0:dr.get("am8_n", 8)], {"s1": s1, "onesf": onesf, "psb": psb,
                                "s_bc": s_bc, "inv_bc": inv_bc})
    p.dma_step("sync", lambda e: [
        e.dma_start(out=wdT[i][:, :], in_=wdq_down[:, 128 * i:128 * (i + 1)], transpose=True)
        for i in range(NF)])
    lnb = (stats6, mv, srt, rstd)
    for tt in range(NT):
        p.dma_step("sync", lambda e, tt=tt: [
            e.dma_start(out=oslc[:, :, :],
                        in_=opre_st[:, 128 * tt:128 * (tt + 1)].rearrange("(a p) c -> p a c", p=128)),
            e.dma_start(out=xt[:, :], in_=x1[128 * tt:128 * (tt + 1), :])])
        p.step("vector", lambda v: v.tensor_scalar(osli[:, :, :], oslc[:, :, :],
                                                   inv_bc[:, :], None, ALU.mult))
        p.step("vector", lambda v: v.tensor_scalar(oslq[:, :, :], osli[:, :, :],
                                                   s_bc[:, :], None, ALU.mult))

        def mm(t, tt=tt):
            for half in range(2):
                for k in range(NF):
                    last = t.matmul(pss[half][:, :], oslq[:, k, :],
                                    wdT[k][:, 512 * half:512 * (half + 1)],
                                    start=(k == 0), stop=(k == NF - 1))
            return last
        p.step("tensor", mm)

        def ev(v, tt=tt):
            v.tensor_add(xa[:, 0:512], pss[0][:, :], xt[:, 0:512])
            return v.tensor_add(xa[:, 512:1024], pss[1][:, :], xt[:, 512:1024])
        p.step("vector", ev)
        ln_steps(p, nc, lnb, xa, x2t, x2b)
        p.step("scalar", lambda se: se.activation(out=sscr[:, :], in_=x2t[:, :],
                                                  func=AF.Square, accum_out=sqs[:, :]))
        p.step("scalar", lambda se, tt=tt: se.activation(out=nrm[:, tt:tt + 1], in_=sqs[:, :],
                                                         func=AF.Sqrt))
        p.step("vector", lambda v, tt=tt: v.tensor_scalar(srt[:, :], nrm[:, tt:tt + 1],
                                                          1e-9, None, ALU.add))
        p.step("vector", lambda v: v.reciprocal(rstd[:, :], srt[:, :]))
        p.step("vector", lambda v: v.tensor_scalar(scb[:, :], x2t[:, :], rstd[:, :],
                                                   None, ALU.mult))
        p.dma_step("scalar", lambda e, tt=tt: [
            e.dma_start(out=x2[128 * tt:128 * (tt + 1), :], in_=x2t[:, :]),
            e.dma_start(out=x2_bf[128 * tt:128 * (tt + 1), :], in_=x2b[:, :]),
            e.dma_start(out=scaled_bf[128 * tt:128 * (tt + 1), :], in_=scb[:, :])])
    p.dma_step("scalar", lambda e: [e.dma_start(out=norm_st[:, :], in_=nrm[:, :])])
    p.step("gpsimd", lambda g: g.nop())


def build_qchain_c():
    nc = bass.Bass("TRN2", target_bir_lowering=False)
    dr = {}
    dr['opre_st'] = nc.dram_tensor("opre_st", [DFF, T], BF16, kind="ExternalInput").ap()
    dr['wdq_down'] = nc.dram_tensor("wdq_down", [D, DFF], BF16, kind="ExternalInput").ap()
    dr['x1'] = nc.dram_tensor("x1", [T, D], F32, kind="ExternalInput").ap()
    dr['am8'] = nc.dram_tensor("am8", [1, 8], F32, kind="ExternalInput").ap()
    dr['x2'] = nc.dram_tensor("x2", [T, D], F32, kind="ExternalOutput").ap()
    dr['x2_bf'] = nc.dram_tensor("x2_bf", [T, D], BF16, kind="ExternalOutput").ap()
    dr['scaled_bf'] = nc.dram_tensor("scaled_bf", [T, D], BF16, kind="ExternalOutput").ap()
    dr['norm_st'] = nc.dram_tensor("norm_st", [128, NT], F32, kind="ExternalOutput").ap()
    with nc.Block() as block:
        p = P(nc, block)
        p.push_phase("qchain_c")
        emit_qchain_c(nc, p, dr)
        p.pop_phase()
    return nc


def emit_l1pre(nc, p, dr):
    scaled_bf = dr["scaled_bf"]
    WqT = dr["WqT"]
    WvT = dr["WvT"]
    bq = dr["bq"]
    bv1024 = dr["bv1024"]
    Amat = dr["Amat"]
    qT_st = dr["qT_st"]
    tqT_st = dr["tqT_st"]
    v_tm_st = dr["v_tm_st"]
    scT = [p.sb(f"scT{i}", [128, T], BF16) for i in range(ND)]
    qT = [p.sb(f"qT{i}", [128, T], BF16) for i in range(ND)]
    wb = [p.sb(f"wb{i}", [128, D], BF16) for i in range(ND)]
    bq_sb = p.sb("bq_sb", [128, ND], F32)
    bv_sb = p.sb("bv_sb", [1, D], BF16)
    onesr = p.sb("onesr", [1, 128], BF16)
    pss = [p.pt(f"pss{i}", [128, 512], F32) for i in range(2)]
    ob = p.sb("ob", [128, T], BF16)
    vtb = p.sb("vtb", [128, D], BF16)

    p.dma_step("sync", lambda e: [
        e.dma_start(out=scT[i][:, :], in_=scaled_bf[:, 128 * i:128 * (i + 1)], transpose=True)
        for i in range(ND)])
    p.dma_step("gpsimd", lambda g: [
        g.dma_start(out=bq_sb[:, :], in_=bq[:, :]),
        g.dma_start(out=bv_sb[:, :], in_=bv1024[:, :])])
    p.step("vector", lambda v: v.memset(onesr[:, :], 1.0))

    p.dma_step("gpsimd", lambda g: [
        g.dma_start(out=wb[i][:, :], in_=WqT[128 * i:128 * (i + 1), :]) for i in range(ND)])
    for ot in range(ND):
        def mms(t, tb, bank, ot=ot):
            for k in range(ND):
                last = t.matmul(pss[bank][:, :], wb[k][:, 128 * ot:128 * (ot + 1)],
                                scT[k][:, 512 * tb:512 * (tb + 1)],
                                start=(k == 0), stop=(k == ND - 1))
            return last

        def ev(v, tb, bank, ot=ot):
            return v.tensor_scalar(qT[ot][:, 512 * tb:512 * (tb + 1)], pss[bank][:, :],
                                   bq_sb[:, ot:ot + 1], None, ALU.add)
        p.mm_pipe(list(range(8)), mms, ev)
        p.dma_step("scalar", lambda e, ot=ot: [
            e.dma_start(out=qT_st[128 * ot:128 * (ot + 1), :], in_=qT[ot][:, :])])

    p.dma_step("gpsimd", lambda g: [
        g.dma_start(out=wb[i][:, :], in_=Amat[128 * i:128 * (i + 1), :]) for i in range(ND)])
    for ot in range(ND):
        def mms(t, tb, bank, ot=ot):
            for k in range(ND):
                last = t.matmul(pss[bank][:, :], wb[k][:, 128 * ot:128 * (ot + 1)],
                                qT[k][:, 512 * tb:512 * (tb + 1)],
                                start=(k == 0), stop=(k == ND - 1))
            return last

        def ev(v, tb, bank, ot=ot):
            return v.tensor_copy(ob[:, 512 * tb:512 * (tb + 1)], pss[bank][:, :])
        p.mm_pipe(list(range(8)), mms, ev)
        p.dma_step("scalar", lambda e, ot=ot: [
            e.dma_start(out=tqT_st[128 * ot:128 * (ot + 1), :], in_=ob[:, :])])

    p.dma_step("gpsimd", lambda g: [
        g.dma_start(out=wb[i][:, :], in_=WvT[128 * i:128 * (i + 1), :]) for i in range(ND)])
    for tt in range(NT):
        def mmv(t, tt=tt):
            for half in range(2):
                for k in range(ND):
                    t.matmul(pss[half][:, :], scT[k][:, 128 * tt:128 * (tt + 1)],
                             wb[k][:, 512 * half:512 * (half + 1)],
                             start=(k == 0), stop=False)
                last = t.matmul(pss[half][:, :], onesr[:, :],
                                bv_sb[:, 512 * half:512 * (half + 1)],
                                start=False, stop=True)
            return last
        p.step("tensor", mmv)

        def evv(v, tt=tt):
            v.tensor_copy(vtb[:, 0:512], pss[0][:, :])
            return v.tensor_copy(vtb[:, 512:1024], pss[1][:, :])
        p.step("vector", evv)
        p.dma_step("scalar", lambda e, tt=tt: [
            e.dma_start(out=v_tm_st[128 * tt:128 * (tt + 1), :], in_=vtb[:, :])])
    p.step("gpsimd", lambda g: g.nop())


def build_l1pre():
    nc = bass.Bass("TRN2", target_bir_lowering=False)
    dr = {}
    dr['scaled_bf'] = nc.dram_tensor("scaled_bf", [T, D], BF16, kind="ExternalInput").ap()
    dr['WqT'] = nc.dram_tensor("WqT", [D, D], F32, kind="ExternalInput").ap()
    dr['WvT'] = nc.dram_tensor("WvT", [D, D], F32, kind="ExternalInput").ap()
    dr['bq'] = nc.dram_tensor("bq", [128, ND], F32, kind="ExternalInput").ap()
    dr['bv1024'] = nc.dram_tensor("bv1024", [1, D], F32, kind="ExternalInput").ap()
    dr['Amat'] = nc.dram_tensor("Amat", [D, D], F32, kind="ExternalInput").ap()
    dr['qT_st'] = nc.dram_tensor("qT_st", [D, T], BF16, kind="ExternalOutput").ap()
    dr['tqT_st'] = nc.dram_tensor("tqT_st", [D, T], BF16, kind="ExternalOutput").ap()
    dr['v_tm_st'] = nc.dram_tensor("v_tm_st", [T, D], BF16, kind="ExternalOutput").ap()
    with nc.Block() as block:
        p = P(nc, block)
        p.push_phase("l1pre")
        emit_l1pre(nc, p, dr)
        p.pop_phase()
    return nc


def emit_l1attn(nc, p, dr):
    qT_st = dr["qT_st"]
    tqT_st = dr["tqT_st"]
    v_tm_st = dr["v_tm_st"]
    x2 = dr["x2"]
    norm_st = dr["norm_st"]
    ident_bf = dr["ident_bf"]
    x3 = dr["x3"]
    x3_bf = dr["x3_bf"]
    tqT = [p.sb(f"tqT{i}", [128, T], BF16) for i in range(ND)]
    vtm = [p.sb(f"vtm{i}", [128, D], BF16) for i in range(NT)]
    qsl = p.sb("qsl", [128, ND, 128], BF16)
    sco = p.sb("sco", [128, T], F32)
    pexp = p.sb("pexp", [128, T], BF16)
    pT = p.sb("pT", [128, NT, 128], BF16)
    idb = p.sb("idb", [128, 128], BF16)
    nrm = p.sb("nrm", [128, NT], F32)
    x2t = p.sb("x2t", [128, D], F32)
    xa = p.sb("xa", [128, D], F32)
    x3t = p.sb("x3t", [128, D], F32)
    x3b = p.sb("x3b", [128, D], BF16)
    mrow = p.sb("mrow", [128, 4], F32)
    stats6 = p.sb("stats6", [128, 12], F32)
    mv = p.sb("mv", [128, 2], F32)
    srt = p.sb("srt", [128, 1], F32)
    rstd = p.sb("rstd", [128, 1], F32)
    psc = [p.pt(f"psc{i}", [128, 512], F32) for i in range(2)]
    ptr = p.pt("ptr", [128, 128], BF16)
    pct = [p.pt(f"pct{i}", [128, 512], F32) for i in range(2)]
    eps_ap(nc, p, 1e-5, "eps5")

    p.dma_step("sync", lambda e: [
        e.dma_start(out=tqT[i][:, :], in_=tqT_st[128 * i:128 * (i + 1), :])
        for i in range(ND)] + [
        e.dma_start(out=vtm[i][:, :], in_=v_tm_st[128 * i:128 * (i + 1), :])
        for i in range(NT)] + [
        e.dma_start(out=idb[:, :], in_=ident_bf[:, :]),
        e.dma_start(out=nrm[:, :], in_=norm_st[:, :])])

    for s in range(NT):
        p.dma_step("sync", lambda e, s=s: [
            e.dma_start(out=qsl[:, :, :],
                        in_=qT_st[:, 128 * s:128 * (s + 1)].rearrange("(a p) c -> p a c", p=128)),
            e.dma_start(out=x2t[:, :], in_=x2[128 * s:128 * (s + 1), :])])

        def mmsc(t, b2, bank, s=s):
            for k in range(ND):
                last = t.matmul(psc[bank][:, :], qsl[:, k, :],
                                tqT[k][:, 512 * b2:512 * (b2 + 1)],
                                start=(k == 0), stop=(k == ND - 1))
            return last

        def evsc(v, b2, bank, s=s):
            return v.tensor_copy(sco[:, 512 * b2:512 * (b2 + 1)], psc[bank][:, :])
        p.mm_pipe(list(range(8)), mmsc, evsc)

        p.step("vector", lambda v: v.tensor_reduce(mrow[:, 0:1], sco[:, :], AX.X, ALU.max))
        p.step("vector", lambda v: v.tensor_scalar(mrow[:, 1:2], mrow[:, 0:1],
                                                   -1.0 / 32, None, ALU.mult))
        p.step("scalar", lambda se: se.activation(out=pexp[:, :], in_=sco[:, :], func=AF.Exp,
                                                  scale=1.0 / 32, bias=mrow[:, 1:2],
                                                  accum_out=mrow[:, 2:3]))
        p.step("vector", lambda v: v.reciprocal(mrow[:, 3:4], mrow[:, 2:3]))
        p.step("vector", lambda v, s=s: v.tensor_mul(mrow[:, 0:1], mrow[:, 3:4],
                                                     nrm[:, s:s + 1]))

        for k in range(NT):
            p.step("tensor", lambda t, k=k: t.transpose(ptr[:, :],
                                                        pexp[:, 128 * k:128 * (k + 1)],
                                                        idb[:, :]))
            p.step("vector", lambda v, k=k: v.tensor_copy(pT[:, k, :], ptr[:, :]))

        def mmct(t, s=s):
            for half in range(2):
                for k in range(NT):
                    last = t.matmul(pct[half][:, :], pT[:, k, :],
                                    vtm[k][:, 512 * half:512 * (half + 1)],
                                    start=(k == 0), stop=(k == NT - 1))
            return last
        p.step("tensor", mmct)

        def evct(v, s=s):
            v.tensor_scalar(xa[:, 0:512], pct[0][:, :], mrow[:, 0:1], None, ALU.mult)
            return v.tensor_scalar(xa[:, 512:1024], pct[1][:, :], mrow[:, 0:1], None, ALU.mult)
        p.step("vector", evct)
        p.step("vector", lambda v: v.tensor_add(xa[:, :], xa[:, :], x2t[:, :]))
        ln_steps(p, nc, (stats6, mv, srt, rstd), xa, x3t, x3b)
        p.dma_step("scalar", lambda e, s=s: [
            e.dma_start(out=x3[128 * s:128 * (s + 1), :], in_=x3t[:, :]),
            e.dma_start(out=x3_bf[128 * s:128 * (s + 1), :], in_=x3b[:, :])])
    p.step("gpsimd", lambda g: g.nop())


def build_l1attn():
    nc = bass.Bass("TRN2", target_bir_lowering=False)
    dr = {}
    dr['qT_st'] = nc.dram_tensor("qT_st", [D, T], BF16, kind="ExternalInput").ap()
    dr['tqT_st'] = nc.dram_tensor("tqT_st", [D, T], BF16, kind="ExternalInput").ap()
    dr['v_tm_st'] = nc.dram_tensor("v_tm_st", [T, D], BF16, kind="ExternalInput").ap()
    dr['x2'] = nc.dram_tensor("x2", [T, D], F32, kind="ExternalInput").ap()
    dr['norm_st'] = nc.dram_tensor("norm_st", [128, NT], F32, kind="ExternalInput").ap()
    dr['ident_bf'] = nc.dram_tensor("ident_bf", [128, 128], BF16, kind="ExternalInput").ap()
    dr['x3'] = nc.dram_tensor("x3", [T, D], F32, kind="ExternalOutput").ap()
    dr['x3_bf'] = nc.dram_tensor("x3_bf", [T, D], BF16, kind="ExternalOutput").ap()
    with nc.Block() as block:
        p = P(nc, block)
        p.push_phase("l1attn")
        emit_l1attn(nc, p, dr)
        p.pop_phase()
    return nc


def emit_moe(nc, p, dr):
    x3 = dr["x3"]
    x3_bf = dr["x3_bf"]
    router = dr["router"]
    eg = dr["eg"]
    eu = dr["eu"]
    ed = dr["ed"]
    sgw = dr["sgw"]
    suw = dr["suw"]
    sdw = dr["sdw"]
    out = dr["out"]
    sh_act = nc.dram_tensor("sh_act", [D, T], BF16)
    e_act = [nc.dram_tensor(f"e_act{e}", [512, T], BF16) for e in range(8)]
    x3T = [p.sb(f"x3T{i}", [128, T], BF16) for i in range(ND)]
    wb = [p.sb(f"wb{i}", [128, D], BF16) for i in range(ND)]
    wb2 = [p.sb(f"wb2{i}", [128, D], BF16) for i in range(ND)]
    rt = p.sb("rt", [128, ND, 8], BF16)
    coef = p.sb("coef", [128, NT, 8], F32)
    p8 = p.sb("p8", [128, 8], F32)
    tmp8 = p.sb("tmp8", [128, 8], F32)
    msk = p.sb("msk", [128, 8], F32)
    sc8 = p.sb("sc8", [128, 6], F32)
    af = p.sb("af", [128, T], BF16)
    bf_ = p.sb("bf_", [128, T], BF16)
    ob = p.sb("ob", [128, T], BF16)
    acc = p.sb("acc", [128, D], F32)
    x3t_f = p.sb("x3t_f", [128, D], F32)
    o4 = p.sb("o4", [128, D], F32)
    slc = p.sb("slc", [128, ND, 128], BF16)
    eslc = p.sb("eslc", [128, 4, 128], BF16)
    edw = [[p.sb(f"edw{e}_{k}", [128, D], BF16) for k in range(4)]
           for e in range(8)]
    stats6 = p.sb("stats6", [128, 12], F32)
    mv = p.sb("mv", [128, 2], F32)
    srt = p.sb("srt", [128, 1], F32)
    rstd = p.sb("rstd", [128, 1], F32)
    pss = [p.pt(f"pss{i}", [128, 512], F32) for i in range(2)]
    psr = p.pt("psr", [128, 8], F32)
    eps_ap(nc, p, 1e-5, "eps5")

    p.dma_step("sync", lambda e: [
        e.dma_start(out=x3T[i][:, :], in_=x3_bf[:, 128 * i:128 * (i + 1)], transpose=True)
        for i in range(ND)])
    p.dma_step("gpsimd", lambda g: [
        g.dma_start(out=rt[:, :, :], in_=router.rearrange("(a p) c -> p a c", p=128))])

    for tt in range(NT):
        def mmr(t, tt=tt):
            for k in range(ND):
                last = t.matmul(psr[:, :], x3T[k][:, 128 * tt:128 * (tt + 1)], rt[:, k, :],
                                start=(k == 0), stop=(k == ND - 1))
            return last
        p.step("tensor", mmr)
        p.step("vector", lambda v: v.tensor_reduce(sc8[:, 0:1], psr[:, :], AX.X, ALU.max))
        p.step("vector", lambda v: v.tensor_scalar(sc8[:, 1:2], sc8[:, 0:1], -1.0, None, ALU.mult))
        p.step("vector", lambda v: v.tensor_copy(p8[:, :], psr[:, :]))
        p.step("scalar", lambda se: se.activation(out=p8[:, :], in_=p8[:, :], func=AF.Exp,
                                                  bias=sc8[:, 1:2], accum_out=sc8[:, 2:3]))
        p.step("vector", lambda v: v.reciprocal(sc8[:, 3:4], sc8[:, 2:3]))
        p.step("vector", lambda v: v.tensor_scalar(p8[:, :], p8[:, :], sc8[:, 3:4], None, ALU.mult))
        p.step("vector", lambda v: v.tensor_reduce(sc8[:, 4:5], p8[:, :], AX.X, ALU.max))
        p.step("vector", lambda v: v.tensor_scalar(msk[:, :], p8[:, :], sc8[:, 4:5],
                                                   None, ALU.is_ge))
        p.step("vector", lambda v: v.scalar_tensor_tensor(tmp8[:, :], msk[:, :], -2.0,
                                                          p8[:, :], ALU.mult, ALU.add))
        p.step("vector", lambda v: v.tensor_reduce(sc8[:, 5:6], tmp8[:, :], AX.X, ALU.max))
        p.step("vector", lambda v: v.tensor_scalar(msk[:, :], p8[:, :], sc8[:, 5:6],
                                                   None, ALU.is_ge))
        p.step("vector", lambda v, tt=tt: v.tensor_tensor(coef[:, tt, :], p8[:, :],
                                                          msk[:, :], ALU.mult))

    p.dma_step("gpsimd", lambda g: [
        g.dma_start(out=wb[i][:, :], in_=sgw[128 * i:128 * (i + 1), :]) for i in range(ND)] + [
        g.dma_start(out=wb2[i][:, :], in_=suw[128 * i:128 * (i + 1), :]) for i in range(ND)])
    for ot in range(ND):
        def mms(t, tb, bank, ot=ot):
            for k in range(ND):
                last = t.matmul(pss[bank][:, :], wb[k][:, 128 * ot:128 * (ot + 1)],
                                x3T[k][:, 512 * tb:512 * (tb + 1)],
                                start=(k == 0), stop=(k == ND - 1))
            return last

        def ev(se, tb, bank, ot=ot):
            return se.activation(out=af[:, 512 * tb:512 * (tb + 1)], in_=pss[bank][:, :],
                                 func=AF.Silu)
        p.mm_pipe(list(range(8)), mms, ev, evac_engine="scalar")

        def mms2(t, tb, bank, ot=ot):
            for k in range(ND):
                last = t.matmul(pss[bank][:, :], wb2[k][:, 128 * ot:128 * (ot + 1)],
                                x3T[k][:, 512 * tb:512 * (tb + 1)],
                                start=(k == 0), stop=(k == ND - 1))
            return last

        def ev2(v, tb, bank, ot=ot):
            return v.tensor_copy(bf_[:, 512 * tb:512 * (tb + 1)], pss[bank][:, :])
        p.mm_pipe(list(range(8)), mms2, ev2)
        p.step("vector", lambda v: v.tensor_mul(ob[:, :], af[:, :], bf_[:, :]))
        p.dma_step("scalar", lambda e, ot=ot: [
            e.dma_start(out=sh_act[128 * ot:128 * (ot + 1), :], in_=ob[:, :])])

    ew = [wb[i] for i in range(ND)]
    ew2 = [wb2[i] for i in range(ND)]
    for e in range(8):
        p.dma_step("gpsimd", lambda g, e=e: [
            g.dma_start(out=ew[i][:, 0:512], in_=eg[e, 128 * i:128 * (i + 1), :])
            for i in range(ND)] + [
            g.dma_start(out=ew2[i][:, 0:512], in_=eu[e, 128 * i:128 * (i + 1), :])
            for i in range(ND)])
        for ot in range(4):
            def mms(t, tb, bank, ot=ot):
                for k in range(ND):
                    last = t.matmul(pss[bank][:, :], ew[k][:, 128 * ot:128 * (ot + 1)],
                                    x3T[k][:, 512 * tb:512 * (tb + 1)],
                                    start=(k == 0), stop=(k == ND - 1))
                return last

            def ev(se, tb, bank, ot=ot):
                return se.activation(out=af[:, 512 * tb:512 * (tb + 1)], in_=pss[bank][:, :],
                                     func=AF.Silu)
            p.mm_pipe(list(range(8)), mms, ev, evac_engine="scalar")

            def mms2(t, tb, bank, ot=ot):
                for k in range(ND):
                    last = t.matmul(pss[bank][:, :], ew2[k][:, 128 * ot:128 * (ot + 1)],
                                    x3T[k][:, 512 * tb:512 * (tb + 1)],
                                    start=(k == 0), stop=(k == ND - 1))
                return last

            def ev2(v, tb, bank, ot=ot):
                return v.tensor_copy(bf_[:, 512 * tb:512 * (tb + 1)], pss[bank][:, :])
            p.mm_pipe(list(range(8)), mms2, ev2)
            p.step("vector", lambda v: v.tensor_mul(ob[:, :], af[:, :], bf_[:, :]))
            p.dma_step("scalar", lambda ee, ot=ot, e=e: [
                ee.dma_start(out=e_act[e][128 * ot:128 * (ot + 1), :], in_=ob[:, :])])

    p.dma_step("gpsimd", lambda g: [
        g.dma_start(out=wb[i][:, :], in_=sdw[128 * i:128 * (i + 1), :]) for i in range(ND)])
    p.dma_step("gpsimd", lambda g: [
        g.dma_start(out=edw[e][k][:, :], in_=ed[e, 128 * k:128 * (k + 1), :])
        for e in range(8) for k in range(4)])
    for tt in range(NT):
        p.dma_step("sync", lambda e, tt=tt: [
            e.dma_start(out=slc[:, :, :],
                        in_=sh_act[:, 128 * tt:128 * (tt + 1)].rearrange("(a p) c -> p a c", p=128)),
            e.dma_start(out=x3t_f[:, :], in_=x3[128 * tt:128 * (tt + 1), :])])

        def mmsh(t, tt=tt):
            for half in range(2):
                for k in range(ND):
                    last = t.matmul(pss[half][:, :], slc[:, k, :],
                                    wb[k][:, 512 * half:512 * (half + 1)],
                                    start=(k == 0), stop=(k == ND - 1))
            return last
        p.step("tensor", mmsh)

        def evsh(v, tt=tt):
            v.tensor_copy(acc[:, 0:512], pss[0][:, :])
            return v.tensor_copy(acc[:, 512:1024], pss[1][:, :])
        p.step("vector", evsh)
        for e in range(8):
            p.dma_step("sync", lambda en, tt=tt, e=e: [
                en.dma_start(out=eslc[:, :, :],
                             in_=e_act[e][:, 128 * tt:128 * (tt + 1)].rearrange("(a p) c -> p a c", p=128))])

            def mme(t, tt=tt, e=e):
                for half in range(2):
                    for k in range(4):
                        last = t.matmul(pss[half][:, :], eslc[:, k, :],
                                        edw[e][k][:, 512 * half:512 * (half + 1)],
                                        start=(k == 0), stop=(k == 3))
                return last
            p.step("tensor", mme)

            def eve(v, tt=tt, e=e):
                v.scalar_tensor_tensor(acc[:, 0:512], pss[0][:, :], coef[:, tt, e:e + 1],
                                       acc[:, 0:512], ALU.mult, ALU.add)
                return v.scalar_tensor_tensor(acc[:, 512:1024], pss[1][:, :],
                                              coef[:, tt, e:e + 1], acc[:, 512:1024],
                                              ALU.mult, ALU.add)
            p.step("vector", eve)
        p.step("vector", lambda v: v.tensor_add(acc[:, :], acc[:, :], x3t_f[:, :]))
        ln_steps(p, nc, (stats6, mv, srt, rstd), acc, o4)
        p.dma_step("scalar", lambda e, tt=tt: [
            e.dma_start(out=out[128 * tt:128 * (tt + 1), :], in_=o4[:, :])])
    p.step("gpsimd", lambda g: g.nop())


def build_moe():
    nc = bass.Bass("TRN2", target_bir_lowering=False)
    dr = {}
    dr['x3'] = nc.dram_tensor("x3", [T, D], F32, kind="ExternalInput").ap()
    dr['x3_bf'] = nc.dram_tensor("x3_bf", [T, D], BF16, kind="ExternalInput").ap()
    dr['router'] = nc.dram_tensor("router", [D, 8], F32, kind="ExternalInput").ap()
    dr['eg'] = nc.dram_tensor("eg", [8, D, 512], F32, kind="ExternalInput").ap()
    dr['eu'] = nc.dram_tensor("eu", [8, D, 512], F32, kind="ExternalInput").ap()
    dr['ed'] = nc.dram_tensor("ed", [8, 512, D], F32, kind="ExternalInput").ap()
    dr['sgw'] = nc.dram_tensor("sgw", [D, D], F32, kind="ExternalInput").ap()
    dr['suw'] = nc.dram_tensor("suw", [D, D], F32, kind="ExternalInput").ap()
    dr['sdw'] = nc.dram_tensor("sdw", [D, D], F32, kind="ExternalInput").ap()
    dr['out'] = nc.dram_tensor("out", [T, D], F32, kind="ExternalOutput").ap()
    with nc.Block() as block:
        p = P(nc, block)
        p.push_phase("moe")
        emit_moe(nc, p, dr)
        p.pop_phase()
    return nc


def host_l1_inputs(inputs, core):
    f = np.ascontiguousarray
    return {
        "x": f(inputs["L0_attn_hidden_states"][core]).astype(np.float32, copy=False),
        "wqT": f(inputs["L0_attn_q_w"].T),
        "wkT": f(inputs["L0_attn_k_w"].T),
        "wvT": f(inputs["L0_attn_v_w"].T),
        "bwT": f(inputs["L0_attn_b_w"].T),
        "qcw": f(inputs["L0_attn_q_cw"].reshape(ND, 128, 4).transpose(1, 0, 2).reshape(128, ND * 4)),
        "kcw": f(inputs["L0_attn_k_cw"].reshape(ND, 128, 4).transpose(1, 0, 2).reshape(128, ND * 4)),
        "vcw": f(inputs["L0_attn_v_cw"].reshape(ND, 128, 4).transpose(1, 0, 2).reshape(128, ND * 4)),
    }


def host_l6a_inputs(l5_all, core):
    n = len(l5_all)
    return {
        "up_st": l5_all[core]["up_st"], "gate_st": l5_all[core]["gate_st"],
        "am8u": _amax8(l5_all, "amax_up"),
        "am8g": _amax8(l5_all, "amax_gate"),
        "ident": np.eye(128, dtype=np.float32),
    }


def host_l6b_inputs(l6a_all, core):
    n = len(l6a_all)
    return {
        "g_st": l6a_all[core]["g_st"], "upq_st": l6a_all[core]["upq_st"],
        "am8": _amax8(l6a_all, "amax_g"),
        "ident": np.eye(128, dtype=np.float32),
    }


def host_l6c_inputs(l5_all, l4b_all, l6b_all, core):
    n = len(l6b_all)
    return {
        "opre_st": l6b_all[core]["opre_st"],
        "wdq_down": l5_all[core]["wdq_down"],
        "x1": l4b_all[core]["x1"],
        "am8": _amax8(l6b_all, "amax_o"),
    }


def host_l7a_inputs(inputs, l6c_out):
    return {
        "scaled_bf": l6c_out["scaled_bf"],
        "WqT": np.ascontiguousarray(inputs["L1_attn_Wq"].T),
        "WvT": np.ascontiguousarray(inputs["L1_attn_Wv"].T),
        "bq": np.ascontiguousarray(inputs["L1_attn_bq"].reshape(ND, 128).T),
        "bv1024": np.ascontiguousarray(inputs["L1_attn_bv"].reshape(1, D)),
        "Amat": np.ascontiguousarray(inputs["L1_attn_anomaly_matrix"]),
    }


def host_l7b_inputs(l6c_out, l7a_out):
    return {
        "qT_st": l7a_out["qT_st"], "tqT_st": l7a_out["tqT_st"], "v_tm_st": l7a_out["v_tm_st"],
        "x2": l6c_out["x2"], "norm_st": l6c_out["norm_st"],
        "ident_bf": np.eye(128, dtype=BF),
    }


def host_l8_inputs(inputs, l7b_out):
    return {
        "x3": l7b_out["x3"], "x3_bf": l7b_out["x3_bf"],
        "router": np.ascontiguousarray(inputs["L1_ffn_router"]),
        "eg": np.ascontiguousarray(inputs["L1_ffn_experts_gate"]),
        "eu": np.ascontiguousarray(inputs["L1_ffn_experts_up"]),
        "ed": np.ascontiguousarray(inputs["L1_ffn_experts_down"]),
        "sgw": np.ascontiguousarray(inputs["L1_ffn_shared_gate"]),
        "suw": np.ascontiguousarray(inputs["L1_ffn_shared_up"]),
        "sdw": np.ascontiguousarray(inputs["L1_ffn_shared_down"]),
    }


def build_fused():
    nc = bass.Bass("TRN2", target_bir_lowering=False)
    E_IN, E_OUT = "ExternalInput", "ExternalOutput"
    dt = nc.dram_tensor
    x = dt("x", [T, D], F32, kind=E_IN).ap()
    wT = {nm: dt(f"w{nm}T", [D, D], F32, kind=E_IN).ap() for nm in "qkv"}
    bwT = dt("bwT", [D, H], F32, kind=E_IN).ap()
    cw = {nm: dt(f"{nm}cw", [128, ND * 4], F32, kind=E_IN).ap() for nm in "qkv"}
    eye64 = dt("eye64", [C2, C2], BF16, kind=E_IN).ap()
    masks = dt("masks", [C2, 3 * C2], F32, kind=E_IN).ap()
    fsw = dt("fsw", [128, ND * 5], F32, kind=E_IN).ap()
    flw = dt("flw", [128, ND * 64], F32, kind=E_IN).ap()
    mlp1T = dt("mlp1T", [1040, D], F32, kind=E_IN).ap()
    mlp1b = dt("mlp1b", [128, ND], F32, kind=E_IN).ap()
    mlp2T = dt("mlp2T", [D, H], F32, kind=E_IN).ap()
    mlp2b = dt("mlp2b", [1, H], F32, kind=E_IN).ap()
    ltemp = dt("ltemp", [1, 1], F32, kind=E_IN).ap()
    crl = dt("crl", [1, H], F32, kind=E_IN).ap()
    ident = dt("ident", [128, 128], F32, kind=E_IN).ap()
    ident_bf = dt("ident_bf", [128, 128], BF16, kind=E_IN).ap()
    owT = dt("owT", [D, D], F32, kind=E_IN).ap()
    nw_bc = dt("nw_bc", [128, ND], F32, kind=E_IN).ap()
    w_up = dt("w_up", [DFF, D], F32, kind=E_IN).ap()
    w_gate = dt("w_gate", [DFF, D], F32, kind=E_IN).ap()
    w_down = dt("w_down", [D, DFF], F32, kind=E_IN).ap()
    WqT = dt("WqT", [D, D], F32, kind=E_IN).ap()
    WvT = dt("WvT", [D, D], F32, kind=E_IN).ap()
    bq = dt("bq", [128, ND], F32, kind=E_IN).ap()
    bv1024 = dt("bv1024", [1, D], F32, kind=E_IN).ap()
    Amat = dt("Amat", [D, D], F32, kind=E_IN).ap()
    router = dt("router", [D, 8], F32, kind=E_IN).ap()
    eg = dt("eg", [8, D, 512], F32, kind=E_IN).ap()
    eu = dt("eu", [8, D, 512], F32, kind=E_IN).ap()
    ed = dt("ed", [8, 512, D], F32, kind=E_IN).ap()
    sgw = dt("sgw", [D, D], F32, kind=E_IN).ap()
    suw = dt("suw", [D, D], F32, kind=E_IN).ap()
    sdw = dt("sdw", [D, D], F32, kind=E_IN).ap()
    out = dt("out", [T, D], F32, kind=E_OUT).ap()
    qn_tm = dt("qn_tm", [T, D], BF16).ap()
    kn_tm = dt("kn_tm", [T, D], BF16).ap()
    v_tm = dt("v_tm", [T, D], BF16).ap()
    v_fm = dt("v_fm", [D, T], BF16).ap()
    beta_tm = dt("beta_tm", [T, H], F32).ap()
    x_bf = dt("x_bf", [T, D], BF16).ap()
    delta_tm = dt("delta_tm", [T, D], BF16).ap()
    ls_fm = dt("ls_fm", [D, T], BF16).ap()
    ll_fm = dt("ll_fm", [D, T], BF16).ap()
    o_tm = dt("o_tm", [T, D], BF16).ap()
    x1 = dt("x1", [T, D], F32).ap()
    x1_bf = dt("x1_bf", [T, D], BF16).ap()
    up_st = dt("up_st", [DFF, T], BF16).ap()
    gate_st = dt("gate_st", [DFF, T], BF16).ap()
    wdq_down = dt("wdq_down", [D, DFF], BF16).ap()
    upq_st = dt("upq_st", [DFF, T], BF16).ap()
    g_st = dt("g_st", [DFF, T], BF16).ap()
    opre_st = dt("opre_st", [DFF, T], BF16).ap()
    x2 = dt("x2", [T, D], F32).ap()
    x2_bf = dt("x2_bf", [T, D], BF16).ap()
    scaled_bf = dt("scaled_bf", [T, D], BF16).ap()
    norm_stg = dt("norm_stg", [128, NT], F32).ap()
    qT_st = dt("qT_st", [D, T], BF16).ap()
    tqT_st = dt("tqT_st", [D, T], BF16).ap()
    v_tm_st = dt("v_tm_st", [T, D], BF16).ap()
    x3 = dt("x3", [T, D], F32).ap()
    x3_bf = dt("x3_bf", [T, D], BF16).ap()
    amx1 = dt("amx1", [1, 8], F32).ap()
    amup = dt("amup", [1, 8], F32).ap()
    amgate = dt("amgate", [1, 8], F32).ap()
    amg = dt("amg", [1, 8], F32).ap()
    amo = dt("amo", [1, 8], F32).ap()

    with nc.Block() as block:
        p = P(nc, block)
        phases = [
            ("qkvb", emit_qkvb, dict(x=x, wT=wT, bwT=bwT, cw=cw, qn_tm=qn_tm,
                                     kn_tm=kn_tm, v_tm=v_tm, v_fm=v_fm,
                                     beta_tm=beta_tm, x_bf=x_bf)),
            ("delta", emit_delta, dict(qn_tm=qn_tm, kn_tm=kn_tm, v_tm=v_tm,
                                       beta_tm=beta_tm, eye64=eye64, masks=masks,
                                       delta_tm=delta_tm)),
            ("fir", emit_fir, dict(v_fm=v_fm, fsw=fsw, flw=flw, ls_fm=ls_fm, ll_fm=ll_fm)),
            ("gate", emit_gate, dict(x_bf=x_bf, v_tm=v_tm, delta_tm=delta_tm,
                                     ls_fm=ls_fm, ll_fm=ll_fm, mlp1T=mlp1T, mlp1b=mlp1b,
                                     mlp2T=mlp2T, mlp2b=mlp2b, ltemp=ltemp, crl=crl,
                                     ident=ident, o_tm=o_tm)),
            ("ow", emit_ow, dict(x=x, o_tm=o_tm, owT=owT, nw_bc=nw_bc, ident=ident,
                                 x1=x1, x1_bf=x1_bf, amax=amx1[0:1, 0:1])),
            ("ffn_up", emit_ffn_up, dict(x1_bf=x1_bf, w_up=w_up, w_gate=w_gate,
                                         w_down=w_down, amax8=amx1, ident=ident,
                                         up_st=up_st, gate_st=gate_st, wdq_down=wdq_down,
                                         amax_up=amup[0:1, 0:1], amax_gate=amgate[0:1, 0:1],
                                         am8_n=1)),
            ("qchain_a", emit_qchain_a, dict(up_st=up_st, gate_st=gate_st, am8u=amup,
                                             am8g=amgate, ident=ident, upq_st=upq_st,
                                             g_st=g_st, amax_g=amg[0:1, 0:1], am8_n=1)),
            ("qchain_b", emit_qchain_b, dict(g_st=g_st, upq_st=upq_st, am8=amg,
                                             ident=ident, opre_st=opre_st,
                                             amax_o=amo[0:1, 0:1], am8_n=1)),
            ("qchain_c", emit_qchain_c, dict(opre_st=opre_st, wdq_down=wdq_down, x1=x1,
                                             am8=amo, x2=x2, x2_bf=x2_bf,
                                             scaled_bf=scaled_bf, norm_st=norm_stg,
                                             am8_n=1)),
            ("l1pre", emit_l1pre, dict(scaled_bf=scaled_bf, WqT=WqT, WvT=WvT, bq=bq,
                                       bv1024=bv1024, Amat=Amat, qT_st=qT_st,
                                       tqT_st=tqT_st, v_tm_st=v_tm_st)),
            ("l1attn", emit_l1attn, dict(qT_st=qT_st, tqT_st=tqT_st, v_tm_st=v_tm_st,
                                         x2=x2, norm_st=norm_stg, ident_bf=ident_bf,
                                         x3=x3, x3_bf=x3_bf)),
            ("moe", emit_moe, dict(x3=x3, x3_bf=x3_bf, router=router, eg=eg, eu=eu,
                                   ed=ed, sgw=sgw, suw=suw, sdw=sdw, out=out)),
        ]
        for name, fn, dr in phases:
            p.push_phase(name)
            fn(nc, p, dr)
            p.pop_phase()
    return nc


def host_fused_inputs(inputs, core):
    d = {}
    d.update(host_l1_inputs(inputs, core))
    d.update({k: v for k, v in host_l2_inputs({"qn_tm": None, "kn_tm": None,
                                               "v_tm": None, "beta_tm": None}).items()
              if k in ("eye64", "masks")})
    d.update({k: v for k, v in host_l3_inputs(inputs, {"v_fm": None}).items()
              if k in ("fsw", "flw")})
    l4a = host_l4a_inputs(inputs, {"x_bf": None, "v_tm": None}, {"delta_tm": None},
                          {"ls_fm": None, "ll_fm": None})
    d.update({k: v for k, v in l4a.items() if v is not None and k not in
              ("x_bf", "v_tm", "delta_tm", "ls_fm", "ll_fm")})
    l4b = host_l4b_inputs(inputs, {"o_tm": None}, core)
    d.update({k: v for k, v in l4b.items() if k in ("owT", "nw_bc")})
    d["ident_bf"] = np.eye(128, dtype=BF)
    d["w_up"] = np.ascontiguousarray(inputs["L0_ffn_w_up"])
    d["w_gate"] = np.ascontiguousarray(inputs["L0_ffn_w_gate"])
    d["w_down"] = np.ascontiguousarray(inputs["L0_ffn_w_down"])
    l7a = host_l7a_inputs(inputs, {"scaled_bf": None})
    d.update({k: v for k, v in l7a.items() if k != "scaled_bf"})
    l8 = host_l8_inputs(inputs, {"x3": None, "x3_bf": None})
    d.update({k: v for k, v in l8.items() if k not in ("x3", "x3_bf")})
    return d


def kernel(**inputs):
    cores = list(range(B))
    nc = build_fused()
    maps = [host_fused_inputs(inputs, c) for c in cores]
    res = run_bass_kernel_spmd(nc, maps, core_ids=cores)
    return np.stack([np.asarray(res.results[c]["out"]) for c in cores])


def kernel_multilaunch(**inputs):
    n = B
    cores = list(range(n))

    def run(nc, maps):
        return run_bass_kernel_spmd(nc, maps, core_ids=cores).results

    r1 = run(build_qkvb(), [host_l1_inputs(inputs, c) for c in cores])
    r2 = run(build_delta(), [host_l2_inputs(r1[c]) for c in cores])
    r3 = run(build_fir(), [host_l3_inputs(inputs, r1[c]) for c in cores])
    r4a = run(build_gate(), [host_l4a_inputs(inputs, r1[c], r2[c], r3[c]) for c in cores])
    r4b = run(build_ow(), [host_l4b_inputs(inputs, r4a[c], c) for c in cores])
    r5 = run(build_ffn_up(), [host_l5_inputs(inputs, r4b, c) for c in cores])
    r6a = run(build_qchain_a(), [host_l6a_inputs(r5, c) for c in cores])
    r6b = run(build_qchain_b(), [host_l6b_inputs(r6a, c) for c in cores])
    r6c = run(build_qchain_c(), [host_l6c_inputs(r5, r4b, r6b, c) for c in cores])
    r7a = run(build_l1pre(), [host_l7a_inputs(inputs, r6c[c]) for c in cores])
    r7b = run(build_l1attn(), [host_l7b_inputs(r6c[c], r7a[c]) for c in cores])
    r8 = run(build_moe(), [host_l8_inputs(inputs, r7b[c]) for c in cores])
    return np.stack([np.asarray(r8[c]["out"]) for c in cores])


# version 03
# speedup vs baseline: 1.1285x; 1.1204x over previous
import math
import numpy as np
import ml_dtypes
import concourse.bass as bass
import concourse.mybir as mybir
from concourse.bass_utils import run_bass_kernel_spmd

F32 = mybir.dt.float32
BF16 = mybir.dt.bfloat16
I32 = mybir.dt.int32
AF = mybir.ActivationFunctionType
ALU = mybir.AluOpType
AX = mybir.AxisListType

B, T, D = 8, 4096, 1024
H, DK, DV = 4, 256, 256
NT = T // 128
ND = D // 128
CHUNK = 32
NCH = T // CHUNK
DFF = 3072
NF = DFF // 128

BF = ml_dtypes.bfloat16


class Sem:

    def __init__(self, nc, name):
        self.h = nc.alloc_semaphore(name)
        self.n = 0

    def inc(self, instr, k=1):
        instr.then_inc(self.h, k)
        self.n += k
        return instr

    def wait(self, eng, target=None):
        eng.wait_ge(self.h, self.n if target is None else target)


class P:

    def __init__(self, nc, block):
        self.nc = nc
        self.block = block
        self.bar = nc.alloc_semaphore("bar")
        self.cnt = 0
        self._pool = {}
        self.phase = ""
        self._stack = None

    def push_phase(self, name):
        from contextlib import ExitStack
        self.phase = name
        self._eps = {}
        self._stack = ExitStack()
        self._stack.__enter__()

    def pop_phase(self):
        self._stack.__exit__(None, None, None)
        self._stack = None

    def sb(self, name, shape, dtype):
        return self._stack.enter_context(
            self.nc.sbuf_tensor(f"{self.phase}_{name}", shape, dtype))

    def pt(self, name, shape, dtype=mybir.dt.float32):
        return self._stack.enter_context(
            self.nc.psum_tensor(f"{self.phase}_{name}", shape, dtype))

    def dram(self, name, shape, dtype):
        return self.nc.dram_tensor(f"{self.phase}_{name}", shape, dtype)

    def sem(self, name):
        if name not in self._pool:
            self._pool[name] = Sem(self.nc, name)
        return self._pool[name]

    def aux_sem(self):
        raise RuntimeError("use p.sem(name) — hardware semaphores are scarce")

    def step(self, engine, fn):
        need = self.cnt

        def body(eng):
            if need:
                eng.wait_ge(self.bar, need)
            last = fn(eng)
            assert last is not None, "step fn must return its last instruction"
            last.then_inc(self.bar, 1)
        getattr(self.block, engine)(body)
        self.cnt += 1

    def dma_step(self, engine, fn):
        need = self.cnt
        cnt_holder = [None]

        def body(eng):
            if need:
                eng.wait_ge(self.bar, need)
            insts = fn(eng)
            for ins in insts:
                ins.then_inc(self.bar, 16)
            cnt_holder[0] = len(insts)
        getattr(self.block, engine)(body)
        assert cnt_holder[0] and cnt_holder[0] > 0
        self.cnt += 16 * cnt_holder[0]

    def mm_pipe(self, jobs, emit_mms, emit_evac, evac_engine="vector", nbanks=2):
        s_mm = self.sem("mmp_mm")
        s_ev = self.sem("mmp_ev")
        mm0, ev0 = s_mm.n, s_ev.n
        start = self.cnt
        njobs = len(jobs)

        def body_t(t):
            if start:
                t.wait_ge(self.bar, start)
            for i, job in enumerate(jobs):
                if i >= nbanks:
                    t.wait_ge(s_ev.h, ev0 + i - nbanks + 1)
                emit_mms(t, job, i % nbanks).then_inc(s_mm.h, 1)

        def body_v(v):
            for i, job in enumerate(jobs):
                v.wait_ge(s_mm.h, mm0 + i + 1)
                emit_evac(v, job, i % nbanks).then_inc(s_ev.h, 1)
            v.wait_ge(s_ev.h, ev0 + njobs)
            v.nop().then_inc(self.bar, 1)
        self.block.tensor(body_t)
        getattr(self.block, evac_engine)(body_v)
        s_mm.n += njobs
        s_ev.n += njobs
        self.cnt += 1


def conv_taps_fm(p, src, dst, wcol_ap_fn, ntaps, width=T):
    n = len(src)

    def tap_al(v):
        last = None
        for ot in range(n):
            last = v.tensor_scalar(dst[ot][:, :], src[ot][:, :],
                                   wcol_ap_fn(ot, ntaps - 1), None, ALU.mult)
        return last
    p.step("vector", tap_al)
    for d in range(1, ntaps):
        def tap_d(v, d=d):
            last = None
            for ot in range(n):
                last = v.scalar_tensor_tensor(
                    dst[ot][:, d:width], src[ot][:, 0:width - d],
                    wcol_ap_fn(ot, ntaps - 1 - d), dst[ot][:, d:width],
                    ALU.mult, ALU.add)
            return last
        p.step("vector", tap_d)


def emit_qkvb(nc, p, dr):
    x = dr["x"]
    wT = dr["wT"]
    bwT = dr["bwT"]
    cw = dr["cw"]
    qn_tm = dr["qn_tm"]
    kn_tm = dr["kn_tm"]
    v_tm = dr["v_tm"]
    v_fm = dr["v_fm"]
    beta_tm = dr["beta_tm"]
    x_bf = dr["x_bf"]
    q_raw = nc.dram_tensor("q_raw", [D, T], BF16)
    k_raw = nc.dram_tensor("k_raw", [D, T], BF16)
    xT = [p.sb(f"xT{i}", [128, T], BF16) for i in range(ND)]
    w_sb = [p.sb(f"wsb{i}", [128, D], BF16) for i in range(ND)]
    bw_sb = [p.sb(f"bw{i}", [128, H], BF16) for i in range(ND)]
    cw_sb = {nm: p.sb(f"cw{nm}", [128, ND * 4], F32) for nm in "qkv"}
    xf32 = p.sb("xf32", [128, D], F32)
    xbf = p.sb("xbf", [128, D], BF16)
    pf32 = p.sb("pf32", [128, T], F32)
    pcv = p.sb("pcv", [128, T], F32)
    pbf = p.sb("pbf", [128, T], BF16)
    ps = [p.pt(f"ps{i}", [128, 512], F32) for i in range(2)]
    psb = p.pt("psb", [128, H], F32)
    btile = p.sb("btile", [128, H], F32)
    nb = 8
    tmi = [p.sb(f"tmi{i}", [128, D], BF16) for i in range(nb)]
    tmo = [p.sb(f"tmo{i}", [128, D], BF16) for i in range(nb)]
    sqs = p.sb("sqs", [128, 256], F32)
    ss = p.sb("ss", [128, nb * H], F32)
    rss = p.sb("rss", [128, nb * H], F32)
    eps = p.sb("eps", [128, 1], F32)

    def ld_cw(g):
        out = []
        for nm in "qkv":
            out.append(g.dma_start(out=cw_sb[nm][:, :], in_=cw[nm][:, :]))
        for i in range(ND):
            out.append(g.dma_start(out=bw_sb[i][:, :], in_=bwT[128 * i:128 * (i + 1), :]))
        return out
    p.dma_step("gpsimd", ld_cw)
    p.step("vector", lambda v: v.memset(eps[:, :], 1e-6))

    for i in range(NT):
        p.dma_step("sync", lambda e, i=i: [e.dma_start(out=xf32[:, :], in_=x[128 * i:128 * (i + 1), :])])
        p.step("vector", lambda v: v.tensor_copy(xbf[:, :], xf32[:, :]))
        p.dma_step("scalar", lambda e, i=i: [e.dma_start(out=x_bf[128 * i:128 * (i + 1), :], in_=xbf[:, :])])

    p.dma_step("sync", lambda e: [
        e.dma_start(out=xT[i][:, :], in_=x_bf[:, 128 * i:128 * (i + 1)], transpose=True)
        for i in range(ND)])

    btile2 = p.sb("btile2", [128, H], F32)
    s_mx = p.sem("s_mx").h
    s_sg = p.sem("s_sg").h
    s_sg2 = p.sem("s_sg2").h
    s_st = p.sem("s_st").h
    start0 = p.cnt

    def beta_t(t):
        t.wait_ge(p.bar, start0)
        for tt in range(NT):
            if tt >= 1:
                t.wait_ge(s_sg, tt)
            for k in range(ND):
                last = t.matmul(psb[:, :], xT[k][:, 128 * tt:128 * (tt + 1)],
                                bw_sb[k][:, :], start=(k == 0), stop=(k == ND - 1))
            last.then_inc(s_mx, 1)

    def beta_sc(se):
        for tt in range(NT):
            se.wait_ge(s_mx, tt + 1)
            if tt >= 1:
                se.wait_ge(s_sg2, tt)
            se.activation(out=btile[:, :], in_=psb[:, :], func=AF.Sigmoid).then_inc(s_sg, 1)

    def beta_v(v):
        for tt in range(NT):
            v.wait_ge(s_sg, tt + 1)
            if tt >= 1:
                v.wait_ge(s_st, 16 * tt)
            v.tensor_scalar(btile2[:, :], btile[:, :], 1e-6, None, ALU.max).then_inc(s_sg2, 1)

    def beta_g(g):
        for tt in range(NT):
            g.wait_ge(s_sg2, tt + 1)
            g.dma_start(out=beta_tm[128 * tt:128 * (tt + 1), :], in_=btile2[:, :]).then_inc(s_st, 16)
        g.wait_ge(s_st, 16 * NT)
        g.nop().then_inc(p.bar, 1)
    p.block.tensor(beta_t)
    p.block.scalar(beta_sc)
    p.block.vector(beta_v)
    p.block.gpsimd(beta_g)
    p.cnt += 1

    for nm, stash in (("q", q_raw), ("k", k_raw), ("v", v_fm)):
        p.dma_step("gpsimd", lambda g, nm=nm: [
            g.dma_start(out=w_sb[i][:, :], in_=wT[nm][128 * i:128 * (i + 1), :])
            for i in range(ND)])
        for ot in range(ND):
            def mms(t, tb, bank, ot=ot):
                for k in range(ND):
                    last = t.matmul(ps[bank][:, :], w_sb[k][:, 128 * ot:128 * (ot + 1)],
                                    xT[k][:, 512 * tb:512 * (tb + 1)],
                                    start=(k == 0), stop=(k == ND - 1))
                return last

            def ev(v, tb, bank, ot=ot):
                return v.tensor_copy(pf32[:, 512 * tb:512 * (tb + 1)], ps[bank][:, :])
            p.mm_pipe(list(range(8)), mms, ev)
            conv_taps_fm(p, [pf32], [pcv],
                         lambda o, j, nm=nm, ot=ot: cw_sb[nm][:, 4 * ot + j:4 * ot + j + 1], 4)
            p.step("scalar", lambda se: se.activation(out=pbf[:, :], in_=pcv[:, :], func=AF.Silu))
            p.dma_step("scalar", lambda e, ot=ot, stash=stash: [
                e.dma_start(out=stash[128 * ot:128 * (ot + 1), :], in_=pbf[:, :])])

    for src, dst, norm in ((q_raw, qn_tm, True), (k_raw, kn_tm, True), (v_fm, v_tm, False)):
        for b0 in range(0, NT, nb):
            p.dma_step("sync", lambda e, src=src, b0=b0: [
                e.dma_start(out=tmi[i][:, :], in_=src[:, 128 * (b0 + i):128 * (b0 + i + 1)],
                            transpose=True)
                for i in range(nb)])
            if norm:
                def sq_acc(se):
                    last = None
                    for i in range(nb):
                        for h in range(H):
                            last = se.activation(out=sqs[:, :], in_=tmi[i][:, 256 * h:256 * (h + 1)],
                                                 func=AF.Square,
                                                 accum_out=ss[:, i * H + h:i * H + h + 1])
                    return last
                p.step("scalar", sq_acc)
                p.step("scalar", lambda se: se.activation(out=rss[:, :], in_=ss[:, :],
                                                          func=AF.Sqrt, bias=eps[:, :]))
                p.step("vector", lambda v: v.reciprocal(ss[:, :], rss[:, :]))
                def scale(v):
                    last = None
                    for i in range(nb):
                        for h in range(H):
                            last = v.tensor_scalar(tmo[i][:, 256 * h:256 * (h + 1)],
                                                   tmi[i][:, 256 * h:256 * (h + 1)],
                                                   ss[:, i * H + h:i * H + h + 1], None, ALU.mult)
                    return last
                p.step("vector", scale)
                out_t = tmo
            else:
                out_t = tmi
            p.dma_step("scalar", lambda e, dst=dst, b0=b0, out_t=out_t: [
                e.dma_start(out=dst[128 * (b0 + i):128 * (b0 + i + 1), :], in_=out_t[i][:, :])
                for i in range(nb)])

    def fin(g):
        return g.nop()
    p.step("gpsimd", fin)


def build_qkvb():
    nc = bass.Bass("TRN2", target_bir_lowering=False)
    dr = {}
    dr['x'] = nc.dram_tensor("x", [T, D], F32, kind="ExternalInput").ap()
    dr['wT'] = {nm: nc.dram_tensor(f"w{nm}T", [D, D], F32, kind="ExternalInput").ap()
          for nm in "qkv"}
    dr['bwT'] = nc.dram_tensor("bwT", [D, H], F32, kind="ExternalInput").ap()
    dr['cw'] = {nm: nc.dram_tensor(f"{nm}cw", [128, ND * 4], F32, kind="ExternalInput").ap()
          for nm in "qkv"}
    dr['qn_tm'] = nc.dram_tensor("qn_tm", [T, D], BF16, kind="ExternalOutput").ap()
    dr['kn_tm'] = nc.dram_tensor("kn_tm", [T, D], BF16, kind="ExternalOutput").ap()
    dr['v_tm'] = nc.dram_tensor("v_tm", [T, D], BF16, kind="ExternalOutput").ap()
    dr['v_fm'] = nc.dram_tensor("v_fm", [D, T], BF16, kind="ExternalOutput").ap()
    dr['beta_tm'] = nc.dram_tensor("beta_tm", [T, H], F32, kind="ExternalOutput").ap()
    dr['x_bf'] = nc.dram_tensor("x_bf", [T, D], BF16, kind="ExternalOutput").ap()
    with nc.Block() as block:
        p = P(nc, block)
        p.push_phase("qkvb")
        emit_qkvb(nc, p, dr)
        p.pop_phase()
    return nc


C2 = 64
NC2 = T // C2


def emit_delta(nc, p, dr):
    qn_tm = dr["qn_tm"]
    kn_tm = dr["kn_tm"]
    v_tm = dr["v_tm"]
    beta_tm = dr["beta_tm"]
    eye64 = dr["eye64"]
    masks = dr["masks"]
    delta_tm = dr["delta_tm"]
    dbg = dr.get("dbg")
    kb_st = nc.dram_tensor("kb_st", [T, DK], BF16)
    vb_st = nc.dram_tensor("vb_st", [T, DK], BF16)
    qT = [p.sb(f"qT{i}", [128, T], BF16) for i in range(2)]
    kT = [p.sb(f"kT{i}", [128, T], BF16) for i in range(2)]
    kbT = [p.sb(f"kbT{i}", [128, T], BF16) for i in range(2)]
    wT = kbT
    k_cm = p.sb("k_cm", [C2, NC2, DK], BF16)
    kb_cm = p.sb("kb_cm", [C2, NC2, DK], BF16)
    vb_cm = p.sb("vb_cm", [C2, NC2, DK], BF16)
    u_cm = vb_cm
    ktmp = p.sb("ktmp", [128, 8, DK], BF16)
    beta_all = p.sb("beta_all", [128, NT, H], F32)
    kbtmp = p.sb("kbtmp", [128, 8, DK], BF16)
    N_all = p.sb("N_all", [C2, NC2, C2], BF16)
    M_all = p.sb("M_all", [C2, NC2, C2], BF16)
    NM = p.sb("NM", [C2, NC2, 2 * C2], BF16)
    P_c = p.sb("P_c", [C2, NC2, C2], BF16)
    attnT = p.sb("attnT", [C2, NC2, C2], BF16)
    eye_sb = p.sb("eye_sb", [C2, C2], BF16)
    msk_sb = p.sb("msk_sb", [C2, 3 * C2], F32)
    S_sb = p.sb("S_sb", [128, 2, DK], BF16)
    u_adj = p.sb("u_adj", [C2, DK], BF16)
    d_ring = [p.sb(f"d_ring{i}", [C2, DK], BF16) for i in range(4)]
    pp = [p.pt(f"pp{i}", [C2, 256], F32) for i in range(2)]
    pw = [p.pt(f"pw{i}", [128, 2 * C2], F32) for i in range(2)]
    ps_q = p.pt("ps_q", [C2, 256], F32)
    ps_w = p.pt("ps_w", [C2, 256], F32)
    ps_S = p.pt("ps_S", [128, 512], F32)

    p.dma_step("gpsimd", lambda g: [
        g.dma_start(out=eye_sb[:, :], in_=eye64[:, :]),
        g.dma_start(out=msk_sb[:, :], in_=masks[:, :]),
        g.dma_start(out=beta_all[:, :, :],
                    in_=beta_tm.rearrange("(n p) c -> p n c", p=128))])

    def ch(n, w=C2):
        return slice(C2 * n, C2 * n + w)

    m_su = msk_sb[:, 0:C2]
    m_sl = msk_sb[:, C2:2 * C2]
    m_iu = msk_sb[:, 2 * C2:3 * C2]

    for h in range(H):
        seg = slice(256 * h, 256 * (h + 1))
        for src, dst in ((kn_tm, kb_st), (v_tm, vb_st)):
            for r in range(4):
                rows = slice(1024 * r, 1024 * (r + 1))
                p.dma_step("sync", lambda e, src=src, rows=rows: [
                    e.dma_start(out=ktmp[:, :, :],
                                in_=src[rows, seg].rearrange("(n p) c -> p n c", p=128))])

                def bmul(v, r=r):
                    last = None
                    for n in range(8):
                        last = v.tensor_scalar(kbtmp[:, n, :], ktmp[:, n, :],
                                               beta_all[:, 8 * r + n, h:h + 1].opt(),
                                               None, ALU.mult)
                    return last
                p.step("vector", bmul)
                p.dma_step("scalar", lambda e, dst=dst, rows=rows: [
                    e.dma_start(out=dst[rows, :].rearrange("(n p) c -> p n c", p=128),
                                in_=kbtmp[:, :, :])])
        p.dma_step("sync", lambda e: [
            e.dma_start(out=qT[i][:, :],
                        in_=qn_tm[:, 256 * h + 128 * i:256 * h + 128 * (i + 1)],
                        transpose=True) for i in range(2)] + [
            e.dma_start(out=kT[i][:, :],
                        in_=kn_tm[:, 256 * h + 128 * i:256 * h + 128 * (i + 1)],
                        transpose=True) for i in range(2)] + [
            e.dma_start(out=kbT[i][:, :], in_=kb_st[:, 128 * i:128 * (i + 1)],
                        transpose=True) for i in range(2)] + [
            e.dma_start(out=k_cm[:, :, :],
                        in_=kn_tm[:, seg].rearrange("(n p) c -> p n c", p=C2)),
            e.dma_start(out=kb_cm[:, :, :],
                        in_=kb_st.rearrange("(n p) c -> p n c", p=C2)),
            e.dma_start(out=vb_cm[:, :, :],
                        in_=vb_st.rearrange("(n p) c -> p n c", p=C2))])

        CH = list(range(NC2))

        def s13_mm(t, n, b):
            t.matmul(pp[b][:, 0:C2], kT[0][:, ch(n)], qT[0][:, ch(n)], start=True, stop=False)
            return t.matmul(pp[b][:, 0:C2], kT[1][:, ch(n)], qT[1][:, ch(n)],
                            start=False, stop=True)
        p.mm_pipe(CH, s13_mm,
                  lambda v, n, b: v.tensor_tensor(attnT[:, n, :], pp[b][:, 0:C2], m_iu, ALU.mult))

        def s1_mm(t, n, b):
            t.matmul(pp[b][:, 0:C2], kT[0][:, ch(n)], kbT[0][:, ch(n)], start=True, stop=False)
            t.matmul(pp[b][:, 0:C2], kT[1][:, ch(n)], kbT[1][:, ch(n)], start=False, stop=True)
            t.matmul(pp[b][:, C2:2 * C2], kbT[0][:, ch(n)], kT[0][:, ch(n)], start=True, stop=False)
            return t.matmul(pp[b][:, C2:2 * C2], kbT[1][:, ch(n)], kT[1][:, ch(n)],
                            start=False, stop=True)

        def s1_ev(v, n, b):
            v.tensor_tensor(N_all[:, n, :], pp[b][:, 0:C2], m_su, ALU.mult)
            return v.tensor_tensor(M_all[:, n, :], pp[b][:, C2:2 * C2], m_sl, ALU.mult)
        p.mm_pipe(CH, s1_mm, s1_ev)

        def s6(v):
            return v.tensor_tensor(P_c[:, :, :],
                                   eye_sb[:, :].unsqueeze(1).broadcast_to([C2, NC2, C2]),
                                   N_all[:, :, :], ALU.subtract)
        p.step("vector", s6)

        def sq_stage(nsrc, msrc):
            def mm(t, n, b):
                t.matmul(pp[b][:, 0:C2], msrc(n), nsrc(n), start=True, stop=True)
                return t.matmul(pp[b][:, C2:2 * C2], nsrc(n), msrc(n), start=True, stop=True)

            def ev(v, n, b):
                return v.tensor_copy(NM[:, n, :], pp[b][:, 0:2 * C2])
            p.mm_pipe(CH, mm, ev)

        def prod_stage(msrc):
            def mm(t, n, b):
                t.matmul(pp[b][:, 0:C2], eye_sb[:, :], P_c[:, n, :], start=True, stop=False)
                return t.matmul(pp[b][:, 0:C2], msrc(n), P_c[:, n, :], start=False, stop=True)

            def ev(v, n, b):
                return v.tensor_copy(P_c[:, n, :], pp[b][:, 0:C2])
            p.mm_pipe(CH, mm, ev)

        sq_stage(lambda n: N_all[:, n, :], lambda n: M_all[:, n, :])
        prod_stage(lambda n: NM[:, n, C2:2 * C2])
        sq_stage(lambda n: NM[:, n, 0:C2], lambda n: NM[:, n, C2:2 * C2])
        prod_stage(lambda n: NM[:, n, C2:2 * C2])
        sq_stage(lambda n: NM[:, n, 0:C2], lambda n: NM[:, n, C2:2 * C2])
        prod_stage(lambda n: NM[:, n, C2:2 * C2])
        sq_stage(lambda n: NM[:, n, 0:C2], lambda n: NM[:, n, C2:2 * C2])
        prod_stage(lambda n: NM[:, n, C2:2 * C2])

        def s5_mm(t, n, b):
            return t.matmul(pp[b][:, 0:C2], NM[:, n, 0:C2], NM[:, n, C2:2 * C2],
                            start=True, stop=True)
        p.mm_pipe(CH, s5_mm, lambda v, n, b: v.tensor_copy(M_all[:, n, :], pp[b][:, 0:C2]))
        prod_stage(lambda n: M_all[:, n, :])

        def s11_mm(t, n, b):
            return t.matmul(pp[b][:, 0:256], P_c[:, n, :], vb_cm[:, n, :],
                            start=True, stop=True)
        p.mm_pipe(CH, s11_mm, lambda v, n, b: v.tensor_copy(u_cm[:, n, :], pp[b][:, 0:256]))

        def s12_mm(t, n, b):
            t.matmul(pw[b][:, 0:C2], kb_cm[:, n, 0:128], P_c[:, n, :], start=True, stop=True)
            return t.matmul(pw[b][:, C2:2 * C2], kb_cm[:, n, 128:256], P_c[:, n, :],
                            start=True, stop=True)

        def s12_ev(v, n, b):
            v.tensor_copy(wT[0][:, ch(n)], pw[b][:, 0:C2])
            return v.tensor_copy(wT[1][:, ch(n)], pw[b][:, C2:2 * C2])
        p.mm_pipe(CH, s12_mm, s12_ev)

        if dbg is not None and h == 0:
            p.dma_step("gpsimd", lambda g: [
                g.dma_start(out=dbg["TT"][:, :, :], in_=P_c[:, :, :]),
                g.dma_start(out=dbg["u"][:, :, :], in_=u_cm[:, :, :]),
                g.dma_start(out=dbg["attnT"][:, :, :], in_=attnT[:, :, :]),
                g.dma_start(out=dbg["N"][:, :, :], in_=N_all[:, :, :]),
                g.dma_start(out=dbg["wT0"][:, :], in_=wT[0][:, :]),
                g.dma_start(out=dbg["wT1"][:, :], in_=wT[1][:, :]),
                g.dma_start(out=dbg["kcm"][:, :, :], in_=k_cm[:, :, :]),
                g.dma_start(out=dbg["kbcm"][:, :, :], in_=kb_cm[:, :, :])])

        p.step("vector", lambda v: v.memset(S_sb[:, :, :], 0.0))
        S_pw, S_pq, S_pS = p.sem("s_pw"), p.sem("s_pq"), p.sem("s_pS")
        S_ua, S_do, S_Sc, S_dd = p.sem("s_ua"), p.sem("s_do"), p.sem("s_Sc"), p.sem("s_dd")
        pw0, pq0, pS0 = S_pw.n, S_pq.n, S_pS.n
        ua0, do0, Sc0, dd0 = S_ua.n, S_do.n, S_Sc.n, S_dd.n
        s_pw, s_pq, s_pS = S_pw.h, S_pq.h, S_pS.h
        s_ua, s_do, s_Sc, s_dd = S_ua.h, S_do.h, S_Sc.h, S_dd.h
        start2 = p.cnt

        def scan_pe(t):
            t.wait_ge(p.bar, start2)
            for n in range(NC2):
                t.wait_ge(s_Sc, Sc0 + n)
                t.wait_ge(s_ua, ua0 + n)
                t.matmul(ps_w[:, :], wT[0][:, ch(n)], S_sb[:, 0, :],
                         start=True, stop=False, skip_group_check=True)
                t.matmul(ps_w[:, :], wT[1][:, ch(n)], S_sb[:, 1, :],
                         start=False, stop=True, skip_group_check=True).then_inc(s_pw, 1)
                t.wait_ge(s_do, do0 + n)
                t.matmul(ps_q[:, :], qT[0][:, ch(n)], S_sb[:, 0, :],
                         start=True, stop=False, skip_group_check=True)
                t.matmul(ps_q[:, :], qT[1][:, ch(n)], S_sb[:, 1, :],
                         start=False, stop=False, skip_group_check=True)
                t.wait_ge(s_ua, ua0 + n + 1)
                t.matmul(ps_q[:, :], attnT[:, n, :], u_adj[:, :],
                         start=False, stop=True, skip_group_check=True).then_inc(s_pq, 1)
                t.matmul(ps_S[:, 0:256], k_cm[:, n, 0:128], u_adj[:, :],
                         start=(n == 0), stop=False, skip_group_check=True)
                t.matmul(ps_S[:, 256:512], k_cm[:, n, 128:256], u_adj[:, :],
                         start=False, stop=(n == NC2 - 1),
                         skip_group_check=True).then_inc(s_pS, 1)
            t.nop().then_inc(p.bar, 1)

        def scan_dve(v):
            for n in range(NC2):
                v.wait_ge(s_pw, pw0 + n + 1)
                v.tensor_tensor(u_adj[:, :], u_cm[:, n, :], ps_w[:, :],
                                ALU.subtract).then_inc(s_ua, 1)
                v.wait_ge(s_pq, pq0 + n + 1)
                if n >= 4:
                    v.wait_ge(s_dd, dd0 + 16 * (n - 3))
                v.tensor_copy(d_ring[n % 4][:, :], ps_q[:, :]).then_inc(s_do, 1)
            v.nop().then_inc(p.bar, 1)

        def scan_act(se):
            for n in range(NC2):
                se.wait_ge(s_pS, pS0 + n + 1)
                se.activation(out=S_sb[:, :, :], in_=ps_S[:, :], func=AF.Copy).then_inc(s_Sc, 1)
            se.nop().then_inc(p.bar, 1)

        def scan_g(g):
            for n in range(NC2):
                g.wait_ge(s_do, do0 + n + 1)
                g.dma_start(out=delta_tm[C2 * n:C2 * (n + 1), seg],
                            in_=d_ring[n % 4][:, :]).then_inc(s_dd, 16)
            g.wait_ge(s_dd, dd0 + 16 * NC2)
            g.nop().then_inc(p.bar, 1)
        S_pw.n += NC2; S_pq.n += NC2; S_pS.n += NC2
        S_ua.n += NC2; S_do.n += NC2; S_Sc.n += NC2; S_dd.n += 16 * NC2
        p.block.tensor(scan_pe)
        p.block.vector(scan_dve)
        p.block.scalar(scan_act)
        p.block.gpsimd(scan_g)
        p.cnt += 4

    p.step("gpsimd", lambda g: g.nop())


def build_delta(debug=False):
    nc = bass.Bass("TRN2", target_bir_lowering=False)
    dr = {}
    dr['qn_tm'] = nc.dram_tensor("qn_tm", [T, D], BF16, kind="ExternalInput").ap()
    dr['kn_tm'] = nc.dram_tensor("kn_tm", [T, D], BF16, kind="ExternalInput").ap()
    dr['v_tm'] = nc.dram_tensor("v_tm", [T, D], BF16, kind="ExternalInput").ap()
    dr['beta_tm'] = nc.dram_tensor("beta_tm", [T, H], F32, kind="ExternalInput").ap()
    dr['eye64'] = nc.dram_tensor("eye64", [C2, C2], BF16, kind="ExternalInput").ap()
    dr['masks'] = nc.dram_tensor("masks", [C2, 3 * C2], F32, kind="ExternalInput").ap()
    dr['delta_tm'] = nc.dram_tensor("delta_tm", [T, D], BF16, kind="ExternalOutput").ap()
    if debug:
        dr['dbg'] = {nm: nc.dram_tensor(f"dbg_{nm}", shp, BF16, kind="ExternalOutput").ap()
                     for nm, shp in [("TT", [C2, NC2, C2]), ("u", [C2, NC2, DK]),
                                     ("attnT", [C2, NC2, C2]), ("N", [C2, NC2, C2]),
                                     ("wT0", [128, T]), ("wT1", [128, T]),
                                     ("kcm", [C2, NC2, DK]), ("kbcm", [C2, NC2, DK])]}
    with nc.Block() as block:
        p = P(nc, block)
        p.push_phase("delta")
        emit_delta(nc, p, dr)
        p.pop_phase()
    return nc


def host_l2_inputs(l1_out):
    mask_su = np.triu(np.ones((C2, C2), np.float32), 1)
    mask_sl = mask_su.T.copy()
    mask_iu = np.triu(np.ones((C2, C2), np.float32), 0)
    return {
        "qn_tm": l1_out["qn_tm"], "kn_tm": l1_out["kn_tm"], "v_tm": l1_out["v_tm"],
        "beta_tm": l1_out["beta_tm"],
        "eye64": np.eye(C2, dtype=BF),
        "masks": np.ascontiguousarray(np.concatenate([mask_su, mask_sl, mask_iu], 1)),
    }


def emit_fir(nc, p, dr):
    v_fm = dr["v_fm"]
    fsw = dr["fsw"]
    flw = dr["flw"]
    ls_fm = dr["ls_fm"]
    ll_fm = dr["ll_fm"]
    vt = [p.sb(f"vt{i}", [128, T], BF16) for i in range(ND)]
    dst = [p.sb(f"dst{i}", [128, T], F32) for i in range(ND)]
    obf = p.sb("obf", [128, T], BF16)
    fsw_sb = p.sb("fsw_sb", [128, ND * 5], F32)
    flw_sb = p.sb("flw_sb", [128, ND * 64], F32)

    p.dma_step("gpsimd", lambda g: [
        g.dma_start(out=fsw_sb[:, :], in_=fsw[:, :]),
        g.dma_start(out=flw_sb[:, :], in_=flw[:, :])])
    p.dma_step("sync", lambda e: [
        e.dma_start(out=vt[i][:, :], in_=v_fm[128 * i:128 * (i + 1), :])
        for i in range(ND)])
    for wsb, ntaps, out in ((fsw_sb, 5, ls_fm), (flw_sb, 64, ll_fm)):
        conv_taps_fm(p, vt, dst,
                     lambda ot, j, wsb=wsb, ntaps=ntaps: wsb[:, ntaps * ot + j:ntaps * ot + j + 1],
                     ntaps)
        for i in range(ND):
            p.step("vector", lambda v, i=i: v.tensor_copy(obf[:, :], dst[i][:, :]))
            p.dma_step("scalar", lambda e, i=i, out=out: [
                e.dma_start(out=out[128 * i:128 * (i + 1), :], in_=obf[:, :])])
    p.step("gpsimd", lambda g: g.nop())


def build_fir():
    nc = bass.Bass("TRN2", target_bir_lowering=False)
    dr = {}
    dr['v_fm'] = nc.dram_tensor("v_fm", [D, T], BF16, kind="ExternalInput").ap()
    dr['fsw'] = nc.dram_tensor("fsw", [128, ND * 5], F32, kind="ExternalInput").ap()
    dr['flw'] = nc.dram_tensor("flw", [128, ND * 64], F32, kind="ExternalInput").ap()
    dr['ls_fm'] = nc.dram_tensor("ls_fm", [D, T], BF16, kind="ExternalOutput").ap()
    dr['ll_fm'] = nc.dram_tensor("ll_fm", [D, T], BF16, kind="ExternalOutput").ap()
    with nc.Block() as block:
        p = P(nc, block)
        p.push_phase("fir")
        emit_fir(nc, p, dr)
        p.pop_phase()
    return nc


def host_l3_inputs(inputs, l1_out):
    def lay(w, k):
        return np.ascontiguousarray(
            w.reshape(1024, k).reshape(ND, 128, k).transpose(1, 0, 2).reshape(128, ND * k))
    return {"v_fm": l1_out["v_fm"],
            "fsw": lay(inputs["L0_attn_fir_short_w"], 5),
            "flw": lay(inputs["L0_attn_fir_long_w"], 64)}


def emit_gate(nc, p, dr):
    x_bf = dr["x_bf"]
    v_tm = dr["v_tm"]
    delta_tm = dr["delta_tm"]
    ls_fm = dr["ls_fm"]
    ll_fm = dr["ll_fm"]
    mlp1T = dr["mlp1T"]
    mlp1b = dr["mlp1b"]
    mlp2T = dr["mlp2T"]
    mlp2b = dr["mlp2b"]
    ltemp = dr["ltemp"]
    crl = dr["crl"]
    ident = dr["ident"]
    o_tm = dr["o_tm"]
    GB = 512

    xT = [p.sb(f"xT{i}", [128, T], BF16) for i in range(ND)]
    w1 = [p.sb(f"w1_{i}", [128, D], BF16) for i in range(ND)]
    w1s = p.sb("w1s", [16, D], BF16)
    w2 = [p.sb(f"w2_{i}", [128, H], BF16) for i in range(ND)]
    b1 = p.sb("b1", [128, ND], F32)
    b2 = p.sb("b2", [1, H], BF16)
    onesr = p.sb("onesr", [1, 128], BF16)
    id_sb = p.sb("id_sb", [128, 128], F32)
    ltp = p.sb("ltp", [1, 8], F32)
    crl_sb = p.sb("crl_sb", [1, H], F32)
    scrl_bc = p.sb("scrl_bc", [128, H], F32)
    itmp_bc = p.sb("itmp_bc", [128, 1], F32)
    one1 = p.sb("one1", [1, 1], F32)
    onesf = p.sb("onesf", [1, 128], F32)
    nb = 4
    tmb = {nm: [p.sb(f"tm_{nm}{i}", [128, D], BF16) for i in range(nb)]
           for nm in ("ls", "ll", "dl", "vv")}
    NS = nb * H * 4
    accs = p.sb("accs", [128, NS], F32)
    accq = p.sb("accq", [128, NS], F32)
    acca = p.sb("acca", [128, NS], F32)
    sscr = p.sb("sscr", [128, 256], F32)
    dst_all = p.sb("dst_all", [128, nb * H, 16], F32)
    statsT = [p.sb(f"statsT{h}", [16, T], BF16) for h in range(H)]
    pstat = p.pt("pstat", [16, 1024], F32)
    hps = [p.pt(f"hps{i}", [128, GB], F32) for i in range(4)]
    z4 = [p.sb(f"z4_{i}", [128, GB], F32) for i in range(4)]
    sq4 = [p.sb(f"sq4_{i}", [128, GB], F32) for i in range(4)]
    v24 = [p.sb(f"v24_{i}", [128, GB], F32) for i in range(4)]
    ug4 = v24
    h1b = [p.sb(f"h1b{i}", [128, GB], BF16) for i in range(ND)]
    plog = p.pt("plog", [128, 16], F32)
    lgt = p.sb("lgt", [128, 16], F32)
    p_all = p.sb("p_all", [128, NT * H, H], F32)
    psc = p.sb("psc", [128, 32], F32)
    otile = [p.sb(f"otile{i}", [128, D], F32) for i in range(nb)]
    obf = tmb["ls"]
    eps5 = eps_ap(nc, p, 1e-5, "eps5")

    p.dma_step("gpsimd", lambda g: [
        g.dma_start(out=w1[i][:, :], in_=mlp1T[128 * i:128 * (i + 1), :]) for i in range(ND)] + [
        g.dma_start(out=w1s[:, :], in_=mlp1T[1024:1040, :]),
        g.dma_start(out=b1[:, :], in_=mlp1b[:, :]),
        g.dma_start(out=b2[:, :], in_=mlp2b[:, :]),
        g.dma_start(out=ltp[0:1, 0:1], in_=ltemp[:, :]),
        g.dma_start(out=crl_sb[:, :], in_=crl[:, :]),
        g.dma_start(out=id_sb[:, :], in_=ident[:, :])] + [
        g.dma_start(out=w2[i][:, :], in_=mlp2T[128 * i:128 * (i + 1), :]) for i in range(ND)])
    p.dma_step("sync", lambda e: [
        e.dma_start(out=xT[i][:, :], in_=x_bf[:, 128 * i:128 * (i + 1)], transpose=True)
        for i in range(ND)])

    def initc(v):
        v.memset(onesr[:, :], 1.0)
        v.memset(onesf[:, :], 1.0)
        return v.memset(one1[:, :], 1.0)
    p.step("vector", initc)

    def scl1(se):
        se.activation(out=ltp[0:1, 1:2], in_=ltp[0:1, 0:1], func=AF.Abs)
        return se.activation(out=crl_sb[0:1, :], in_=crl_sb[0:1, :], func=AF.Sigmoid)
    p.step("scalar", scl1)
    p.step("scalar", lambda se: se.activation(out=ltp[0:1, 2:3], in_=ltp[0:1, 1:2],
                                              func=AF.Exp, scale=-1.0))
    p.step("scalar", lambda se: se.activation(out=ltp[0:1, 3:4], in_=ltp[0:1, 2:3],
                                              func=AF.Ln, bias=one1[0:1, 0:1]))
    p.step("vector", lambda v: v.tensor_scalar(ltp[0:1, 4:5], ltp[0:1, 0:1], 0.0, None, ALU.max))
    p.step("vector", lambda v: v.tensor_add(ltp[0:1, 5:6], ltp[0:1, 3:4], ltp[0:1, 4:5]))
    p.step("vector", lambda v: v.reciprocal(ltp[0:1, 6:7], ltp[0:1, 5:6]))

    def bc_mm2(t):
        t.matmul(hps[0][:, 0:1], onesf[:, :], ltp[0:1, 6:7], start=True, stop=True)
        return t.matmul(plog[:, 0:4], onesf[:, :], crl_sb[0:1, :], start=True, stop=True)
    p.step("tensor", bc_mm2)

    def bc_cp(v):
        v.tensor_copy(itmp_bc[:, :], hps[0][:, 0:1])
        return v.tensor_copy(scrl_bc[:, :], plog[:, 0:4])
    p.step("vector", bc_cp)

    TEN = ("ls", "ll", "dl", "vv")
    for b0 in range(0, NT, nb):
        p.dma_step("sync", lambda e, b0=b0: [
            e.dma_start(out=tmb["ls"][i][:, :], in_=ls_fm[:, 128 * (b0 + i):128 * (b0 + i + 1)], transpose=True)
            for i in range(nb)] + [
            e.dma_start(out=tmb["ll"][i][:, :], in_=ll_fm[:, 128 * (b0 + i):128 * (b0 + i + 1)], transpose=True)
            for i in range(nb)] + [
            e.dma_start(out=tmb["dl"][i][:, :], in_=delta_tm[128 * (b0 + i):128 * (b0 + i + 1), :])
            for i in range(nb)] + [
            e.dma_start(out=tmb["vv"][i][:, :], in_=v_tm[128 * (b0 + i):128 * (b0 + i + 1), :])
            for i in range(nb)])

        def slot(i, h, ti):
            return (i * H + h) * 4 + ti

        def acc_all(se, b0=b0):
            last = None
            for i in range(nb):
                for h in range(H):
                    for ti, nm in enumerate(TEN):
                        seg = tmb[nm][i][:, 256 * h:256 * (h + 1)]
                        sl = slot(i, h, ti)
                        se.activation(out=sscr[:, :], in_=seg, func=AF.Copy,
                                      accum_out=accs[:, sl:sl + 1])
                        se.activation(out=sscr[:, :], in_=seg, func=AF.Square,
                                      accum_out=accq[:, sl:sl + 1])
                        last = se.activation(out=sscr[:, :], in_=seg, func=AF.Abs,
                                             accum_out=acca[:, sl:sl + 1])
            return last
        p.step("scalar", acc_all)

        def der1(v):
            v.tensor_scalar(dst_all[:, :, 0].opt() if False else dst_all[:, 0:nb * H, 0:1],
                            accs[:, 0:NS].rearrange("p (s f) -> p s f", f=4)[:, :, 0:1]
                            if False else accs[:, 0:NS], 1.0 / 256, None, ALU.mult)
            return v.nop()
        def der_mean(v):
            return v.tensor_scalar(
                dst_all[:, :, :].rearrange("p k s -> p (k s)")[
                    :, 0:nb * H * 16].rearrange("p (k t s) -> p k t s", t=4, s=4)[:, :, :, 0:1],
                accs[:, 0:NS].rearrange("p (k t) -> p k t", t=4).unsqueeze(3),
                1.0 / 256, None, ALU.mult)
        p.step("vector", der_mean)

        def der_am(v):
            return v.tensor_scalar(
                dst_all[:, :, :].rearrange("p k s -> p (k s)")[
                    :, 0:nb * H * 16].rearrange("p (k t s) -> p k t s", t=4, s=4)[:, :, :, 2:3],
                acca[:, 0:NS].rearrange("p (k t) -> p k t", t=4).unsqueeze(3),
                1.0 / 256, None, ALU.mult)
        p.step("vector", der_am)

        def der_msq(v):
            return v.tensor_scalar(msq[:, 0:NS // 4].unsqueeze(2) if False else msq[:, 0:NS],
                                   accq[:, 0:NS], 1.0 / 256, None, ALU.mult)
        def der_msq2(v):
            return v.tensor_scalar(accq[:, 0:NS], accq[:, 0:NS], 1.0 / 256, None, ALU.mult)
        p.step("vector", der_msq2)

        def der_var(v):
            return v.tensor_tensor(accs[:, 0:NS], accs[:, 0:NS], accs[:, 0:NS], ALU.mult)
        def mk_m2(v):
            dsel = dst_all[:, :, :].rearrange("p k s -> p (k s)")[
                :, 0:nb * H * 16].rearrange("p (k t s) -> p k t s", t=4, s=4)
            return v.tensor_tensor(accs[:, 0:NS].rearrange("p (k t) -> p k t", t=4),
                                   dsel[:, :, :, 0].opt() if False else dsel[:, :, :, 0:1].rearrange("p k t s -> p k (t s)"),
                                   dsel[:, :, :, 0:1].rearrange("p k t s -> p k (t s)"),
                                   ALU.mult)
        p.step("vector", mk_m2)

        def mk_var(v):
            dsel = dst_all[:, :, :].rearrange("p k s -> p (k s)")[
                :, 0:nb * H * 16].rearrange("p (k t s) -> p k t s", t=4, s=4)
            return v.tensor_tensor(dsel[:, :, :, 1:2].rearrange("p k t s -> p k (t s)"),
                                   accq[:, 0:NS].rearrange("p (k t) -> p k t", t=4),
                                   accs[:, 0:NS].rearrange("p (k t) -> p k t", t=4),
                                   ALU.subtract)
        p.step("vector", mk_var)

        def mk_l2(se):
            dsel = dst_all[:, :, :].rearrange("p k s -> p (k s)")[
                :, 0:nb * H * 16].rearrange("p (k t s) -> p k t s", t=4, s=4)
            return se.activation(out=dsel[:, :, :, 3:4].rearrange("p k t s -> p k (t s)"),
                                 in_=accq[:, 0:NS].rearrange("p (k t) -> p k t", t=4),
                                 func=AF.Sqrt, scale=256.0)
        p.step("scalar", mk_l2)

        for rnd in range(2):
            def trs(t, b0=b0, rnd=rnd):
                last = None
                for j in range(8):
                    k = rnd * 8 + j
                    last = t.transpose(pstat[:, 128 * j:128 * (j + 1)],
                                       dst_all[:, k, :], id_sb[:, :])
                return last
            p.step("tensor", trs)

            def evs(v, b0=b0, rnd=rnd):
                last = None
                for j in range(8):
                    k = rnd * 8 + j
                    i, h = divmod(k, H)
                    tt = b0 + i
                    last = v.tensor_copy(statsT[h][:, 128 * tt:128 * (tt + 1)],
                                         pstat[:, 128 * j:128 * (j + 1)])
                return last
            p.step("vector", evs)

    for h in range(H):
        for tb in range(8):
            t0 = GB * tb
            for half in range(2):
                def mm1(t, h=h, t0=t0, half=half):
                    for oi in range(4):
                        ot = 4 * half + oi
                        for k in range(ND):
                            t.matmul(hps[oi][:, :], w1[k][:, 128 * ot:128 * (ot + 1)],
                                     xT[k][:, t0:t0 + GB], start=(k == 0), stop=False)
                        last = t.matmul(hps[oi][:, :], w1s[:, 128 * ot:128 * (ot + 1)],
                                        statsT[h][:, t0:t0 + GB], start=False, stop=True)
                    return last
                p.step("tensor", mm1)

                def g_z(v, half=half):
                    last = None
                    for oi in range(4):
                        ot = 4 * half + oi
                        last = v.tensor_scalar(z4[oi][:, :], hps[oi][:, :],
                                               b1[:, ot:ot + 1], None, ALU.add)
                    return last
                p.step("vector", g_z)

                def g_sq(se):
                    last = None
                    for oi in range(4):
                        last = se.activation(out=sq4[oi][:, :], in_=z4[oi][:, :], func=AF.Square)
                    return last
                p.step("scalar", g_sq)

                def g_w(v):
                    last = None
                    for oi in range(4):
                        last = v.tensor_scalar(sq4[oi][:, :], sq4[oi][:, :], 0.044715, 1.0,
                                               ALU.mult, ALU.add)
                    return last
                p.step("vector", g_w)

                def g_v2(v):
                    last = None
                    for oi in range(4):
                        last = v.tensor_mul(v24[oi][:, :], z4[oi][:, :], sq4[oi][:, :])
                    return last
                p.step("vector", g_v2)

                def g_sig(se):
                    last = None
                    for oi in range(4):
                        last = se.activation(out=ug4[oi][:, :], in_=v24[oi][:, :],
                                             func=AF.Sigmoid, scale=1.5957691216057308)
                    return last
                p.step("scalar", g_sig)

                def g_h1(v, half=half):
                    last = None
                    for oi in range(4):
                        ot = 4 * half + oi
                        last = v.tensor_mul(h1b[ot][:, :], z4[oi][:, :], ug4[oi][:, :])
                    return last
                p.step("vector", g_h1)

            def mm2(t, h=h, t0=t0):
                for g in range(4):
                    for k in range(ND):
                        t.matmul(plog[:, 4 * g:4 * (g + 1)], h1b[k][:, 128 * g:128 * (g + 1)],
                                 w2[k][:, :], start=(k == 0), stop=False)
                    last = t.matmul(plog[:, 4 * g:4 * (g + 1)], onesr[:, :], b2[:, :],
                                    start=False, stop=True)
                return last
            p.step("tensor", mm2)

            p.step("vector", lambda v: v.tensor_copy(lgt[:, :], plog[:, :]))
            p.step("vector", lambda v: v.tensor_reduce(
                psc[:, 0:4], lgt[:, :].rearrange("p (g j) -> p g j", j=4), AX.X, ALU.max))
            p.step("vector", lambda v: v.tensor_scalar(psc[:, 4:8], psc[:, 0:4],
                                                       itmp_bc[:, :], None, ALU.mult))
            p.step("vector", lambda v: v.tensor_scalar(psc[:, 8:12], psc[:, 4:8],
                                                       -1.0, None, ALU.mult))

            def sm_exp(se):
                last = None
                for g in range(4):
                    last = se.activation(out=lgt[:, 4 * g:4 * (g + 1)],
                                         in_=lgt[:, 4 * g:4 * (g + 1)], func=AF.Exp,
                                         scale=itmp_bc[:, :], bias=psc[:, 8 + g:9 + g],
                                         accum_out=psc[:, 12 + g:13 + g])
                return last
            p.step("scalar", sm_exp)
            p.step("vector", lambda v: v.reciprocal(psc[:, 16:20], psc[:, 12:16]))

            def sm_div(v):
                last = None
                for g in range(4):
                    last = v.tensor_scalar(lgt[:, 4 * g:4 * (g + 1)], lgt[:, 4 * g:4 * (g + 1)],
                                           psc[:, 16 + g:17 + g], None, ALU.mult)
                return last
            p.step("vector", sm_div)
            p.step("vector", lambda v: v.tensor_scalar(lgt[:, :], lgt[:, :], 0.02, None, ALU.max))
            p.step("vector", lambda v: v.tensor_reduce(
                psc[:, 20:24], lgt[:, :].rearrange("p (g j) -> p g j", j=4), AX.X, ALU.add))
            p.step("vector", lambda v: v.reciprocal(psc[:, 24:28], psc[:, 20:24]))

            def sm_fin(v, h=h, t0=t0):
                last = None
                for g in range(4):
                    last = v.tensor_scalar(p_all[:, h * NT + (t0 // 128 + g), :],
                                           lgt[:, 4 * g:4 * (g + 1)],
                                           psc[:, 24 + g:25 + g], None, ALU.mult)
                return last
            p.step("vector", sm_fin)

    def p0adj(v):
        last = None
        for h in range(H):
            last = v.tensor_scalar(p_all[:, h * NT:(h + 1) * NT, 0:1],
                                   p_all[:, h * NT:(h + 1) * NT, 0:1],
                                   scrl_bc[:, h:h + 1], None, ALU.add)
        return last
    p.step("vector", p0adj)

    for b0 in range(0, NT, nb):
        p.dma_step("sync", lambda e, b0=b0: [
            e.dma_start(out=tmb["ls"][i][:, :], in_=ls_fm[:, 128 * (b0 + i):128 * (b0 + i + 1)], transpose=True)
            for i in range(nb)] + [
            e.dma_start(out=tmb["ll"][i][:, :], in_=ll_fm[:, 128 * (b0 + i):128 * (b0 + i + 1)], transpose=True)
            for i in range(nb)] + [
            e.dma_start(out=tmb["dl"][i][:, :], in_=delta_tm[128 * (b0 + i):128 * (b0 + i + 1), :])
            for i in range(nb)] + [
            e.dma_start(out=tmb["vv"][i][:, :], in_=v_tm[128 * (b0 + i):128 * (b0 + i + 1), :])
            for i in range(nb)])

        def comb0(v, b0=b0):
            last = None
            for i in range(nb):
                for h in range(H):
                    seg = slice(256 * h, 256 * (h + 1))
                    last = v.tensor_scalar(otile[i][:, seg], tmb["ls"][i][:, seg],
                                           p_all[:, h * NT + b0 + i, 0:1], None, ALU.mult)
            return last
        p.step("vector", comb0)
        for j, nm in ((1, "ll"), (2, "dl"), (3, "vv")):
            def combj(v, b0=b0, j=j, nm=nm):
                last = None
                for i in range(nb):
                    for h in range(H):
                        seg = slice(256 * h, 256 * (h + 1))
                        last = v.scalar_tensor_tensor(otile[i][:, seg], tmb[nm][i][:, seg],
                                                      p_all[:, h * NT + b0 + i, j:j + 1],
                                                      otile[i][:, seg], ALU.mult, ALU.add)
                return last
            p.step("vector", combj)

        def rms_acc(se, b0=b0):
            last = None
            for i in range(nb):
                for h in range(H):
                    last = se.activation(out=sscr[:, :], in_=otile[i][:, 256 * h:256 * (h + 1)],
                                         func=AF.Square, accum_out=psc[:, i * H + h:i * H + h + 1])
            return last
        p.step("scalar", rms_acc)
        p.step("scalar", lambda se: se.activation(out=psc[:, 16:16 + nb * H], in_=psc[:, 0:nb * H],
                                                  func=AF.Sqrt, scale=1.0 / 256, bias=eps5))
        p.step("vector", lambda v: v.reciprocal(psc[:, 0:nb * H], psc[:, 16:16 + nb * H]))

        def rms_scale(v, b0=b0):
            last = None
            for i in range(nb):
                for h in range(H):
                    last = v.tensor_scalar(obf[i][:, 256 * h:256 * (h + 1)],
                                           otile[i][:, 256 * h:256 * (h + 1)],
                                           psc[:, i * H + h:i * H + h + 1], None, ALU.mult)
            return last
        p.step("vector", rms_scale)
        p.dma_step("scalar", lambda e, b0=b0: [
            e.dma_start(out=o_tm[128 * (b0 + i):128 * (b0 + i + 1), :], in_=obf[i][:, :])
            for i in range(nb)])
    p.step("gpsimd", lambda g: g.nop())


def build_gate():
    nc = bass.Bass("TRN2", target_bir_lowering=False)
    dr = {}
    dr['x_bf'] = nc.dram_tensor("x_bf", [T, D], BF16, kind="ExternalInput").ap()
    dr['v_tm'] = nc.dram_tensor("v_tm", [T, D], BF16, kind="ExternalInput").ap()
    dr['delta_tm'] = nc.dram_tensor("delta_tm", [T, D], BF16, kind="ExternalInput").ap()
    dr['ls_fm'] = nc.dram_tensor("ls_fm", [D, T], BF16, kind="ExternalInput").ap()
    dr['ll_fm'] = nc.dram_tensor("ll_fm", [D, T], BF16, kind="ExternalInput").ap()
    dr['mlp1T'] = nc.dram_tensor("mlp1T", [1040, D], F32, kind="ExternalInput").ap()
    dr['mlp1b'] = nc.dram_tensor("mlp1b", [128, ND], F32, kind="ExternalInput").ap()
    dr['mlp2T'] = nc.dram_tensor("mlp2T", [D, H], F32, kind="ExternalInput").ap()
    dr['mlp2b'] = nc.dram_tensor("mlp2b", [1, H], F32, kind="ExternalInput").ap()
    dr['ltemp'] = nc.dram_tensor("ltemp", [1, 1], F32, kind="ExternalInput").ap()
    dr['crl'] = nc.dram_tensor("crl", [1, H], F32, kind="ExternalInput").ap()
    dr['ident'] = nc.dram_tensor("ident", [128, 128], F32, kind="ExternalInput").ap()
    dr['o_tm'] = nc.dram_tensor("o_tm", [T, D], BF16, kind="ExternalOutput").ap()
    with nc.Block() as block:
        p = P(nc, block)
        p.push_phase("gate")
        emit_gate(nc, p, dr)
        p.pop_phase()
    return nc


def eps_ap(nc, p, val=1e-5, name="eps5"):
    if not hasattr(p, "_eps") or p._eps is None:
        p._eps = {}
    if name not in p._eps:
        t = p.sb(name, [128, 1], F32)
        p.step("vector", lambda v: v.memset(t[:, :], val))
        p._eps[name] = t
    return p._eps[name][:, :]


def host_l4a_inputs(inputs, l1_out, l2_out, l3_out):
    mlp1 = inputs["L0_attn_mlp1_w"]
    return {
        "x_bf": l1_out["x_bf"], "v_tm": l1_out["v_tm"], "delta_tm": l2_out["delta_tm"],
        "ls_fm": l3_out["ls_fm"], "ll_fm": l3_out["ll_fm"],
        "mlp1T": np.ascontiguousarray(mlp1.T),
        "mlp1b": np.ascontiguousarray(inputs["L0_attn_mlp1_b"].reshape(ND, 128).T),
        "mlp2T": np.ascontiguousarray(inputs["L0_attn_mlp2_w"].T),
        "mlp2b": np.ascontiguousarray(inputs["L0_attn_mlp2_b"].reshape(1, H)),
        "ltemp": np.ascontiguousarray(inputs["L0_attn_logit_temp"].reshape(1, 1)),
        "crl": np.ascontiguousarray(inputs["L0_attn_conv_res_logit"].reshape(1, H)),
        "ident": np.eye(128, dtype=np.float32),
    }


def ln_steps(p, nc, bufs, xa, out_f32, out_bf=None, width=D):
    stats6, mv, srt, rstd = bufs
    eps5 = p._eps["eps5"][:, :]
    nh = width // 512

    def st(v):
        last = None
        for i in range(nh):
            last = v.bn_stats(stats6[:, 6 * i:6 * (i + 1)], xa[:, 512 * i:512 * (i + 1)])
        return last
    p.step("vector", st)
    p.step("vector", lambda v: v.bn_aggr(mv[:, :], stats6[:, 0:6 * nh]))
    p.step("scalar", lambda se: se.activation(out=srt[:, :], in_=mv[:, 1:2],
                                              func=AF.Sqrt, bias=eps5))
    p.step("vector", lambda v: v.reciprocal(rstd[:, :], srt[:, :]))
    p.step("vector", lambda v: v.tensor_scalar(out_f32[:, :], xa[:, :], mv[:, 0:1],
                                               rstd[:, :], ALU.subtract, ALU.mult))
    if out_bf is not None:
        p.step("vector", lambda v: v.tensor_copy(out_bf[:, :], out_f32[:, :]))


def scale_steps(p, nc, amax8_sb, scr, qmax=32767.0):
    s1, onesf, psb = scr["s1"], scr["onesf"], scr["psb"]
    s_bc, inv_bc = scr["s_bc"], scr["inv_bc"]
    p.step("vector", lambda v: v.tensor_reduce(s1[0:1, 0:1], amax8_sb[:, :], AX.X, ALU.max))
    p.step("vector", lambda v: v.tensor_scalar(s1[0:1, 1:2], s1[0:1, 0:1], 1.0 / qmax, 1e-12,
                                               ALU.mult, ALU.max))
    p.step("vector", lambda v: v.reciprocal(s1[0:1, 2:3], s1[0:1, 1:2]))

    def bc(t):
        t.matmul(psb[:, 0:1], onesf[:, :], s1[0:1, 1:2], start=True, stop=True)
        return t.matmul(psb[:, 1:2], onesf[:, :], s1[0:1, 2:3], start=True, stop=True)
    p.step("tensor", bc)
    p.step("vector", lambda v: v.tensor_copy(s_bc[:, :], psb[:, 0:1]))
    p.step("vector", lambda v: v.tensor_copy(inv_bc[:, :], psb[:, 1:2]))


def amax_final_steps(p, nc, slots, width, scr, out_dram):
    a1, id_sb, pts = scr["s1"], scr["id_f32"], scr["pts"]
    p.step("vector", lambda v: v.tensor_reduce(a1[:, 3:4], slots[:, 0:width], AX.X, ALU.max))
    p.step("tensor", lambda t: t.transpose(pts[:, :], a1[:, 3:4], id_sb[:, :]))
    p.step("vector", lambda v: v.tensor_reduce(a1[0:1, 4:5], pts[0:1, :], AX.X, ALU.max))
    p.dma_step("scalar", lambda e: [e.dma_start(out=out_dram[:, :], in_=a1[0:1, 4:5])])


def emit_ow(nc, p, dr):
    x = dr["x"]
    o_tm = dr["o_tm"]
    owT = dr["owT"]
    nw_bc = dr["nw_bc"]
    ident = dr["ident"]
    x1 = dr["x1"]
    x1_bf = dr["x1_bf"]
    amax = dr["amax"]
    oT = [p.sb(f"oT{i}", [128, T], BF16) for i in range(ND)]
    wf = p.sb("wf", [128, D], F32)
    wb = [p.sb(f"wb{i}", [128, D], BF16) for i in range(ND)]
    pss = [p.pt(f"pss{i}", [128, 512], F32) for i in range(2)]
    pts = p.pt("pts", [1, 128], F32)
    xt = p.sb("xt", [128, D], F32)
    xa = p.sb("xa", [128, D], F32)
    x1t = p.sb("x1t", [128, D], F32)
    x1b = p.sb("x1b", [128, D], BF16)
    stats6 = p.sb("stats6", [128, 12], F32)
    mv = p.sb("mv", [128, 2], F32)
    srt = p.sb("srt", [128, 1], F32)
    rstd = p.sb("rstd", [128, 1], F32)
    slots = p.sb("slots", [128, NT], F32)
    s1 = p.sb("s1", [128, 8], F32)
    id_sb = p.sb("id_sb", [128, 128], F32)
    eps_ap(nc, p, 1e-5, "eps5")

    p.dma_step("sync", lambda e: [
        e.dma_start(out=oT[i][:, :], in_=o_tm[:, 128 * i:128 * (i + 1)], transpose=True)
        for i in range(ND)] + [e.dma_start(out=id_sb[:, :], in_=ident[:, :])])
    nw_sb = p.sb("nw_sb", [128, ND], F32)
    p.dma_step("gpsimd", lambda g: [g.dma_start(out=nw_sb[:, :], in_=nw_bc[:, :])])
    for i in range(ND):
        p.dma_step("gpsimd", lambda g, i=i: [
            g.dma_start(out=wf[:, :], in_=owT[128 * i:128 * (i + 1), :])])
        p.step("vector", lambda v, i=i: v.tensor_scalar(wb[i][:, :], wf[:, :],
                                                        nw_sb[:, i:i + 1], None, ALU.mult))

    lnb = (stats6, mv, srt, rstd)
    for tt in range(NT):
        p.dma_step("sync", lambda e, tt=tt: [
            e.dma_start(out=xt[:, :], in_=x[128 * tt:128 * (tt + 1), :])])

        def mm(t, tt=tt):
            for half in range(2):
                for k in range(ND):
                    last = t.matmul(pss[half][:, :], oT[k][:, 128 * tt:128 * (tt + 1)],
                                    wb[k][:, 512 * half:512 * (half + 1)],
                                    start=(k == 0), stop=(k == ND - 1))
            return last
        p.step("tensor", mm)

        def ev(v, tt=tt):
            v.tensor_add(xa[:, 0:512], pss[0][:, :], xt[:, 0:512])
            return v.tensor_add(xa[:, 512:1024], pss[1][:, :], xt[:, 512:1024])
        p.step("vector", ev)
        ln_steps(p, nc, lnb, xa, x1t, x1b)
        p.step("vector", lambda v, tt=tt: v.tensor_reduce(slots[:, tt:tt + 1], x1t[:, :],
                                                          AX.X, ALU.max,
                                                          apply_absolute_value=True))
        p.dma_step("scalar", lambda e, tt=tt: [
            e.dma_start(out=x1[128 * tt:128 * (tt + 1), :], in_=x1t[:, :]),
            e.dma_start(out=x1_bf[128 * tt:128 * (tt + 1), :], in_=x1b[:, :])])
    amax_final_steps(p, nc, slots, NT, {"s1": s1, "id_f32": id_sb, "pts": pts}, amax)
    p.step("gpsimd", lambda g: g.nop())


def build_ow():
    nc = bass.Bass("TRN2", target_bir_lowering=False)
    dr = {}
    dr['x'] = nc.dram_tensor("x", [T, D], F32, kind="ExternalInput").ap()
    dr['o_tm'] = nc.dram_tensor("o_tm", [T, D], BF16, kind="ExternalInput").ap()
    dr['owT'] = nc.dram_tensor("owT", [D, D], F32, kind="ExternalInput").ap()
    dr['nw_bc'] = nc.dram_tensor("nw_bc", [128, ND], F32, kind="ExternalInput").ap()
    dr['ident'] = nc.dram_tensor("ident", [128, 128], F32, kind="ExternalInput").ap()
    dr['x1'] = nc.dram_tensor("x1", [T, D], F32, kind="ExternalOutput").ap()
    dr['x1_bf'] = nc.dram_tensor("x1_bf", [T, D], BF16, kind="ExternalOutput").ap()
    dr['amax'] = nc.dram_tensor("amax", [1, 1], F32, kind="ExternalOutput").ap()
    with nc.Block() as block:
        p = P(nc, block)
        p.push_phase("ow")
        emit_ow(nc, p, dr)
        p.pop_phase()
    return nc


def host_l4b_inputs(inputs, l4a_out, core):
    nw = inputs["L0_attn_norm_w"]
    nw_full = np.tile(nw, D // 256)
    return {
        "x": np.ascontiguousarray(inputs["L0_attn_hidden_states"][core]).astype(np.float32, copy=False),
        "o_tm": l4a_out["o_tm"],
        "owT": np.ascontiguousarray(inputs["L0_attn_o_w"].T),
        "nw_bc": np.ascontiguousarray(nw_full.reshape(ND, 128).T),
        "ident": np.eye(128, dtype=np.float32),
    }


def _dequant_steps(p, nc, w_dram, rows, I, bufs, out_dram):
    wf, wi, wbf, bmax, sw, inv = bufs
    nblk = I // 32
    for r in range(rows // 128):
        p.dma_step("gpsimd", lambda g, r=r: [
            g.dma_start(out=wf[:, 0:I], in_=w_dram[128 * r:128 * (r + 1), :])])
        p.step("vector", lambda v: v.tensor_reduce(
            bmax[:, 0:nblk], wf[:, 0:I].rearrange("p (b k) -> p b k", k=32), AX.X, ALU.max,
            apply_absolute_value=True))
        p.step("vector", lambda v: v.tensor_scalar(sw[:, 0:nblk], bmax[:, 0:nblk],
                                                   1.0 / 7, 1e-12, ALU.mult, ALU.max))
        p.step("vector", lambda v: v.reciprocal(inv[:, 0:nblk], sw[:, 0:nblk]))
        p.step("vector", lambda v: v.tensor_tensor(
            wi[:, 0:I].rearrange("p (b k) -> p b k", k=32),
            wf[:, 0:I].rearrange("p (b k) -> p b k", k=32),
            inv[:, 0:nblk].unsqueeze(2).broadcast_to([128, nblk, 32]), ALU.mult))
        p.step("vector", lambda v: v.tensor_tensor(
            wbf[:, 0:I].rearrange("p (b k) -> p b k", k=32),
            wi[:, 0:I].rearrange("p (b k) -> p b k", k=32),
            sw[:, 0:nblk].unsqueeze(2).broadcast_to([128, nblk, 32]), ALU.mult))
        p.dma_step("scalar", lambda e, r=r: [
            e.dma_start(out=out_dram[128 * r:128 * (r + 1), :], in_=wbf[:, 0:I])])


def emit_ffn_up(nc, p, dr):
    x1_bf = dr["x1_bf"]
    w_up = dr["w_up"]
    w_gate = dr["w_gate"]
    w_down = dr["w_down"]
    amax8 = dr["amax8"]
    ident = dr["ident"]
    up_st = dr["up_st"]
    gate_st = dr["gate_st"]
    wdq_down = dr["wdq_down"]
    amax_up = dr["amax_up"]
    amax_gate = dr["amax_gate"]
    wdq_up = nc.dram_tensor("wdq_up", [DFF, D], BF16)
    wdq_gate = nc.dram_tensor("wdq_gate", [DFF, D], BF16)
    xqT = [p.sb(f"xqT{i}", [128, T], BF16) for i in range(ND)]
    wT = [p.sb(f"wT{i}", [128, DFF], BF16) for i in range(ND)]
    wf = p.sb("wf", [128, DFF], F32)
    wi = p.sb("wi", [128, DFF], I32)
    wbf = p.sb("wbf", [128, DFF], BF16)
    bmax = p.sb("bmax", [128, 96], F32)
    sw = p.sb("sw", [128, 96], F32)
    invw = p.sb("invw", [128, 96], F32)
    xb = p.sb("xb", [128, T], BF16)
    xi = p.sb("xi", [128, T], I32)
    ev_f = p.sb("ev_f", [128, 512], F32)
    ob = p.sb("ob", [128, T], BF16)
    slots = p.sb("slots", [128, 192], F32)
    slots2 = p.sb("slots2", [128, 192], F32)
    s1 = p.sb("s1", [128, 8], F32)
    onesf = p.sb("onesf", [1, 128], F32)
    psb = p.pt("psb", [128, 2], F32)
    pts = p.pt("pts", [1, 128], F32)
    pss = [p.pt(f"pss{i}", [128, 512], F32) for i in range(2)]
    s_bc = p.sb("s_bc", [128, 1], F32)
    inv_bc = p.sb("inv_bc", [128, 1], F32)
    am8 = p.sb("am8", [1, 8], F32)
    id_sb = p.sb("id_sb", [128, 128], F32)

    p.dma_step("gpsimd", lambda g: [
        g.dma_start(out=am8[:, :], in_=amax8[:, :]),
        g.dma_start(out=id_sb[:, :], in_=ident[:, :])])
    p.step("vector", lambda v: v.memset(onesf[:, :], 1.0))
    scr = {"s1": s1, "onesf": onesf, "psb": psb, "s_bc": s_bc, "inv_bc": inv_bc,
           "id_f32": id_sb, "pts": pts}
    scale_steps(p, nc, am8[0:1, 0:dr.get("am8_n", 8)], scr)

    for i in range(ND):
        p.dma_step("sync", lambda e, i=i: [
            e.dma_start(out=xb[:, :], in_=x1_bf[:, 128 * i:128 * (i + 1)], transpose=True)])
        p.step("vector", lambda v: v.tensor_scalar(xi[:, :], xb[:, :], inv_bc[:, :],
                                                   None, ALU.mult))
        p.step("vector", lambda v, i=i: v.tensor_scalar(xqT[i][:, :], xi[:, :], s_bc[:, :],
                                                        None, ALU.mult))

    dq_bufs = (wf, wi, wbf, bmax, sw, invw)
    _dequant_steps(p, nc, w_up, DFF, D, dq_bufs, wdq_up)
    _dequant_steps(p, nc, w_gate, DFF, D, dq_bufs, wdq_gate)
    _dequant_steps(p, nc, w_down, D, DFF, dq_bufs, wdq_down)

    for wdq, st_out, slot, am_out in ((wdq_up, up_st, slots, amax_up),
                                      (wdq_gate, gate_st, slots2, amax_gate)):
        p.dma_step("sync", lambda e, wdq=wdq: [
            e.dma_start(out=wT[i][:, :], in_=wdq[:, 128 * i:128 * (i + 1)], transpose=True)
            for i in range(ND)])
        for ot in range(NF):
            def mms(t, tb, bank, ot=ot):
                for k in range(ND):
                    last = t.matmul(pss[bank][:, :], wT[k][:, 128 * ot:128 * (ot + 1)],
                                    xqT[k][:, 512 * tb:512 * (tb + 1)],
                                    start=(k == 0), stop=(k == ND - 1))
                return last

            def ev(v, tb, bank, ot=ot, slot=slot):
                v.tensor_reduce(slot[:, 8 * ot + tb:8 * ot + tb + 1], pss[bank][:, :],
                                AX.X, ALU.max, apply_absolute_value=True)
                return v.tensor_copy(ob[:, 512 * tb:512 * (tb + 1)], pss[bank][:, :])
            p.mm_pipe(list(range(8)), mms, ev)
            p.dma_step("scalar", lambda e, ot=ot, st_out=st_out: [
                e.dma_start(out=st_out[128 * ot:128 * (ot + 1), :], in_=ob[:, :])])
        amax_final_steps(p, nc, slot, 192, scr, am_out)
    p.step("gpsimd", lambda g: g.nop())


def build_ffn_up():
    nc = bass.Bass("TRN2", target_bir_lowering=False)
    dr = {}
    dr['x1_bf'] = nc.dram_tensor("x1_bf", [T, D], BF16, kind="ExternalInput").ap()
    dr['w_up'] = nc.dram_tensor("w_up", [DFF, D], F32, kind="ExternalInput").ap()
    dr['w_gate'] = nc.dram_tensor("w_gate", [DFF, D], F32, kind="ExternalInput").ap()
    dr['w_down'] = nc.dram_tensor("w_down", [D, DFF], F32, kind="ExternalInput").ap()
    dr['amax8'] = nc.dram_tensor("amax8", [1, 8], F32, kind="ExternalInput").ap()
    dr['ident'] = nc.dram_tensor("ident", [128, 128], F32, kind="ExternalInput").ap()
    dr['up_st'] = nc.dram_tensor("up_st", [DFF, T], BF16, kind="ExternalOutput").ap()
    dr['gate_st'] = nc.dram_tensor("gate_st", [DFF, T], BF16, kind="ExternalOutput").ap()
    dr['wdq_down'] = nc.dram_tensor("wdq_down", [D, DFF], BF16, kind="ExternalOutput").ap()
    dr['amax_up'] = nc.dram_tensor("amax_up", [1, 1], F32, kind="ExternalOutput").ap()
    dr['amax_gate'] = nc.dram_tensor("amax_gate", [1, 1], F32, kind="ExternalOutput").ap()
    with nc.Block() as block:
        p = P(nc, block)
        p.push_phase("ffn_up")
        emit_ffn_up(nc, p, dr)
        p.pop_phase()
    return nc


def _amax8(outs_all, key):
    vals = [np.asarray(outs_all[c][key]).reshape(-1)[0] for c in range(len(outs_all))]
    vals = vals + [np.float32(0.0)] * (8 - len(vals))
    return np.ascontiguousarray(np.asarray(vals, np.float32).reshape(1, 8))


def host_l5_inputs(inputs, l4b_outs_all, core):
    return {
        "x1_bf": l4b_outs_all[core]["x1_bf"],
        "w_up": np.ascontiguousarray(inputs["L0_ffn_w_up"]),
        "w_gate": np.ascontiguousarray(inputs["L0_ffn_w_gate"]),
        "w_down": np.ascontiguousarray(inputs["L0_ffn_w_down"]),
        "amax8": _amax8(l4b_outs_all, "amax"),
        "ident": np.eye(128, dtype=np.float32),
    }


def emit_qchain_a(nc, p, dr):
    up_st = dr["up_st"]
    gate_st = dr["gate_st"]
    am8u = dr["am8u"]
    am8g = dr["am8g"]
    ident = dr["ident"]
    upq_st = dr["upq_st"]
    g_st = dr["g_st"]
    amax_g = dr["amax_g"]
    tb = p.sb("tb", [128, T], BF16)
    ti = p.sb("ti", [128, T], I32)
    gq = p.sb("gq", [128, T], F32)
    sg = p.sb("sg", [128, T], F32)
    ob = p.sb("ob", [128, T], BF16)
    slots = p.sb("slots", [128, NF], F32)
    s1 = p.sb("s1", [128, 8], F32)
    s1g = p.sb("s1g", [128, 8], F32)
    onesf = p.sb("onesf", [1, 128], F32)
    psb = p.pt("psb", [128, 2], F32)
    pts = p.pt("pts", [1, 128], F32)
    su_bc = p.sb("su_bc", [128, 1], F32)
    iu_bc = p.sb("iu_bc", [128, 1], F32)
    sg_bc = p.sb("sg_bc", [128, 1], F32)
    ig_bc = p.sb("ig_bc", [128, 1], F32)
    am8u_sb = p.sb("am8u_sb", [1, 8], F32)
    am8g_sb = p.sb("am8g_sb", [1, 8], F32)
    id_sb = p.sb("id_sb", [128, 128], F32)

    p.dma_step("gpsimd", lambda g: [
        g.dma_start(out=am8u_sb[:, :], in_=am8u[:, :]),
        g.dma_start(out=am8g_sb[:, :], in_=am8g[:, :]),
        g.dma_start(out=id_sb[:, :], in_=ident[:, :])])
    p.step("vector", lambda v: v.memset(onesf[:, :], 1.0))
    scale_steps(p, nc, am8u_sb[0:1, 0:dr.get("am8_n", 8)], {"s1": s1, "onesf": onesf, "psb": psb,
                                 "s_bc": su_bc, "inv_bc": iu_bc})
    scale_steps(p, nc, am8g_sb[0:1, 0:dr.get("am8_n", 8)], {"s1": s1g, "onesf": onesf, "psb": psb,
                                 "s_bc": sg_bc, "inv_bc": ig_bc})
    for r in range(NF):
        p.dma_step("sync", lambda e, r=r: [
            e.dma_start(out=tb[:, :], in_=up_st[128 * r:128 * (r + 1), :])])
        p.step("vector", lambda v: v.tensor_scalar(ti[:, :], tb[:, :], iu_bc[:, :], None, ALU.mult))
        p.step("vector", lambda v: v.tensor_scalar(ob[:, :], ti[:, :], su_bc[:, :], None, ALU.mult))
        p.dma_step("scalar", lambda e, r=r: [
            e.dma_start(out=upq_st[128 * r:128 * (r + 1), :], in_=ob[:, :])])
        p.dma_step("sync", lambda e, r=r: [
            e.dma_start(out=tb[:, :], in_=gate_st[128 * r:128 * (r + 1), :])])
        p.step("vector", lambda v: v.tensor_scalar(ti[:, :], tb[:, :], ig_bc[:, :], None, ALU.mult))
        p.step("vector", lambda v: v.tensor_scalar(gq[:, :], ti[:, :], sg_bc[:, :], None, ALU.mult))
        p.step("scalar", lambda se: se.activation(out=sg[:, :], in_=gq[:, :], func=AF.Sigmoid))
        p.step("vector", lambda v: v.tensor_scalar(ti[:, :], sg[:, :], 65536.0, None, ALU.mult))
        p.step("vector", lambda v: v.tensor_scalar(sg[:, :], ti[:, :], 65535.0, 1.0 / 65536,
                                                   ALU.min, ALU.mult))
        p.step("vector", lambda v: v.tensor_mul(ob[:, :], gq[:, :], sg[:, :]))
        p.step("vector", lambda v, r=r: v.tensor_reduce(slots[:, r:r + 1], ob[:, :],
                                                        AX.X, ALU.max, apply_absolute_value=True))
        p.dma_step("scalar", lambda e, r=r: [
            e.dma_start(out=g_st[128 * r:128 * (r + 1), :], in_=ob[:, :])])
    amax_final_steps(p, nc, slots, NF, {"s1": s1, "id_f32": id_sb, "pts": pts}, amax_g)
    p.step("gpsimd", lambda g: g.nop())


def build_qchain_a():
    nc = bass.Bass("TRN2", target_bir_lowering=False)
    dr = {}
    dr['up_st'] = nc.dram_tensor("up_st", [DFF, T], BF16, kind="ExternalInput").ap()
    dr['gate_st'] = nc.dram_tensor("gate_st", [DFF, T], BF16, kind="ExternalInput").ap()
    dr['am8u'] = nc.dram_tensor("am8u", [1, 8], F32, kind="ExternalInput").ap()
    dr['am8g'] = nc.dram_tensor("am8g", [1, 8], F32, kind="ExternalInput").ap()
    dr['ident'] = nc.dram_tensor("ident", [128, 128], F32, kind="ExternalInput").ap()
    dr['upq_st'] = nc.dram_tensor("upq_st", [DFF, T], BF16, kind="ExternalOutput").ap()
    dr['g_st'] = nc.dram_tensor("g_st", [DFF, T], BF16, kind="ExternalOutput").ap()
    dr['amax_g'] = nc.dram_tensor("amax_g", [1, 1], F32, kind="ExternalOutput").ap()
    with nc.Block() as block:
        p = P(nc, block)
        p.push_phase("qchain_a")
        emit_qchain_a(nc, p, dr)
        p.pop_phase()
    return nc


def emit_qchain_b(nc, p, dr):
    g_st = dr["g_st"]
    upq_st = dr["upq_st"]
    am8 = dr["am8"]
    ident = dr["ident"]
    opre_st = dr["opre_st"]
    amax_o = dr["amax_o"]
    tb = p.sb("tb", [128, T], BF16)
    ub = p.sb("ub", [128, T], BF16)
    ti = p.sb("ti", [128, T], I32)
    g2 = p.sb("g2", [128, T], F32)
    ob = p.sb("ob", [128, T], BF16)
    slots = p.sb("slots", [128, NF], F32)
    s1 = p.sb("s1", [128, 8], F32)
    onesf = p.sb("onesf", [1, 128], F32)
    psb = p.pt("psb", [128, 2], F32)
    pts = p.pt("pts", [1, 128], F32)
    s_bc = p.sb("s_bc", [128, 1], F32)
    inv_bc = p.sb("inv_bc", [128, 1], F32)
    am8_sb = p.sb("am8_sb", [1, 8], F32)
    id_sb = p.sb("id_sb", [128, 128], F32)

    p.dma_step("gpsimd", lambda g: [
        g.dma_start(out=am8_sb[:, :], in_=am8[:, :]),
        g.dma_start(out=id_sb[:, :], in_=ident[:, :])])
    p.step("vector", lambda v: v.memset(onesf[:, :], 1.0))
    scale_steps(p, nc, am8_sb[0:1, 0:dr.get("am8_n", 8)], {"s1": s1, "onesf": onesf, "psb": psb,
                                "s_bc": s_bc, "inv_bc": inv_bc})
    for r in range(NF):
        p.dma_step("sync", lambda e, r=r: [
            e.dma_start(out=tb[:, :], in_=g_st[128 * r:128 * (r + 1), :]),
            e.dma_start(out=ub[:, :], in_=upq_st[128 * r:128 * (r + 1), :])])
        p.step("vector", lambda v: v.tensor_scalar(ti[:, :], tb[:, :], inv_bc[:, :], None, ALU.mult))
        p.step("vector", lambda v: v.tensor_scalar(g2[:, :], ti[:, :], s_bc[:, :], None, ALU.mult))
        p.step("vector", lambda v: v.tensor_mul(ob[:, :], g2[:, :], ub[:, :]))
        p.step("vector", lambda v, r=r: v.tensor_reduce(slots[:, r:r + 1], ob[:, :],
                                                        AX.X, ALU.max, apply_absolute_value=True))
        p.dma_step("scalar", lambda e, r=r: [
            e.dma_start(out=opre_st[128 * r:128 * (r + 1), :], in_=ob[:, :])])
    amax_final_steps(p, nc, slots, NF, {"s1": s1, "id_f32": id_sb, "pts": pts}, amax_o)
    p.step("gpsimd", lambda g: g.nop())


def build_qchain_b():
    nc = bass.Bass("TRN2", target_bir_lowering=False)
    dr = {}
    dr['g_st'] = nc.dram_tensor("g_st", [DFF, T], BF16, kind="ExternalInput").ap()
    dr['upq_st'] = nc.dram_tensor("upq_st", [DFF, T], BF16, kind="ExternalInput").ap()
    dr['am8'] = nc.dram_tensor("am8", [1, 8], F32, kind="ExternalInput").ap()
    dr['ident'] = nc.dram_tensor("ident", [128, 128], F32, kind="ExternalInput").ap()
    dr['opre_st'] = nc.dram_tensor("opre_st", [DFF, T], BF16, kind="ExternalOutput").ap()
    dr['amax_o'] = nc.dram_tensor("amax_o", [1, 1], F32, kind="ExternalOutput").ap()
    with nc.Block() as block:
        p = P(nc, block)
        p.push_phase("qchain_b")
        emit_qchain_b(nc, p, dr)
        p.pop_phase()
    return nc


def emit_qchain_c(nc, p, dr):
    opre_st = dr["opre_st"]
    wdq_down = dr["wdq_down"]
    x1 = dr["x1"]
    am8 = dr["am8"]
    x2 = dr["x2"]
    x2_bf = dr["x2_bf"]
    scaled_bf = dr["scaled_bf"]
    norm_st = dr["norm_st"]
    wdT = [p.sb(f"wdT{i}", [128, D], BF16) for i in range(NF)]
    oslc = p.sb("oslc", [128, NF, 128], BF16)
    osli = p.sb("osli", [128, NF, 128], I32)
    oslq = p.sb("oslq", [128, NF, 128], BF16)
    xt = p.sb("xt", [128, D], F32)
    xa = p.sb("xa", [128, D], F32)
    x2t = p.sb("x2t", [128, D], F32)
    x2b = p.sb("x2b", [128, D], BF16)
    scb = p.sb("scb", [128, D], BF16)
    stats6 = p.sb("stats6", [128, 12], F32)
    mv = p.sb("mv", [128, 2], F32)
    srt = p.sb("srt", [128, 1], F32)
    rstd = p.sb("rstd", [128, 1], F32)
    nrm = p.sb("nrm", [128, NT], F32)
    sqs = p.sb("sqs", [128, 1], F32)
    s1 = p.sb("s1", [128, 8], F32)
    onesf = p.sb("onesf", [1, 128], F32)
    psb = p.pt("psb", [128, 2], F32)
    pss = [p.pt(f"pss{i}", [128, 512], F32) for i in range(2)]
    s_bc = p.sb("s_bc", [128, 1], F32)
    inv_bc = p.sb("inv_bc", [128, 1], F32)
    am8_sb = p.sb("am8_sb", [1, 8], F32)
    sscr = p.sb("sscr", [128, D], F32)
    eps_ap(nc, p, 1e-5, "eps5")

    p.dma_step("gpsimd", lambda g: [g.dma_start(out=am8_sb[:, :], in_=am8[:, :])])
    p.step("vector", lambda v: v.memset(onesf[:, :], 1.0))
    scale_steps(p, nc, am8_sb[0:1, 0:dr.get("am8_n", 8)], {"s1": s1, "onesf": onesf, "psb": psb,
                                "s_bc": s_bc, "inv_bc": inv_bc})
    p.dma_step("sync", lambda e: [
        e.dma_start(out=wdT[i][:, :], in_=wdq_down[:, 128 * i:128 * (i + 1)], transpose=True)
        for i in range(NF)])
    lnb = (stats6, mv, srt, rstd)
    for tt in range(NT):
        p.dma_step("sync", lambda e, tt=tt: [
            e.dma_start(out=oslc[:, :, :],
                        in_=opre_st[:, 128 * tt:128 * (tt + 1)].rearrange("(a p) c -> p a c", p=128)),
            e.dma_start(out=xt[:, :], in_=x1[128 * tt:128 * (tt + 1), :])])
        p.step("vector", lambda v: v.tensor_scalar(osli[:, :, :], oslc[:, :, :],
                                                   inv_bc[:, :], None, ALU.mult))
        p.step("vector", lambda v: v.tensor_scalar(oslq[:, :, :], osli[:, :, :],
                                                   s_bc[:, :], None, ALU.mult))

        def mm(t, tt=tt):
            for half in range(2):
                for k in range(NF):
                    last = t.matmul(pss[half][:, :], oslq[:, k, :],
                                    wdT[k][:, 512 * half:512 * (half + 1)],
                                    start=(k == 0), stop=(k == NF - 1))
            return last
        p.step("tensor", mm)

        def ev(v, tt=tt):
            v.tensor_add(xa[:, 0:512], pss[0][:, :], xt[:, 0:512])
            return v.tensor_add(xa[:, 512:1024], pss[1][:, :], xt[:, 512:1024])
        p.step("vector", ev)
        ln_steps(p, nc, lnb, xa, x2t, x2b)
        p.step("scalar", lambda se: se.activation(out=sscr[:, :], in_=x2t[:, :],
                                                  func=AF.Square, accum_out=sqs[:, :]))
        p.step("scalar", lambda se, tt=tt: se.activation(out=nrm[:, tt:tt + 1], in_=sqs[:, :],
                                                         func=AF.Sqrt))
        p.step("vector", lambda v, tt=tt: v.tensor_scalar(srt[:, :], nrm[:, tt:tt + 1],
                                                          1e-9, None, ALU.add))
        p.step("vector", lambda v: v.reciprocal(rstd[:, :], srt[:, :]))
        p.step("vector", lambda v: v.tensor_scalar(scb[:, :], x2t[:, :], rstd[:, :],
                                                   None, ALU.mult))
        p.dma_step("scalar", lambda e, tt=tt: [
            e.dma_start(out=x2[128 * tt:128 * (tt + 1), :], in_=x2t[:, :]),
            e.dma_start(out=x2_bf[128 * tt:128 * (tt + 1), :], in_=x2b[:, :]),
            e.dma_start(out=scaled_bf[128 * tt:128 * (tt + 1), :], in_=scb[:, :])])
    p.dma_step("scalar", lambda e: [e.dma_start(out=norm_st[:, :], in_=nrm[:, :])])
    p.step("gpsimd", lambda g: g.nop())


def build_qchain_c():
    nc = bass.Bass("TRN2", target_bir_lowering=False)
    dr = {}
    dr['opre_st'] = nc.dram_tensor("opre_st", [DFF, T], BF16, kind="ExternalInput").ap()
    dr['wdq_down'] = nc.dram_tensor("wdq_down", [D, DFF], BF16, kind="ExternalInput").ap()
    dr['x1'] = nc.dram_tensor("x1", [T, D], F32, kind="ExternalInput").ap()
    dr['am8'] = nc.dram_tensor("am8", [1, 8], F32, kind="ExternalInput").ap()
    dr['x2'] = nc.dram_tensor("x2", [T, D], F32, kind="ExternalOutput").ap()
    dr['x2_bf'] = nc.dram_tensor("x2_bf", [T, D], BF16, kind="ExternalOutput").ap()
    dr['scaled_bf'] = nc.dram_tensor("scaled_bf", [T, D], BF16, kind="ExternalOutput").ap()
    dr['norm_st'] = nc.dram_tensor("norm_st", [128, NT], F32, kind="ExternalOutput").ap()
    with nc.Block() as block:
        p = P(nc, block)
        p.push_phase("qchain_c")
        emit_qchain_c(nc, p, dr)
        p.pop_phase()
    return nc


def emit_l1pre(nc, p, dr):
    scaled_bf = dr["scaled_bf"]
    WqT = dr["WqT"]
    WvT = dr["WvT"]
    bq = dr["bq"]
    bv1024 = dr["bv1024"]
    Amat = dr["Amat"]
    qT_st = dr["qT_st"]
    tqT_st = dr["tqT_st"]
    v_tm_st = dr["v_tm_st"]
    scT = [p.sb(f"scT{i}", [128, T], BF16) for i in range(ND)]
    qT = [p.sb(f"qT{i}", [128, T], BF16) for i in range(ND)]
    wb = [p.sb(f"wb{i}", [128, D], BF16) for i in range(ND)]
    bq_sb = p.sb("bq_sb", [128, ND], F32)
    bv_sb = p.sb("bv_sb", [1, D], BF16)
    onesr = p.sb("onesr", [1, 128], BF16)
    pss = [p.pt(f"pss{i}", [128, 512], F32) for i in range(2)]
    ob = p.sb("ob", [128, T], BF16)
    vtb = p.sb("vtb", [128, D], BF16)

    p.dma_step("sync", lambda e: [
        e.dma_start(out=scT[i][:, :], in_=scaled_bf[:, 128 * i:128 * (i + 1)], transpose=True)
        for i in range(ND)])
    p.dma_step("gpsimd", lambda g: [
        g.dma_start(out=bq_sb[:, :], in_=bq[:, :]),
        g.dma_start(out=bv_sb[:, :], in_=bv1024[:, :])])
    p.step("vector", lambda v: v.memset(onesr[:, :], 1.0))

    p.dma_step("gpsimd", lambda g: [
        g.dma_start(out=wb[i][:, :], in_=WqT[128 * i:128 * (i + 1), :]) for i in range(ND)])
    for ot in range(ND):
        def mms(t, tb, bank, ot=ot):
            for k in range(ND):
                last = t.matmul(pss[bank][:, :], wb[k][:, 128 * ot:128 * (ot + 1)],
                                scT[k][:, 512 * tb:512 * (tb + 1)],
                                start=(k == 0), stop=(k == ND - 1))
            return last

        def ev(v, tb, bank, ot=ot):
            return v.tensor_scalar(qT[ot][:, 512 * tb:512 * (tb + 1)], pss[bank][:, :],
                                   bq_sb[:, ot:ot + 1], None, ALU.add)
        p.mm_pipe(list(range(8)), mms, ev)
        p.dma_step("scalar", lambda e, ot=ot: [
            e.dma_start(out=qT_st[128 * ot:128 * (ot + 1), :], in_=qT[ot][:, :])])

    p.dma_step("gpsimd", lambda g: [
        g.dma_start(out=wb[i][:, :], in_=Amat[128 * i:128 * (i + 1), :]) for i in range(ND)])
    for ot in range(ND):
        def mms(t, tb, bank, ot=ot):
            for k in range(ND):
                last = t.matmul(pss[bank][:, :], wb[k][:, 128 * ot:128 * (ot + 1)],
                                qT[k][:, 512 * tb:512 * (tb + 1)],
                                start=(k == 0), stop=(k == ND - 1))
            return last

        def ev(v, tb, bank, ot=ot):
            return v.tensor_copy(ob[:, 512 * tb:512 * (tb + 1)], pss[bank][:, :])
        p.mm_pipe(list(range(8)), mms, ev)
        p.dma_step("scalar", lambda e, ot=ot: [
            e.dma_start(out=tqT_st[128 * ot:128 * (ot + 1), :], in_=ob[:, :])])

    p.dma_step("gpsimd", lambda g: [
        g.dma_start(out=wb[i][:, :], in_=WvT[128 * i:128 * (i + 1), :]) for i in range(ND)])
    for tt in range(NT):
        def mmv(t, tt=tt):
            for half in range(2):
                for k in range(ND):
                    t.matmul(pss[half][:, :], scT[k][:, 128 * tt:128 * (tt + 1)],
                             wb[k][:, 512 * half:512 * (half + 1)],
                             start=(k == 0), stop=False)
                last = t.matmul(pss[half][:, :], onesr[:, :],
                                bv_sb[:, 512 * half:512 * (half + 1)],
                                start=False, stop=True)
            return last
        p.step("tensor", mmv)

        def evv(v, tt=tt):
            v.tensor_copy(vtb[:, 0:512], pss[0][:, :])
            return v.tensor_copy(vtb[:, 512:1024], pss[1][:, :])
        p.step("vector", evv)
        p.dma_step("scalar", lambda e, tt=tt: [
            e.dma_start(out=v_tm_st[128 * tt:128 * (tt + 1), :], in_=vtb[:, :])])
    p.step("gpsimd", lambda g: g.nop())


def build_l1pre():
    nc = bass.Bass("TRN2", target_bir_lowering=False)
    dr = {}
    dr['scaled_bf'] = nc.dram_tensor("scaled_bf", [T, D], BF16, kind="ExternalInput").ap()
    dr['WqT'] = nc.dram_tensor("WqT", [D, D], F32, kind="ExternalInput").ap()
    dr['WvT'] = nc.dram_tensor("WvT", [D, D], F32, kind="ExternalInput").ap()
    dr['bq'] = nc.dram_tensor("bq", [128, ND], F32, kind="ExternalInput").ap()
    dr['bv1024'] = nc.dram_tensor("bv1024", [1, D], F32, kind="ExternalInput").ap()
    dr['Amat'] = nc.dram_tensor("Amat", [D, D], F32, kind="ExternalInput").ap()
    dr['qT_st'] = nc.dram_tensor("qT_st", [D, T], BF16, kind="ExternalOutput").ap()
    dr['tqT_st'] = nc.dram_tensor("tqT_st", [D, T], BF16, kind="ExternalOutput").ap()
    dr['v_tm_st'] = nc.dram_tensor("v_tm_st", [T, D], BF16, kind="ExternalOutput").ap()
    with nc.Block() as block:
        p = P(nc, block)
        p.push_phase("l1pre")
        emit_l1pre(nc, p, dr)
        p.pop_phase()
    return nc


def emit_l1attn(nc, p, dr):
    qT_st = dr["qT_st"]
    tqT_st = dr["tqT_st"]
    v_tm_st = dr["v_tm_st"]
    x2 = dr["x2"]
    norm_st = dr["norm_st"]
    ident_bf = dr["ident_bf"]
    x3 = dr["x3"]
    x3_bf = dr["x3_bf"]
    tqT = [p.sb(f"tqT{i}", [128, T], BF16) for i in range(ND)]
    vtm = [p.sb(f"vtm{i}", [128, D], BF16) for i in range(NT)]
    qsl = p.sb("qsl", [128, ND, 128], BF16)
    sco = p.sb("sco", [128, T], F32)
    pexp = p.sb("pexp", [128, T], BF16)
    pT = p.sb("pT", [128, NT, 128], BF16)
    idb = p.sb("idb", [128, 128], BF16)
    nrm = p.sb("nrm", [128, NT], F32)
    x2t = p.sb("x2t", [128, D], F32)
    xa = p.sb("xa", [128, D], F32)
    x3t = p.sb("x3t", [128, D], F32)
    x3b = p.sb("x3b", [128, D], BF16)
    mrow = p.sb("mrow", [128, 4], F32)
    stats6 = p.sb("stats6", [128, 12], F32)
    mv = p.sb("mv", [128, 2], F32)
    srt = p.sb("srt", [128, 1], F32)
    rstd = p.sb("rstd", [128, 1], F32)
    psc = [p.pt(f"psc{i}", [128, 512], F32) for i in range(2)]
    ptr = p.pt("ptr", [128, 128], BF16)
    pct = [p.pt(f"pct{i}", [128, 512], F32) for i in range(2)]
    eps_ap(nc, p, 1e-5, "eps5")

    p.dma_step("sync", lambda e: [
        e.dma_start(out=tqT[i][:, :], in_=tqT_st[128 * i:128 * (i + 1), :])
        for i in range(ND)] + [
        e.dma_start(out=vtm[i][:, :], in_=v_tm_st[128 * i:128 * (i + 1), :])
        for i in range(NT)] + [
        e.dma_start(out=idb[:, :], in_=ident_bf[:, :]),
        e.dma_start(out=nrm[:, :], in_=norm_st[:, :])])

    for s in range(NT):
        p.dma_step("sync", lambda e, s=s: [
            e.dma_start(out=qsl[:, :, :],
                        in_=qT_st[:, 128 * s:128 * (s + 1)].rearrange("(a p) c -> p a c", p=128)),
            e.dma_start(out=x2t[:, :], in_=x2[128 * s:128 * (s + 1), :])])

        def mmsc(t, b2, bank, s=s):
            for k in range(ND):
                last = t.matmul(psc[bank][:, :], qsl[:, k, :],
                                tqT[k][:, 512 * b2:512 * (b2 + 1)],
                                start=(k == 0), stop=(k == ND - 1))
            return last

        def evsc(v, b2, bank, s=s):
            return v.tensor_copy(sco[:, 512 * b2:512 * (b2 + 1)], psc[bank][:, :])
        p.mm_pipe(list(range(8)), mmsc, evsc)

        p.step("vector", lambda v: v.tensor_reduce(mrow[:, 0:1], sco[:, :], AX.X, ALU.max))
        p.step("vector", lambda v: v.tensor_scalar(mrow[:, 1:2], mrow[:, 0:1],
                                                   -1.0 / 32, None, ALU.mult))
        p.step("scalar", lambda se: se.activation(out=pexp[:, :], in_=sco[:, :], func=AF.Exp,
                                                  scale=1.0 / 32, bias=mrow[:, 1:2],
                                                  accum_out=mrow[:, 2:3]))
        p.step("vector", lambda v: v.reciprocal(mrow[:, 3:4], mrow[:, 2:3]))
        p.step("vector", lambda v, s=s: v.tensor_mul(mrow[:, 0:1], mrow[:, 3:4],
                                                     nrm[:, s:s + 1]))

        for k in range(NT):
            p.step("tensor", lambda t, k=k: t.transpose(ptr[:, :],
                                                        pexp[:, 128 * k:128 * (k + 1)],
                                                        idb[:, :]))
            p.step("vector", lambda v, k=k: v.tensor_copy(pT[:, k, :], ptr[:, :]))

        def mmct(t, s=s):
            for half in range(2):
                for k in range(NT):
                    last = t.matmul(pct[half][:, :], pT[:, k, :],
                                    vtm[k][:, 512 * half:512 * (half + 1)],
                                    start=(k == 0), stop=(k == NT - 1))
            return last
        p.step("tensor", mmct)

        def evct(v, s=s):
            v.tensor_scalar(xa[:, 0:512], pct[0][:, :], mrow[:, 0:1], None, ALU.mult)
            return v.tensor_scalar(xa[:, 512:1024], pct[1][:, :], mrow[:, 0:1], None, ALU.mult)
        p.step("vector", evct)
        p.step("vector", lambda v: v.tensor_add(xa[:, :], xa[:, :], x2t[:, :]))
        ln_steps(p, nc, (stats6, mv, srt, rstd), xa, x3t, x3b)
        p.dma_step("scalar", lambda e, s=s: [
            e.dma_start(out=x3[128 * s:128 * (s + 1), :], in_=x3t[:, :]),
            e.dma_start(out=x3_bf[128 * s:128 * (s + 1), :], in_=x3b[:, :])])
    p.step("gpsimd", lambda g: g.nop())


def build_l1attn():
    nc = bass.Bass("TRN2", target_bir_lowering=False)
    dr = {}
    dr['qT_st'] = nc.dram_tensor("qT_st", [D, T], BF16, kind="ExternalInput").ap()
    dr['tqT_st'] = nc.dram_tensor("tqT_st", [D, T], BF16, kind="ExternalInput").ap()
    dr['v_tm_st'] = nc.dram_tensor("v_tm_st", [T, D], BF16, kind="ExternalInput").ap()
    dr['x2'] = nc.dram_tensor("x2", [T, D], F32, kind="ExternalInput").ap()
    dr['norm_st'] = nc.dram_tensor("norm_st", [128, NT], F32, kind="ExternalInput").ap()
    dr['ident_bf'] = nc.dram_tensor("ident_bf", [128, 128], BF16, kind="ExternalInput").ap()
    dr['x3'] = nc.dram_tensor("x3", [T, D], F32, kind="ExternalOutput").ap()
    dr['x3_bf'] = nc.dram_tensor("x3_bf", [T, D], BF16, kind="ExternalOutput").ap()
    with nc.Block() as block:
        p = P(nc, block)
        p.push_phase("l1attn")
        emit_l1attn(nc, p, dr)
        p.pop_phase()
    return nc


def emit_moe(nc, p, dr):
    x3 = dr["x3"]
    x3_bf = dr["x3_bf"]
    router = dr["router"]
    eg = dr["eg"]
    eu = dr["eu"]
    ed = dr["ed"]
    sgw = dr["sgw"]
    suw = dr["suw"]
    sdw = dr["sdw"]
    out = dr["out"]
    sh_act = nc.dram_tensor("sh_act", [D, T], BF16)
    e_act = [nc.dram_tensor(f"e_act{e}", [512, T], BF16) for e in range(8)]
    x3T = [p.sb(f"x3T{i}", [128, T], BF16) for i in range(ND)]
    wb = [p.sb(f"wb{i}", [128, D], BF16) for i in range(ND)]
    wb2 = [p.sb(f"wb2{i}", [128, D], BF16) for i in range(ND)]
    rt = p.sb("rt", [128, ND, 8], BF16)
    coef = p.sb("coef", [128, NT, 8], F32)
    p8 = p.sb("p8", [128, 8], F32)
    tmp8 = p.sb("tmp8", [128, 8], F32)
    msk = p.sb("msk", [128, 8], F32)
    sc8 = p.sb("sc8", [128, 6], F32)
    af = p.sb("af", [128, T], BF16)
    bf_ = p.sb("bf_", [128, T], BF16)
    ob = p.sb("ob", [128, T], BF16)
    acc = p.sb("acc", [128, D], F32)
    x3t_f = p.sb("x3t_f", [128, D], F32)
    o4 = p.sb("o4", [128, D], F32)
    slc = p.sb("slc", [128, ND, 128], BF16)
    eslc = p.sb("eslc", [128, 4, 128], BF16)
    edw = [[p.sb(f"edw{e}_{k}", [128, D], BF16) for k in range(4)]
           for e in range(8)]
    stats6 = p.sb("stats6", [128, 12], F32)
    mv = p.sb("mv", [128, 2], F32)
    srt = p.sb("srt", [128, 1], F32)
    rstd = p.sb("rstd", [128, 1], F32)
    pss = [p.pt(f"pss{i}", [128, 512], F32) for i in range(2)]
    psr = p.pt("psr", [128, 8], F32)
    eps_ap(nc, p, 1e-5, "eps5")

    p.dma_step("sync", lambda e: [
        e.dma_start(out=x3T[i][:, :], in_=x3_bf[:, 128 * i:128 * (i + 1)], transpose=True)
        for i in range(ND)])
    p.dma_step("gpsimd", lambda g: [
        g.dma_start(out=rt[:, :, :], in_=router.rearrange("(a p) c -> p a c", p=128))])

    for tt in range(NT):
        def mmr(t, tt=tt):
            for k in range(ND):
                last = t.matmul(psr[:, :], x3T[k][:, 128 * tt:128 * (tt + 1)], rt[:, k, :],
                                start=(k == 0), stop=(k == ND - 1))
            return last
        p.step("tensor", mmr)
        p.step("vector", lambda v: v.tensor_reduce(sc8[:, 0:1], psr[:, :], AX.X, ALU.max))
        p.step("vector", lambda v: v.tensor_scalar(sc8[:, 1:2], sc8[:, 0:1], -1.0, None, ALU.mult))
        p.step("vector", lambda v: v.tensor_copy(p8[:, :], psr[:, :]))
        p.step("scalar", lambda se: se.activation(out=p8[:, :], in_=p8[:, :], func=AF.Exp,
                                                  bias=sc8[:, 1:2], accum_out=sc8[:, 2:3]))
        p.step("vector", lambda v: v.reciprocal(sc8[:, 3:4], sc8[:, 2:3]))
        p.step("vector", lambda v: v.tensor_scalar(p8[:, :], p8[:, :], sc8[:, 3:4], None, ALU.mult))
        p.step("vector", lambda v: v.tensor_reduce(sc8[:, 4:5], p8[:, :], AX.X, ALU.max))
        p.step("vector", lambda v: v.tensor_scalar(msk[:, :], p8[:, :], sc8[:, 4:5],
                                                   None, ALU.is_ge))
        p.step("vector", lambda v: v.scalar_tensor_tensor(tmp8[:, :], msk[:, :], -2.0,
                                                          p8[:, :], ALU.mult, ALU.add))
        p.step("vector", lambda v: v.tensor_reduce(sc8[:, 5:6], tmp8[:, :], AX.X, ALU.max))
        p.step("vector", lambda v: v.tensor_scalar(msk[:, :], p8[:, :], sc8[:, 5:6],
                                                   None, ALU.is_ge))
        p.step("vector", lambda v, tt=tt: v.tensor_tensor(coef[:, tt, :], p8[:, :],
                                                          msk[:, :], ALU.mult))

    p.dma_step("gpsimd", lambda g: [
        g.dma_start(out=wb[i][:, :], in_=sgw[128 * i:128 * (i + 1), :]) for i in range(ND)] + [
        g.dma_start(out=wb2[i][:, :], in_=suw[128 * i:128 * (i + 1), :]) for i in range(ND)])
    for ot in range(ND):
        def mms(t, tb, bank, ot=ot):
            for k in range(ND):
                last = t.matmul(pss[bank][:, :], wb[k][:, 128 * ot:128 * (ot + 1)],
                                x3T[k][:, 512 * tb:512 * (tb + 1)],
                                start=(k == 0), stop=(k == ND - 1))
            return last

        def ev(se, tb, bank, ot=ot):
            return se.activation(out=af[:, 512 * tb:512 * (tb + 1)], in_=pss[bank][:, :],
                                 func=AF.Silu)
        p.mm_pipe(list(range(8)), mms, ev, evac_engine="scalar")

        def mms2(t, tb, bank, ot=ot):
            for k in range(ND):
                last = t.matmul(pss[bank][:, :], wb2[k][:, 128 * ot:128 * (ot + 1)],
                                x3T[k][:, 512 * tb:512 * (tb + 1)],
                                start=(k == 0), stop=(k == ND - 1))
            return last

        def ev2(v, tb, bank, ot=ot):
            return v.tensor_copy(bf_[:, 512 * tb:512 * (tb + 1)], pss[bank][:, :])
        p.mm_pipe(list(range(8)), mms2, ev2)
        p.step("vector", lambda v: v.tensor_mul(ob[:, :], af[:, :], bf_[:, :]))
        p.dma_step("scalar", lambda e, ot=ot: [
            e.dma_start(out=sh_act[128 * ot:128 * (ot + 1), :], in_=ob[:, :])])

    ew = [wb[i] for i in range(ND)]
    ew2 = [wb2[i] for i in range(ND)]
    for e in range(8):
        p.dma_step("gpsimd", lambda g, e=e: [
            g.dma_start(out=ew[i][:, 0:512], in_=eg[e, 128 * i:128 * (i + 1), :])
            for i in range(ND)] + [
            g.dma_start(out=ew2[i][:, 0:512], in_=eu[e, 128 * i:128 * (i + 1), :])
            for i in range(ND)])
        for ot in range(4):
            def mms(t, tb, bank, ot=ot):
                for k in range(ND):
                    last = t.matmul(pss[bank][:, :], ew[k][:, 128 * ot:128 * (ot + 1)],
                                    x3T[k][:, 512 * tb:512 * (tb + 1)],
                                    start=(k == 0), stop=(k == ND - 1))
                return last

            def ev(se, tb, bank, ot=ot):
                return se.activation(out=af[:, 512 * tb:512 * (tb + 1)], in_=pss[bank][:, :],
                                     func=AF.Silu)
            p.mm_pipe(list(range(8)), mms, ev, evac_engine="scalar")

            def mms2(t, tb, bank, ot=ot):
                for k in range(ND):
                    last = t.matmul(pss[bank][:, :], ew2[k][:, 128 * ot:128 * (ot + 1)],
                                    x3T[k][:, 512 * tb:512 * (tb + 1)],
                                    start=(k == 0), stop=(k == ND - 1))
                return last

            def ev2(v, tb, bank, ot=ot):
                return v.tensor_copy(bf_[:, 512 * tb:512 * (tb + 1)], pss[bank][:, :])
            p.mm_pipe(list(range(8)), mms2, ev2)
            p.step("vector", lambda v: v.tensor_mul(ob[:, :], af[:, :], bf_[:, :]))
            p.dma_step("scalar", lambda ee, ot=ot, e=e: [
                ee.dma_start(out=e_act[e][128 * ot:128 * (ot + 1), :], in_=ob[:, :])])

    p.dma_step("gpsimd", lambda g: [
        g.dma_start(out=wb[i][:, :], in_=sdw[128 * i:128 * (i + 1), :]) for i in range(ND)])
    p.dma_step("gpsimd", lambda g: [
        g.dma_start(out=edw[e][k][:, :], in_=ed[e, 128 * k:128 * (k + 1), :])
        for e in range(8) for k in range(4)])
    for tt in range(NT):
        p.dma_step("sync", lambda e, tt=tt: [
            e.dma_start(out=slc[:, :, :],
                        in_=sh_act[:, 128 * tt:128 * (tt + 1)].rearrange("(a p) c -> p a c", p=128)),
            e.dma_start(out=x3t_f[:, :], in_=x3[128 * tt:128 * (tt + 1), :])])

        def mmsh(t, tt=tt):
            for half in range(2):
                for k in range(ND):
                    last = t.matmul(pss[half][:, :], slc[:, k, :],
                                    wb[k][:, 512 * half:512 * (half + 1)],
                                    start=(k == 0), stop=(k == ND - 1))
            return last
        p.step("tensor", mmsh)

        def evsh(v, tt=tt):
            v.tensor_copy(acc[:, 0:512], pss[0][:, :])
            return v.tensor_copy(acc[:, 512:1024], pss[1][:, :])
        p.step("vector", evsh)
        for e in range(8):
            p.dma_step("sync", lambda en, tt=tt, e=e: [
                en.dma_start(out=eslc[:, :, :],
                             in_=e_act[e][:, 128 * tt:128 * (tt + 1)].rearrange("(a p) c -> p a c", p=128))])

            def mme(t, tt=tt, e=e):
                for half in range(2):
                    for k in range(4):
                        last = t.matmul(pss[half][:, :], eslc[:, k, :],
                                        edw[e][k][:, 512 * half:512 * (half + 1)],
                                        start=(k == 0), stop=(k == 3))
                return last
            p.step("tensor", mme)

            def eve(v, tt=tt, e=e):
                v.scalar_tensor_tensor(acc[:, 0:512], pss[0][:, :], coef[:, tt, e:e + 1],
                                       acc[:, 0:512], ALU.mult, ALU.add)
                return v.scalar_tensor_tensor(acc[:, 512:1024], pss[1][:, :],
                                              coef[:, tt, e:e + 1], acc[:, 512:1024],
                                              ALU.mult, ALU.add)
            p.step("vector", eve)
        p.step("vector", lambda v: v.tensor_add(acc[:, :], acc[:, :], x3t_f[:, :]))
        ln_steps(p, nc, (stats6, mv, srt, rstd), acc, o4)
        p.dma_step("scalar", lambda e, tt=tt: [
            e.dma_start(out=out[128 * tt:128 * (tt + 1), :], in_=o4[:, :])])
    p.step("gpsimd", lambda g: g.nop())


def build_moe():
    nc = bass.Bass("TRN2", target_bir_lowering=False)
    dr = {}
    dr['x3'] = nc.dram_tensor("x3", [T, D], F32, kind="ExternalInput").ap()
    dr['x3_bf'] = nc.dram_tensor("x3_bf", [T, D], BF16, kind="ExternalInput").ap()
    dr['router'] = nc.dram_tensor("router", [D, 8], F32, kind="ExternalInput").ap()
    dr['eg'] = nc.dram_tensor("eg", [8, D, 512], F32, kind="ExternalInput").ap()
    dr['eu'] = nc.dram_tensor("eu", [8, D, 512], F32, kind="ExternalInput").ap()
    dr['ed'] = nc.dram_tensor("ed", [8, 512, D], F32, kind="ExternalInput").ap()
    dr['sgw'] = nc.dram_tensor("sgw", [D, D], F32, kind="ExternalInput").ap()
    dr['suw'] = nc.dram_tensor("suw", [D, D], F32, kind="ExternalInput").ap()
    dr['sdw'] = nc.dram_tensor("sdw", [D, D], F32, kind="ExternalInput").ap()
    dr['out'] = nc.dram_tensor("out", [T, D], F32, kind="ExternalOutput").ap()
    with nc.Block() as block:
        p = P(nc, block)
        p.push_phase("moe")
        emit_moe(nc, p, dr)
        p.pop_phase()
    return nc


def host_l1_inputs(inputs, core):
    f = np.ascontiguousarray
    return {
        "x": f(inputs["L0_attn_hidden_states"][core]).astype(np.float32, copy=False),
        "wqT": f(inputs["L0_attn_q_w"].T),
        "wkT": f(inputs["L0_attn_k_w"].T),
        "wvT": f(inputs["L0_attn_v_w"].T),
        "bwT": f(inputs["L0_attn_b_w"].T),
        "qcw": f(inputs["L0_attn_q_cw"].reshape(ND, 128, 4).transpose(1, 0, 2).reshape(128, ND * 4)),
        "kcw": f(inputs["L0_attn_k_cw"].reshape(ND, 128, 4).transpose(1, 0, 2).reshape(128, ND * 4)),
        "vcw": f(inputs["L0_attn_v_cw"].reshape(ND, 128, 4).transpose(1, 0, 2).reshape(128, ND * 4)),
    }


def host_l6a_inputs(l5_all, core):
    n = len(l5_all)
    return {
        "up_st": l5_all[core]["up_st"], "gate_st": l5_all[core]["gate_st"],
        "am8u": _amax8(l5_all, "amax_up"),
        "am8g": _amax8(l5_all, "amax_gate"),
        "ident": np.eye(128, dtype=np.float32),
    }


def host_l6b_inputs(l6a_all, core):
    n = len(l6a_all)
    return {
        "g_st": l6a_all[core]["g_st"], "upq_st": l6a_all[core]["upq_st"],
        "am8": _amax8(l6a_all, "amax_g"),
        "ident": np.eye(128, dtype=np.float32),
    }


def host_l6c_inputs(l5_all, l4b_all, l6b_all, core):
    n = len(l6b_all)
    return {
        "opre_st": l6b_all[core]["opre_st"],
        "wdq_down": l5_all[core]["wdq_down"],
        "x1": l4b_all[core]["x1"],
        "am8": _amax8(l6b_all, "amax_o"),
    }


def host_l7a_inputs(inputs, l6c_out):
    return {
        "scaled_bf": l6c_out["scaled_bf"],
        "WqT": np.ascontiguousarray(inputs["L1_attn_Wq"].T),
        "WvT": np.ascontiguousarray(inputs["L1_attn_Wv"].T),
        "bq": np.ascontiguousarray(inputs["L1_attn_bq"].reshape(ND, 128).T),
        "bv1024": np.ascontiguousarray(inputs["L1_attn_bv"].reshape(1, D)),
        "Amat": np.ascontiguousarray(inputs["L1_attn_anomaly_matrix"]),
    }


def host_l7b_inputs(l6c_out, l7a_out):
    return {
        "qT_st": l7a_out["qT_st"], "tqT_st": l7a_out["tqT_st"], "v_tm_st": l7a_out["v_tm_st"],
        "x2": l6c_out["x2"], "norm_st": l6c_out["norm_st"],
        "ident_bf": np.eye(128, dtype=BF),
    }


def host_l8_inputs(inputs, l7b_out):
    return {
        "x3": l7b_out["x3"], "x3_bf": l7b_out["x3_bf"],
        "router": np.ascontiguousarray(inputs["L1_ffn_router"]),
        "eg": np.ascontiguousarray(inputs["L1_ffn_experts_gate"]),
        "eu": np.ascontiguousarray(inputs["L1_ffn_experts_up"]),
        "ed": np.ascontiguousarray(inputs["L1_ffn_experts_down"]),
        "sgw": np.ascontiguousarray(inputs["L1_ffn_shared_gate"]),
        "suw": np.ascontiguousarray(inputs["L1_ffn_shared_up"]),
        "sdw": np.ascontiguousarray(inputs["L1_ffn_shared_down"]),
    }


def build_fused():
    nc = bass.Bass("TRN2", target_bir_lowering=False)
    E_IN, E_OUT = "ExternalInput", "ExternalOutput"
    dt = nc.dram_tensor
    x = dt("x", [T, D], F32, kind=E_IN).ap()
    wT = {nm: dt(f"w{nm}T", [D, D], F32, kind=E_IN).ap() for nm in "qkv"}
    bwT = dt("bwT", [D, H], F32, kind=E_IN).ap()
    cw = {nm: dt(f"{nm}cw", [128, ND * 4], F32, kind=E_IN).ap() for nm in "qkv"}
    eye64 = dt("eye64", [C2, C2], BF16, kind=E_IN).ap()
    masks = dt("masks", [C2, 3 * C2], F32, kind=E_IN).ap()
    fsw = dt("fsw", [128, ND * 5], F32, kind=E_IN).ap()
    flw = dt("flw", [128, ND * 64], F32, kind=E_IN).ap()
    mlp1T = dt("mlp1T", [1040, D], F32, kind=E_IN).ap()
    mlp1b = dt("mlp1b", [128, ND], F32, kind=E_IN).ap()
    mlp2T = dt("mlp2T", [D, H], F32, kind=E_IN).ap()
    mlp2b = dt("mlp2b", [1, H], F32, kind=E_IN).ap()
    ltemp = dt("ltemp", [1, 1], F32, kind=E_IN).ap()
    crl = dt("crl", [1, H], F32, kind=E_IN).ap()
    ident = dt("ident", [128, 128], F32, kind=E_IN).ap()
    ident_bf = dt("ident_bf", [128, 128], BF16, kind=E_IN).ap()
    owT = dt("owT", [D, D], F32, kind=E_IN).ap()
    nw_bc = dt("nw_bc", [128, ND], F32, kind=E_IN).ap()
    w_up = dt("w_up", [DFF, D], F32, kind=E_IN).ap()
    w_gate = dt("w_gate", [DFF, D], F32, kind=E_IN).ap()
    w_down = dt("w_down", [D, DFF], F32, kind=E_IN).ap()
    WqT = dt("WqT", [D, D], F32, kind=E_IN).ap()
    WvT = dt("WvT", [D, D], F32, kind=E_IN).ap()
    bq = dt("bq", [128, ND], F32, kind=E_IN).ap()
    bv1024 = dt("bv1024", [1, D], F32, kind=E_IN).ap()
    Amat = dt("Amat", [D, D], F32, kind=E_IN).ap()
    router = dt("router", [D, 8], F32, kind=E_IN).ap()
    eg = dt("eg", [8, D, 512], F32, kind=E_IN).ap()
    eu = dt("eu", [8, D, 512], F32, kind=E_IN).ap()
    ed = dt("ed", [8, 512, D], F32, kind=E_IN).ap()
    sgw = dt("sgw", [D, D], F32, kind=E_IN).ap()
    suw = dt("suw", [D, D], F32, kind=E_IN).ap()
    sdw = dt("sdw", [D, D], F32, kind=E_IN).ap()
    out = dt("out", [T, D], F32, kind=E_OUT).ap()
    qn_tm = dt("qn_tm", [T, D], BF16).ap()
    kn_tm = dt("kn_tm", [T, D], BF16).ap()
    v_tm = dt("v_tm", [T, D], BF16).ap()
    v_fm = dt("v_fm", [D, T], BF16).ap()
    beta_tm = dt("beta_tm", [T, H], F32).ap()
    x_bf = dt("x_bf", [T, D], BF16).ap()
    delta_tm = dt("delta_tm", [T, D], BF16).ap()
    ls_fm = dt("ls_fm", [D, T], BF16).ap()
    ll_fm = dt("ll_fm", [D, T], BF16).ap()
    o_tm = dt("o_tm", [T, D], BF16).ap()
    x1 = dt("x1", [T, D], F32).ap()
    x1_bf = dt("x1_bf", [T, D], BF16).ap()
    up_st = dt("up_st", [DFF, T], BF16).ap()
    gate_st = dt("gate_st", [DFF, T], BF16).ap()
    wdq_down = dt("wdq_down", [D, DFF], BF16).ap()
    upq_st = dt("upq_st", [DFF, T], BF16).ap()
    g_st = dt("g_st", [DFF, T], BF16).ap()
    opre_st = dt("opre_st", [DFF, T], BF16).ap()
    x2 = dt("x2", [T, D], F32).ap()
    x2_bf = dt("x2_bf", [T, D], BF16).ap()
    scaled_bf = dt("scaled_bf", [T, D], BF16).ap()
    norm_stg = dt("norm_stg", [128, NT], F32).ap()
    qT_st = dt("qT_st", [D, T], BF16).ap()
    tqT_st = dt("tqT_st", [D, T], BF16).ap()
    v_tm_st = dt("v_tm_st", [T, D], BF16).ap()
    x3 = dt("x3", [T, D], F32).ap()
    x3_bf = dt("x3_bf", [T, D], BF16).ap()
    amx1 = dt("amx1", [1, 8], F32).ap()
    amup = dt("amup", [1, 8], F32).ap()
    amgate = dt("amgate", [1, 8], F32).ap()
    amg = dt("amg", [1, 8], F32).ap()
    amo = dt("amo", [1, 8], F32).ap()

    with nc.Block() as block:
        p = P(nc, block)
        phases = [
            ("qkvb", emit_qkvb, dict(x=x, wT=wT, bwT=bwT, cw=cw, qn_tm=qn_tm,
                                     kn_tm=kn_tm, v_tm=v_tm, v_fm=v_fm,
                                     beta_tm=beta_tm, x_bf=x_bf)),
            ("delta", emit_delta, dict(qn_tm=qn_tm, kn_tm=kn_tm, v_tm=v_tm,
                                       beta_tm=beta_tm, eye64=eye64, masks=masks,
                                       delta_tm=delta_tm)),
            ("fir", emit_fir, dict(v_fm=v_fm, fsw=fsw, flw=flw, ls_fm=ls_fm, ll_fm=ll_fm)),
            ("gate", emit_gate, dict(x_bf=x_bf, v_tm=v_tm, delta_tm=delta_tm,
                                     ls_fm=ls_fm, ll_fm=ll_fm, mlp1T=mlp1T, mlp1b=mlp1b,
                                     mlp2T=mlp2T, mlp2b=mlp2b, ltemp=ltemp, crl=crl,
                                     ident=ident, o_tm=o_tm)),
            ("ow", emit_ow, dict(x=x, o_tm=o_tm, owT=owT, nw_bc=nw_bc, ident=ident,
                                 x1=x1, x1_bf=x1_bf, amax=amx1[0:1, 0:1])),
            ("ffn_up", emit_ffn_up, dict(x1_bf=x1_bf, w_up=w_up, w_gate=w_gate,
                                         w_down=w_down, amax8=amx1, ident=ident,
                                         up_st=up_st, gate_st=gate_st, wdq_down=wdq_down,
                                         amax_up=amup[0:1, 0:1], amax_gate=amgate[0:1, 0:1],
                                         am8_n=1)),
            ("qchain_a", emit_qchain_a, dict(up_st=up_st, gate_st=gate_st, am8u=amup,
                                             am8g=amgate, ident=ident, upq_st=upq_st,
                                             g_st=g_st, amax_g=amg[0:1, 0:1], am8_n=1)),
            ("qchain_b", emit_qchain_b, dict(g_st=g_st, upq_st=upq_st, am8=amg,
                                             ident=ident, opre_st=opre_st,
                                             amax_o=amo[0:1, 0:1], am8_n=1)),
            ("qchain_c", emit_qchain_c, dict(opre_st=opre_st, wdq_down=wdq_down, x1=x1,
                                             am8=amo, x2=x2, x2_bf=x2_bf,
                                             scaled_bf=scaled_bf, norm_st=norm_stg,
                                             am8_n=1)),
            ("l1pre", emit_l1pre, dict(scaled_bf=scaled_bf, WqT=WqT, WvT=WvT, bq=bq,
                                       bv1024=bv1024, Amat=Amat, qT_st=qT_st,
                                       tqT_st=tqT_st, v_tm_st=v_tm_st)),
            ("l1attn", emit_l1attn, dict(qT_st=qT_st, tqT_st=tqT_st, v_tm_st=v_tm_st,
                                         x2=x2, norm_st=norm_stg, ident_bf=ident_bf,
                                         x3=x3, x3_bf=x3_bf)),
            ("moe", emit_moe, dict(x3=x3, x3_bf=x3_bf, router=router, eg=eg, eu=eu,
                                   ed=ed, sgw=sgw, suw=suw, sdw=sdw, out=out)),
        ]
        for name, fn, dr in phases:
            p.push_phase(name)
            with nc.named_scope(name):
                fn(nc, p, dr)
            p.pop_phase()
    return nc


def host_fused_inputs(inputs, core):
    d = {}
    d.update(host_l1_inputs(inputs, core))
    d.update({k: v for k, v in host_l2_inputs({"qn_tm": None, "kn_tm": None,
                                               "v_tm": None, "beta_tm": None}).items()
              if k in ("eye64", "masks")})
    d.update({k: v for k, v in host_l3_inputs(inputs, {"v_fm": None}).items()
              if k in ("fsw", "flw")})
    l4a = host_l4a_inputs(inputs, {"x_bf": None, "v_tm": None}, {"delta_tm": None},
                          {"ls_fm": None, "ll_fm": None})
    d.update({k: v for k, v in l4a.items() if v is not None and k not in
              ("x_bf", "v_tm", "delta_tm", "ls_fm", "ll_fm")})
    l4b = host_l4b_inputs(inputs, {"o_tm": None}, core)
    d.update({k: v for k, v in l4b.items() if k in ("owT", "nw_bc")})
    d["ident_bf"] = np.eye(128, dtype=BF)
    d["w_up"] = np.ascontiguousarray(inputs["L0_ffn_w_up"])
    d["w_gate"] = np.ascontiguousarray(inputs["L0_ffn_w_gate"])
    d["w_down"] = np.ascontiguousarray(inputs["L0_ffn_w_down"])
    l7a = host_l7a_inputs(inputs, {"scaled_bf": None})
    d.update({k: v for k, v in l7a.items() if k != "scaled_bf"})
    l8 = host_l8_inputs(inputs, {"x3": None, "x3_bf": None})
    d.update({k: v for k, v in l8.items() if k not in ("x3", "x3_bf")})
    return d


def kernel(**inputs):
    cores = list(range(B))
    nc = build_fused()
    maps = [host_fused_inputs(inputs, c) for c in cores]
    res = run_bass_kernel_spmd(nc, maps, core_ids=cores)
    return np.stack([np.asarray(res.results[c]["out"]) for c in cores])


def kernel_multilaunch(**inputs):
    n = B
    cores = list(range(n))

    def run(nc, maps):
        return run_bass_kernel_spmd(nc, maps, core_ids=cores).results

    r1 = run(build_qkvb(), [host_l1_inputs(inputs, c) for c in cores])
    r2 = run(build_delta(), [host_l2_inputs(r1[c]) for c in cores])
    r3 = run(build_fir(), [host_l3_inputs(inputs, r1[c]) for c in cores])
    r4a = run(build_gate(), [host_l4a_inputs(inputs, r1[c], r2[c], r3[c]) for c in cores])
    r4b = run(build_ow(), [host_l4b_inputs(inputs, r4a[c], c) for c in cores])
    r5 = run(build_ffn_up(), [host_l5_inputs(inputs, r4b, c) for c in cores])
    r6a = run(build_qchain_a(), [host_l6a_inputs(r5, c) for c in cores])
    r6b = run(build_qchain_b(), [host_l6b_inputs(r6a, c) for c in cores])
    r6c = run(build_qchain_c(), [host_l6c_inputs(r5, r4b, r6b, c) for c in cores])
    r7a = run(build_l1pre(), [host_l7a_inputs(inputs, r6c[c]) for c in cores])
    r7b = run(build_l1attn(), [host_l7b_inputs(r6c[c], r7a[c]) for c in cores])
    r8 = run(build_moe(), [host_l8_inputs(inputs, r7b[c]) for c in cores])
    return np.stack([np.asarray(r8[c]["out"]) for c in cores])
